# version 07
# baseline (speedup 1.0000x reference)
import numpy as np
from contextlib import ExitStack

import concourse.bass as bass
import concourse.mybir as mybir
from concourse.bass_utils import run_bass_kernel_spmd

F32 = mybir.dt.float32
AF = mybir.ActivationFunctionType
ALU = mybir.AluOpType
AX = mybir.AxisListType.X
AXY = mybir.AxisListType.XY

MAGIC = 12582912.0
QMAX = 32767.0
C_EXP = 14.0
EPS_LN = 1e-5
LN10K = 9.210340371976184
TWO_PI = 6.283185307179586
CW1 = 6.28125
CW2 = float(np.float32(TWO_PI - 6.28125))
INV_2PI = 0.15915494309189535
HALF_PI = float(np.float32(np.pi / 2))
PI_F = float(np.float32(np.pi))
R7A = float(np.float32(1.0 / 7.0))
R7B = float(np.float32(1.0 / 7.0 - float(np.float32(1.0 / 7.0))))
RINV = float(np.float32(np.exp(-LN10K / 32.0)))

T = 2048
D = 1024
NC_D = 8
I3 = 3072
NI = 24
S = 4096
STAGE = 4
FFN_SUB = 7
FLASH_PIPE = True


class Sem:
    def __init__(self, ctx, nc, name):
        self.h = ctx.enter_context(nc.semaphore(name))
        self.n = 0

    def inc(self, k=1):
        self.n += k
        return self.n


_SEQ_ONLY = {"wait_ge", "wait_op", "sem_inc", "nop", "maybe_wait_ge"}


class EW:

    def __init__(self, eng):
        self._e = eng
        self._last = None

    def __getattr__(self, k):
        a = getattr(self._e, k)
        if callable(a) and k not in _SEQ_ONLY and not k.startswith("_"):
            def w(*args, **kw):
                r = a(*args, **kw)
                if k not in _SEQ_ONLY:
                    self._last = r
                return r
            return w
        return a

    def wait_ge(self, sem, val):
        return self._e.wait_ge(sem, val)

    def sem_inc(self, sem, val):
        return self._e.sem_inc(sem, val)

    def fence(self, sem, n=1):
        assert self._last is not None, "fence with no prior datapath op"
        self._last.then_inc(sem, n)


def build_program(stage=STAGE, taps=()):
    nc = bass.Bass("TRN2", target_bir_lowering=False,
                   detect_race_conditions=False)
    tap = set(taps)

    x_in = nc.dram_tensor("x", [T, D], F32, kind="ExternalInput").ap()
    wq_in = nc.dram_tensor("wq", [512, D], F32, kind="ExternalInput").ap()
    wk_in = nc.dram_tensor("wk", [512, D], F32, kind="ExternalInput").ap()
    wv_in = nc.dram_tensor("wv", [512, D], F32, kind="ExternalInput").ap()
    wo_in = nc.dram_tensor("wo", [D, 512], F32, kind="ExternalInput").ap()
    wg_in = nc.dram_tensor("wg", [I3, D], F32, kind="ExternalInput").ap()
    wu_in = nc.dram_tensor("wu", [I3, D], F32, kind="ExternalInput").ap()
    wd_in = nc.dram_tensor("wd", [D, I3], F32, kind="ExternalInput").ap()
    out = nc.dram_tensor("out", [T, D], F32, kind="ExternalOutput").ap()

    tap_outs = {}

    dr = {}
    dr["wqT"] = nc.dram_tensor("wqT", [NC_D, 128, 512], F32).ap()
    dr["wkT"] = nc.dram_tensor("wkT", [NC_D, 128, 512], F32).ap()
    dr["wvT"] = nc.dram_tensor("wvT", [NC_D, 128, 512], F32).ap()
    dr["woT"] = nc.dram_tensor("woT", [4, 128, D], F32).ap()
    dr["wgT"] = nc.dram_tensor("wgT", [NC_D, 128, I3], F32).ap()
    dr["wuT"] = nc.dram_tensor("wuT", [NC_D, 128, I3], F32).ap()
    dr["wdT"] = nc.dram_tensor("wdT", [NI, 128, D], F32).ap()
    dr["up"] = nc.dram_tensor("up_sp", [NI, 128, T], F32).ap()
    dr["gate"] = nc.dram_tensor("gate_sp", [NI, 128, T], F32).ap()
    dr["qu"] = nc.dram_tensor("qu_sp", [NI, 128, T], F32).ap()
    dr["qg"] = nc.dram_tensor("qg_sp", [NI, 128, T], F32).ap()
    dr["qt"] = nc.dram_tensor("qt_sp", [NI, 128, T], F32).ap()
    dr["ag_in"] = nc.dram_tensor("ag_in", [NC_D, 128, T], F32).ap()
    dr["ag_out"] = nc.dram_tensor("ag_out", [2, NC_D, 128, T], F32).ap()
    dr["qsp"] = nc.dram_tensor("q_sp", [32, 128, 512], F32).ap()
    dr["ksp"] = nc.dram_tensor("k_sp", [32, 128, 512], F32).ap()
    dr["vsp"] = nc.dram_tensor("v_sp", [32, 128, 512], F32).ap()
    dr["amine"] = nc.dram_tensor("amine", [32, 128, 512], F32).ap()
    dr["rs_in"] = nc.dram_tensor("rs_in", [S, D], F32).ap()
    dr["rs_out"] = nc.dram_tensor("rs_out", [T, D], F32).ap()
    dr["ar_in"] = nc.dram_tensor("ar_in", [256], F32).ap()
    dr["ar_out"] = nc.dram_tensor("ar_out", [256], F32).ap()
    dr["br"] = nc.dram_tensor("br", [16], F32).ap()
    dr["row"] = nc.dram_tensor("rowb", [4096], F32).ap()

    with ExitStack() as ctx:
        e = ctx.enter_context

        xres = e(nc.sbuf_tensor("xres", [128, NC_D, T], F32))
        xlnT = e(nc.sbuf_tensor("xln", [128, NC_D * T], F32))
        stg = e(nc.sbuf_tensor("stg", [128, 4096], F32))
        stg2 = e(nc.sbuf_tensor("stg2", [128, 4096], F32))
        pool = e(nc.sbuf_tensor("pool", [128, 4096], F32))
        wta = e(nc.sbuf_tensor("wta", [128, 1024], F32))
        wtb = e(nc.sbuf_tensor("wtb", [128, 1024], F32))
        ident = e(nc.sbuf_tensor("ident", [128, 128], F32))
        cbias = e(nc.sbuf_tensor("cbias", [128, 8], F32))
        scal = e(nc.sbuf_tensor("scal", [128, 16], F32))
        sc1 = e(nc.sbuf_tensor("sc1", [1, 384], F32))
        mcol = e(nc.sbuf_tensor("mcol", [128, 224], F32))
        bscale = e(nc.sbuf_tensor("bscale", [128, 96], F32))

        xln = xlnT[:, :].rearrange("p (c t) -> p c t", c=NC_D)
        kfeat = xlnT[:, 0:4096]
        qfeat = xlnT[:, 4096:8192]
        vbufF = xlnT[:, 8192:12352]
        vbuf = vbufF.rearrange("p (a b) -> p a b", b=130)
        bmask = xlnT[:, 12352:13248]
        orow = xlnT[:, 13248:14272]
        mubc = pool[:, 0:2048]
        rsbc = pool[:, 2048:4096]
        ropeC = pool[:, 0:1024].rearrange("p (a b) -> p a b", b=32)
        ropeS = pool[:, 1024:2048].rearrange("p (a b) -> p a b", b=32)
        ptile = pool[:, 2048:4096].rearrange("p (a b) -> p a b", b=512)

        psA = e(nc.psum_tensor("psA", [128, 4, 512], F32))
        psB = e(nc.psum_tensor("psB", [128, 4, 512], F32))

        ld = Sem(ctx, nc, "ld")
        sp = Sem(ctx, nc, "sp")
        mm = Sem(ctx, nc, "mm")
        ev = Sem(ctx, nc, "ev")
        vx = Sem(ctx, nc, "vx")
        gx = Sem(ctx, nc, "gx")
        cc = Sem(ctx, nc, "cc")
        bs = Sem(ctx, nc, "bs")
        sw = Sem(ctx, nc, "sw")
        blk0 = e(nc.Block())

        def vrecip(v, out_ap, in_ap):
            v.reciprocal(out_ap, in_ap).then_inc(sw.h, 1)
            v.wait_ge(sw.h, sw.inc())

        def ss(v):
            v._last.then_inc(sw.h, 1)
            v.wait_ge(sw.h, sw.inc())

        class BlkW:
            def __getattr__(self, name):
                def reg(f):
                    getattr(blk0, name)(lambda eng: f(EW(eng)))
                return reg
        blk = BlkW()

        def barrier():
            tgt = bs.n + 1
            bs.n = tgt

            def gpf(en, targets=(ld.n, sp.n, cc.n, mm.n, ev.n, vx.n, gx.n),
                    t=tgt):
                for sem, val in zip((ld, sp, cc, mm, ev, vx, gx), targets):
                    if val:
                        en.wait_ge(sem.h, val)
                en.sem_inc(bs.h, 1)
            blk.gpsimd(gpf)
            for name in ("sync", "vector", "scalar", "tensor", "gpsimd"):
                def f(en, t=tgt):
                    en.wait_ge(bs.h, t)
                getattr(blk, name)(f)

        def mktap(name, src_ap, shape, wait=None):
            if name not in tap:
                return
            tp = nc.dram_tensor("tap_" + name, shape, F32, kind="ExternalOutput").ap()
            tap_outs[name] = tp

            def gp(g):
                if wait is not None:
                    g.wait_ge(wait[0].h, wait[1])
                g.dma_start(out=tp, in_=src_ap).then_inc(sp.h, 16)
            blk.gpsimd(gp)
            sp.inc(16)
            barrier()

        def p0_consts():
            def vec(v):
                v.memset(cbias[:, :], 0.0)
                ss(v)
                v.memset(cbias[:, 0:1], -C_EXP)
                v.memset(cbias[:, 1:2], EPS_LN)
                v.memset(scal[:, 15:16], 1.0)
                v.memset(stg[:, 0:128], 1.0)
                v.fence(vx.h)
            blk.vector(vec)
            vx.inc()

            def gp(g, t=vx.n):
                g.wait_ge(vx.h, t)
                g.affine_select(ident[:, :], stg[:, 0:128], [[1, 128]],
                                ALU.is_equal, 0.0, base=0, channel_multiplier=-1)
                g.fence(gx.h)
            blk.gpsimd(gp)
            gx.inc()

        def p0_input():
            for tt in range(16):
                buf = stg if tt % 2 == 0 else stg2

                def ldf(sy, tt=tt, buf=buf, t_mm=mm.n):
                    sy.wait_ge(mm.h, t_mm)
                    sy.dma_start(out=buf[:, 0:1024],
                                 in_=x_in[tt * 128:(tt + 1) * 128, :]).then_inc(ld.h, 16)
                blk.sync(ldf)
                t_ld = ld.inc(16)

                for c in range(NC_D):
                    def tens(t, tt=tt, buf=buf, c=c, t_ld=t_ld, t_ev=ev.n):
                        if c == 0:
                            t.wait_ge(ld.h, t_ld)
                        t.wait_ge(ev.h, t_ev)
                        t.transpose(psB[:, c % 4, 0:128],
                                    buf[:, c * 128:(c + 1) * 128],
                                    ident[:, :]).then_inc(mm.h, 1)
                    blk.tensor(tens)
                    t_mm = mm.inc()

                    def sc(v, tt=tt, c=c, t_mm=t_mm):
                        v.wait_ge(mm.h, t_mm)
                        v.tensor_copy(xres[:, c, tt * 128:(tt + 1) * 128],
                                      psB[:, c % 4, 0:128]).then_inc(ev.h, 1)
                    blk.vector(sc)
                    ev.inc()

        def prep_weight(w_ap, rows, dst, lpbq):
            cols = w_ap.shape[1]
            ncol = cols // 128
            wrow = pool[:, 0:cols]
            for r in range(rows // 128):
                def ldf(sy, r=r, t_mm=mm.n):
                    sy.wait_ge(mm.h, t_mm)
                    sy.dma_start(out=wrow[:, :],
                                 in_=w_ap[r * 128:(r + 1) * 128, :]).then_inc(ld.h, 16)
                blk.sync(ldf)
                t_ld = ld.inc(16)

                if lpbq:
                    nb = cols // 32

                    def vec(v, t_ld=t_ld, nb=nb):
                        v.wait_ge(ld.h, t_ld)
                        wv = wrow.rearrange("p (b k) -> p b k", k=32)
                        v.tensor_reduce(bscale[:, 0:nb], wv, AX, ALU.max,
                                        apply_absolute_value=True)
                        ss(v)
                        v.tensor_scalar(mcol[:, 0:nb], bscale[:, 0:nb], R7A,
                                        None, ALU.mult)
                        ss(v)
                        v.scalar_tensor_tensor(mcol[:, 0:nb], bscale[:, 0:nb],
                                               R7B, mcol[:, 0:nb],
                                               ALU.mult, ALU.add)
                        ss(v)
                        vrecip(v, bscale[:, 0:nb], mcol[:, 0:nb])
                        v.tensor_tensor(wv, wv,
                                        bscale[:, 0:nb].unsqueeze(2)
                                        .to_broadcast([128, nb, 32]), ALU.mult)
                        ss(v)
                        v.tensor_scalar(wv, wv, MAGIC, MAGIC, ALU.add, ALU.subtract)
                        ss(v)
                        v.tensor_tensor(wv, wv,
                                        mcol[:, 0:nb].unsqueeze(2)
                                        .to_broadcast([128, nb, 32]), ALU.mult)
                        v.fence(vx.h)
                    blk.vector(vec)
                    rdy = ("vx", vx.inc())
                else:
                    rdy = ("ld", t_ld)

                for c in range(ncol):
                    def tens(t, c=c, rdy=rdy, t_ev=ev.n):
                        if c == 0:
                            t.wait_ge({"vx": vx, "ld": ld}[rdy[0]].h, rdy[1])
                        t.wait_ge(ev.h, t_ev)
                        t.transpose(psB[:, c % 4, 0:128],
                                    wrow[:, c * 128:(c + 1) * 128],
                                    ident[:, :]).then_inc(mm.h, 1)
                    blk.tensor(tens)
                    t_mm = mm.inc()

                    def sc(v, c=c, t_mm=t_mm, t_sp=sp.n):
                        v.wait_ge(mm.h, t_mm)
                        v.wait_ge(sp.h, t_sp)
                        v.tensor_copy(stg2[:, (c % 8) * 128:(c % 8) * 128 + 128],
                                      psB[:, c % 4, 0:128]).then_inc(ev.h, 1)
                    blk.vector(sc)
                    t_ev = ev.inc()

                    def gp(g, c=c, r=r, t_ev=t_ev):
                        g.wait_ge(ev.h, t_ev)
                        g.dma_start(out=dst[c, :, r * 128:(r + 1) * 128],
                                    in_=stg2[:, (c % 8) * 128:(c % 8) * 128 + 128]
                                    ).then_inc(sp.h, 16)
                    blk.gpsimd(gp)
                    sp.inc(16)

        def p0_weights():
            prep_weight(wq_in, 512, dr["wqT"], False)
            prep_weight(wk_in, 512, dr["wkT"], False)
            prep_weight(wv_in, 512, dr["wvT"], False)
            prep_weight(wo_in, D, dr["woT"], False)
            prep_weight(wu_in, I3, dr["wuT"], True)
            prep_weight(wg_in, I3, dr["wgT"], True)
            prep_weight(wd_in, D, dr["wdT"], True)

        def layer_norm(tag):
            for g4 in range(4):
                sl = slice(g4 * 512, (g4 + 1) * 512)

                def sc(v, sl=sl, t_vx=vx.n):
                    v.wait_ge(vx.h, t_vx)
                    v.tensor_tensor(stg[:, :].rearrange("p (c t) -> p c t", c=NC_D),
                                    xres[:, :, sl], xres[:, :, sl],
                                    ALU.mult).then_inc(ev.h, 1)
                blk.vector(sc)
                t_sq = ev.inc()

                def tens(t, sl=sl, t_sq=t_sq, t_vx=vx.n):
                    t.wait_ge(ev.h, t_sq)
                    t.wait_ge(vx.h, t_vx)
                    for c in range(NC_D):
                        t.matmul(psB[0:1, 0, 0:512], scal[:, 15:16],
                                 xres[:, c, sl], start=(c == 0), stop=(c == 7))
                    for c in range(NC_D):
                        t.matmul(psB[0:1, 1, 0:512], scal[:, 15:16],
                                 stg[:, c * 512:(c + 1) * 512],
                                 start=(c == 0), stop=(c == 7))
                    t.fence(mm.h)
                blk.tensor(tens)
                t_mm = mm.inc()

                def vec(v, sl=sl, t_mm=t_mm):
                    v.wait_ge(mm.h, t_mm)
                    v.tensor_scalar(mubc[0:1, sl], psB[0:1, 0, 0:512],
                                    1.0 / 1024.0, None, ALU.mult)
                    v.tensor_scalar(rsbc[0:1, sl],
                                    psB[0:1, 1, 0:512], 1.0 / 1024.0, None, ALU.mult)
                    ss(v)
                    v.tensor_tensor(stg2[0:1, 0:512], mubc[0:1, sl], mubc[0:1, sl],
                                    ALU.mult)
                    ss(v)
                    v.tensor_tensor(rsbc[0:1, sl], rsbc[0:1, sl], stg2[0:1, 0:512],
                                    ALU.subtract)
                    v.fence(vx.h)
                blk.vector(vec)
                t_vx = vx.inc()

                def sc2(s, sl=sl, t_vx=t_vx):
                    s.wait_ge(vx.h, t_vx)
                    s.activation(rsbc[0:1, sl], rsbc[0:1, sl], AF.Ln,
                                 bias=cbias[0:1, 1:2], scale=1.0).then_inc(sw.h, 1)
                    s.wait_ge(sw.h, sw.inc())
                    s.activation(rsbc[0:1, sl], rsbc[0:1, sl], AF.Exp,
                                 bias=cbias[0:1, 2:3], scale=-0.5).then_inc(ev.h, 1)
                blk.scalar(sc2)
                t_ev = ev.inc()

                def vec2(v, sl=sl, t_ev=t_ev):
                    v.wait_ge(ev.h, t_ev)
                    v.fence(vx.h) if False else v.tensor_copy(
                        sc1[0:1, 380:381], rsbc[0:1, sl.start:sl.start + 1])
                    v.fence(vx.h)
                blk.vector(vec2)
                vx.inc()

            def gp(g, t_vx=vx.n):
                g.wait_ge(vx.h, t_vx)
                g.dma_start(out=dr["row"][0:2048], in_=mubc[0:1, :]).then_inc(sp.h, 16)
                g.dma_start(out=dr["row"][2048:4096], in_=rsbc[0:1, :]).then_inc(sp.h, 16)
                g.wait_ge(sp.h, sp.inc(32))
                g.dma_start(out=mubc[:, :],
                            in_=dr["row"][0:2048].unsqueeze(0).to_broadcast([128, 2048])).then_inc(sp.h, 16)
                g.dma_start(out=rsbc[:, :],
                            in_=dr["row"][2048:4096].unsqueeze(0).to_broadcast([128, 2048])).then_inc(sp.h, 16)
                g.wait_ge(sp.h, sp.inc(32))
                g.sem_inc(gx.h, 1)
            blk.gpsimd(gp)
            gx.inc()

            def vec3(v, t_gx=gx.n):
                v.wait_ge(gx.h, t_gx)
                for c in range(NC_D):
                    v.tensor_tensor(xln[:, c, :], xres[:, c, :], mubc[:, :],
                                    ALU.subtract)
                    ss(v)
                    v.tensor_tensor(xln[:, c, :], xln[:, c, :], rsbc[:, :],
                                    ALU.mult)
                v.fence(vx.h)
            blk.vector(vec3)
            vx.inc()
            mktap("xln_" + tag, xlnT[:, :], [128, NC_D * T], wait=(vx, vx.n))

        def allreduce_max(n_vals, compute_scalars, nbc):
            def gp(g, t_vx=vx.n, n_vals=n_vals):
                g.wait_ge(vx.h, t_vx)
                for j in range(n_vals):
                    g.dma_start(out=dr["ar_in"][j * 128:(j + 1) * 128],
                                in_=mcol[:, 200 + j:201 + j]).then_inc(sp.h, 16)
                g.wait_ge(sp.h, sp.inc(16 * n_vals))
                g.collective_compute(
                    "AllReduce", ALU.max,
                    replica_groups=[[0, 1, 2, 3, 4, 5, 6, 7]],
                    ins=[dr["ar_in"][0:n_vals * 128].opt()],
                    outs=[dr["ar_out"][0:n_vals * 128].opt()],
                ).then_inc(cc.h, 1)
                g.wait_ge(cc.h, cc.inc())
                g.dma_start(out=sc1[0:1, 0:n_vals * 128],
                            in_=dr["ar_out"][0:n_vals * 128]).then_inc(sp.h, 16)
                g.wait_ge(sp.h, sp.inc(16))
                g.sem_inc(gx.h, 1)
            blk.gpsimd(gp)
            gx.inc()

            def vec(v, t_gx=gx.n, n_vals=n_vals):
                v.wait_ge(gx.h, t_gx)
                for j in range(n_vals):
                    r = v.tensor_reduce(sc1[0:1, 320 + j:321 + j],
                                        sc1[0:1, j * 128:(j + 1) * 128], AX, ALU.max)
                r.then_inc(sw.h, 1)
                v.wait_ge(sw.h, sw.inc())
                compute_scalars(v)
                v.fence(vx.h)
            blk.vector(vec)
            vx.inc()

            def gp2(g, t_vx=vx.n, nbc=nbc):
                g.wait_ge(vx.h, t_vx)
                g.dma_start(out=dr["br"][0:nbc], in_=sc1[0:1, 340:340 + nbc]).then_inc(sp.h, 16)
                g.wait_ge(sp.h, sp.inc(16))
                g.dma_start(out=scal[:, 0:nbc],
                            in_=dr["br"][0:nbc].unsqueeze(0).to_broadcast([128, nbc])).then_inc(sp.h, 16)
                g.wait_ge(sp.h, sp.inc(16))
                g.sem_inc(gx.h, 1)
            blk.gpsimd(gp2)
            gx.inc()
            barrier()

        def ffn_call(tag):
            sub = FFN_SUB
            layer_norm(tag)
            if sub < 2:
                return

            def vecm(v, t_vx=vx.n):
                v.wait_ge(vx.h, t_vx)
                v.tensor_reduce(mcol[:, 200:201],
                                xlnT[:, :], AX, ALU.max,
                                apply_absolute_value=True)
                v.fence(vx.h)
            blk.vector(vecm)
            vx.inc()

            def scalars1(v):
                vrecip(v, sc1[0:1, 340:341], sc1[0:1, 340:341] if False else sc1[0:1, 320:321])
                v.tensor_scalar(sc1[0:1, 340:341], sc1[0:1, 340:341], QMAX, None,
                                ALU.mult)
                ss(v)
                v.tensor_scalar(sc1[0:1, 300:301], sc1[0:1, 320:321], 1.0 / QMAX,
                                None, ALU.mult)
            allreduce_max(1, scalars1, 1)
            mktap("sc1_" + tag, sc1[0:1, :], [1, 384], wait=(gx, gx.n))
            mktap("scal_" + tag, scal[:, :], [128, 16], wait=(gx, gx.n))
            mktap("mcol_" + tag, mcol[:, :], [128, 224], wait=(gx, gx.n))
            if sub < 3:
                return

            def vecq(v):
                v.tensor_scalar(xln[:, :, :], xln[:, :, :], scal[:, 0:1], MAGIC,
                                ALU.mult, ALU.add)
                ss(v)
                v.tensor_scalar(xln[:, :, :], xln[:, :, :], MAGIC, None,
                                ALU.subtract)
                v.fence(vx.h)
            blk.vector(vecq)
            vx.inc()
            mktap("qx_" + tag, xlnT[:, :], [128, NC_D * T], wait=(vx, vx.n))
            barrier()
            if sub < 4:
                return

            bT = {"mm": {}, "ev": {}, "vx": {}, "sp": {}}
            step = [0]
            for which, wsrc, spdst, mbase in (("u", dr["wuT"], dr["up"], 8),
                                              ("g", dr["wgT"], dr["gate"], 40)):
                for i in range(NI):
                    k_ = step[0]
                    par = k_ % 2
                    wbuf = wta if par == 0 else wtb
                    ps = psA if par == 0 else psB
                    sbuf = stg if par == 0 else stg2

                    def ldf(sy, i=i, wbuf=wbuf, wsrc=wsrc,
                            t_mm=bT["mm"].get(k_ - 2, 0)):
                        if t_mm:
                            sy.wait_ge(mm.h, t_mm)
                        sy.dma_start(out=wbuf[:, :].rearrange("p (c n) -> p c n", c=NC_D),
                                     in_=wsrc[:, :, i * 128:(i + 1) * 128]
                                     .rearrange("c p n -> p c n")).then_inc(ld.h, 16)
                    blk.sync(ldf)
                    t_ld = ld.inc(16)

                    def tens(t, wbuf=wbuf, ps=ps, t_ld=t_ld,
                             t_ev=bT["ev"].get(k_ - 2, 0)):
                        t.wait_ge(ld.h, t_ld)
                        if t_ev:
                            t.wait_ge(ev.h, t_ev)
                        for c in range(NC_D):
                            for tb in range(4):
                                t.matmul(ps[:, tb, :], wbuf[:, c * 128:(c + 1) * 128],
                                         xln[:, c, tb * 512:(tb + 1) * 512],
                                         start=(c == 0), stop=(c == 7))
                        t.fence(mm.h)
                    blk.tensor(tens)
                    t_mm = mm.inc()
                    bT["mm"][k_] = t_mm

                    def sc(v, ps=ps, sbuf=sbuf, t_mm=t_mm,
                           t_sp=bT["sp"].get(k_ - 2, 0),
                           t_vx=bT["vx"].get(k_ - 2, 0)):
                        v.wait_ge(mm.h, t_mm)
                        if t_sp:
                            v.wait_ge(sp.h, t_sp)
                        if t_vx:
                            v.wait_ge(vx.h, t_vx)
                        v.tensor_copy(sbuf[:, 0:2048],
                                      ps[:, :, :].rearrange("p a b -> p (a b)")
                                      ).then_inc(ev.h, 1)
                    blk.vector(sc)
                    t_ev = ev.inc()
                    bT["ev"][k_] = t_ev

                    def vecr(v, i=i, sbuf=sbuf, mbase=mbase, t_ev=t_ev):
                        v.wait_ge(ev.h, t_ev)
                        v.tensor_reduce(mcol[:, mbase + i:mbase + i + 1],
                                        sbuf[:, 0:2048], AX, ALU.max,
                                        apply_absolute_value=True)
                        v.fence(vx.h)
                    blk.vector(vecr)
                    bT["vx"][k_] = vx.inc()

                    def gp(g, i=i, sbuf=sbuf, spdst=spdst, t_ev=t_ev):
                        g.wait_ge(ev.h, t_ev)
                        g.dma_start(out=spdst[i, :, :],
                                    in_=sbuf[:, 0:2048]).then_inc(sp.h, 16)
                    blk.gpsimd(gp)
                    bT["sp"][k_] = sp.inc(16)
                    step[0] += 1

                def vecf(v, mbase=mbase, which=which, t_vx=vx.n):
                    v.wait_ge(vx.h, t_vx)
                    col = 200 if which == "u" else 201
                    v.tensor_reduce(mcol[:, col:col + 1],
                                    mcol[:, mbase:mbase + NI], AX, ALU.max)
                    v.fence(vx.h)
                blk.vector(vecf)
                vx.inc()

            def scalars2(v):
                v.reciprocal(sc1[0:1, 340:341], sc1[0:1, 320:321])
                vrecip(v, sc1[0:1, 341:342], sc1[0:1, 321:322])
                v.tensor_scalar(sc1[0:1, 340:342], sc1[0:1, 340:342], QMAX, None,
                                ALU.mult)
                ss(v)
                v.tensor_scalar(sc1[0:1, 301:303], sc1[0:1, 320:322], 1.0 / QMAX,
                                None, ALU.mult)
                ss(v)
                v.tensor_tensor(sc1[0:1, 301:303], sc1[0:1, 301:303],
                                sc1[0:1, 300:301].to_broadcast([1, 2]), ALU.mult)
                ss(v)
                v.tensor_copy(sc1[0:1, 342:343], sc1[0:1, 302:303])
                ss(v)
                v.tensor_scalar(sc1[0:1, 344:345], sc1[0:1, 302:303], -1.0, None,
                                ALU.mult)
            allreduce_max(2, scalars2, 5)
            if ("upE_" + tag) in tap:
                def ldt(sy, t_mm=mm.n):
                    sy.dma_start(out=stg[:, 0:2048],
                                 in_=dr["up"][0, :, :]).then_inc(ld.h, 16)
                blk.sync(ldt)
                mktap("upE_" + tag, stg[:, 0:2048], [128, 2048],
                      wait=(ld, ld.inc(16)))
            if sub < 5:
                return

            sgb = pool[:, 0:2048]
            for i in range(NI):
                b0 = stg if i % 2 == 0 else stg2

                def ldf(sy, i=i, b0=b0, t_vx=vx.n, t_sp=sp.n):
                    sy.wait_ge(vx.h, t_vx)
                    sy.wait_ge(sp.h, t_sp)
                    sy.dma_start(out=b0[:, 0:2048], in_=dr["up"][i, :, :]).then_inc(ld.h, 16)
                    sy.dma_start(out=b0[:, 2048:4096], in_=dr["gate"][i, :, :]).then_inc(ld.h, 16)
                blk.sync(ldf)
                t_ld = ld.inc(32)

                def vec(v, b0=b0, t_ld=t_ld):
                    v.wait_ge(ld.h, t_ld)
                    v.tensor_scalar(b0[:, 0:2048], b0[:, 0:2048], scal[:, 0:1],
                                    MAGIC, ALU.mult, ALU.add)
                    ss(v)
                    v.tensor_scalar(b0[:, 0:2048], b0[:, 0:2048], MAGIC, None,
                                    ALU.subtract)
                    v.tensor_scalar(b0[:, 2048:4096], b0[:, 2048:4096], scal[:, 1:2],
                                    MAGIC, ALU.mult, ALU.add)
                    ss(v)
                    v.tensor_scalar(b0[:, 2048:4096], b0[:, 2048:4096], MAGIC, None,
                                    ALU.subtract)
                    v.fence(vx.h)
                blk.vector(vec)
                t_q = vx.inc()

                def gp(g, i=i, b0=b0, t_q=t_q):
                    g.wait_ge(vx.h, t_q)
                    g.dma_start(out=dr["qu"][i, :, :], in_=b0[:, 0:2048]).then_inc(sp.h, 16)
                    g.dma_start(out=dr["qg"][i, :, :], in_=b0[:, 2048:4096]).then_inc(sp.h, 16)
                blk.gpsimd(gp)
                sp.inc(32)

                def sc(s, b0=b0, t_q=t_q, t_vx2=vx.n):
                    s.wait_ge(vx.h, t_q)
                    s.activation(sgb, b0[:, 2048:4096], AF.Exp,
                                 bias=cbias[:, 2:3], scale=scal[:, 4:5]).then_inc(ev.h, 1)
                blk.scalar(sc)
                t_sg = ev.inc()

                def vec2(v, i=i, b0=b0, t_sg=t_sg):
                    v.wait_ge(ev.h, t_sg)
                    v.tensor_scalar(sgb, sgb, 1.0, None, ALU.add)
                    ss(v)
                    vrecip(v, sgb, sgb)
                    v.tensor_scalar(sgb, sgb, 65536.0, MAGIC, ALU.mult, ALU.add)
                    ss(v)
                    v.tensor_scalar(sgb, sgb, MAGIC, None, ALU.subtract)
                    ss(v)
                    v.tensor_tensor(sgb, b0[:, 2048:4096], sgb, ALU.mult)
                    ss(v)
                    v.tensor_reduce(mcol[:, 72 + i:73 + i], sgb, AX, ALU.max,
                                    apply_absolute_value=True)
                    v.fence(vx.h)
                blk.vector(vec2)
                vx.inc()

            def vecf2(v, t_vx=vx.n):
                v.wait_ge(vx.h, t_vx)
                v.tensor_reduce(mcol[:, 200:201], mcol[:, 72:72 + NI], AX, ALU.max)
                v.fence(vx.h)
            blk.vector(vecf2)
            vx.inc()

            def scalars3(v):
                vrecip(v, sc1[0:1, 340:341], sc1[0:1, 320:321])
                v.tensor_scalar(sc1[0:1, 340:341], sc1[0:1, 340:341], QMAX, None,
                                ALU.mult)
                ss(v)
                v.tensor_scalar(sc1[0:1, 303:304], sc1[0:1, 320:321],
                                1.0 / 65536.0 / QMAX, None, ALU.mult)
                ss(v)
                v.tensor_tensor(sc1[0:1, 303:304], sc1[0:1, 303:304],
                                sc1[0:1, 302:303], ALU.mult)
                ss(v)
                v.tensor_copy(sc1[0:1, 341:342], sc1[0:1, 302:303])
                ss(v)
                v.tensor_scalar(sc1[0:1, 344:345], sc1[0:1, 302:303], -1.0, None,
                                ALU.mult)
            allreduce_max(1, scalars3, 5)
            if sub < 6:
                return

            for i in range(NI):
                b0 = stg if i % 2 == 0 else stg2

                def ldf(sy, i=i, b0=b0, t_vx=vx.n):
                    sy.wait_ge(vx.h, t_vx)
                    sy.dma_start(out=b0[:, 0:2048], in_=dr["qu"][i, :, :]).then_inc(ld.h, 16)
                    sy.dma_start(out=b0[:, 2048:4096], in_=dr["qg"][i, :, :]).then_inc(ld.h, 16)
                blk.sync(ldf)
                t_ld = ld.inc(32)

                def sc(s, b0=b0, t_ld=t_ld, t_sp=sp.n, t_vx=vx.n):
                    s.wait_ge(ld.h, t_ld)
                    s.wait_ge(sp.h, t_sp)
                    s.wait_ge(vx.h, t_vx)
                    s.activation(sgb, b0[:, 2048:4096], AF.Exp,
                                 bias=cbias[:, 2:3], scale=scal[:, 4:5]).then_inc(ev.h, 1)
                blk.scalar(sc)
                t_sg = ev.inc()

                def vec(v, i=i, b0=b0, t_sg=t_sg):
                    v.wait_ge(ev.h, t_sg)
                    v.tensor_scalar(sgb, sgb, 1.0, None, ALU.add)
                    ss(v)
                    vrecip(v, sgb, sgb)
                    v.tensor_scalar(sgb, sgb, 65536.0, MAGIC, ALU.mult, ALU.add)
                    v.tensor_scalar(sgb, sgb, MAGIC, None, ALU.subtract)
                    v.tensor_tensor(sgb, b0[:, 2048:4096], sgb, ALU.mult)
                    v.tensor_scalar(sgb, sgb, scal[:, 0:1], MAGIC, ALU.mult, ALU.add)
                    v.tensor_scalar(sgb, sgb, MAGIC, None, ALU.subtract)
                    v.tensor_tensor(b0[:, 2048:4096], sgb, b0[:, 0:2048], ALU.mult)
                    v.tensor_reduce(mcol[:, 104 + i:105 + i], b0[:, 2048:4096],
                                    AX, ALU.max, apply_absolute_value=True)
                    v.fence(vx.h)
                blk.vector(vec)
                t_q = vx.inc()

                def gp(g, i=i, t_q=t_q):
                    g.wait_ge(vx.h, t_q)
                    g.dma_start(out=dr["qt"][i, :, :], in_=sgb).then_inc(sp.h, 16)
                blk.gpsimd(gp)
                sp.inc(16)

            def vecf3(v, t_vx=vx.n):
                v.wait_ge(vx.h, t_vx)
                v.tensor_reduce(mcol[:, 200:201], mcol[:, 104:104 + NI], AX, ALU.max)
                v.fence(vx.h)
            blk.vector(vecf3)
            vx.inc()

            def scalars4(v):
                vrecip(v, sc1[0:1, 340:341], sc1[0:1, 320:321])
                v.tensor_scalar(sc1[0:1, 340:341], sc1[0:1, 340:341], QMAX, None,
                                ALU.mult)
                ss(v)
                v.tensor_scalar(sc1[0:1, 304:305], sc1[0:1, 320:321], 1.0 / QMAX,
                                None, ALU.mult)
                ss(v)
                v.tensor_tensor(sc1[0:1, 304:305], sc1[0:1, 304:305],
                                sc1[0:1, 303:304], ALU.mult)
                ss(v)
                v.tensor_tensor(sc1[0:1, 304:305], sc1[0:1, 304:305],
                                sc1[0:1, 301:302], ALU.mult)
                ss(v)
                v.tensor_scalar(sc1[0:1, 341:342], sc1[0:1, 304:305], 0.5, None,
                                ALU.mult)
            allreduce_max(1, scalars4, 2)
            if sub < 7:
                return

            eT = {}
            for tb in range(4):
                for k in range(NI):
                    b0 = stg if k % 2 == 0 else stg2
                    ks_ = tb * NI + k

                    def ldf(sy, tb=tb, k=k, b0=b0, t_mm=eT.get(ks_ - 2, 0)):
                        if t_mm:
                            sy.wait_ge(mm.h, t_mm)
                        sy.dma_start(out=b0[:, 0:512],
                                     in_=dr["qt"][k, :, tb * 512:(tb + 1) * 512]).then_inc(ld.h, 16)
                        sy.dma_start(out=b0[:, 512:1024],
                                     in_=dr["qu"][k, :, tb * 512:(tb + 1) * 512]).then_inc(ld.h, 16)
                        sy.dma_start(out=b0[:, 1024:2048],
                                     in_=dr["wdT"][k, :, :]).then_inc(ld.h, 16)
                    blk.sync(ldf)
                    t_ld = ld.inc(48)

                    def vec(v, b0=b0, t_ld=t_ld):
                        v.wait_ge(ld.h, t_ld)
                        v.tensor_tensor(b0[:, 0:512], b0[:, 0:512], b0[:, 512:1024],
                                        ALU.mult)
                        v.tensor_scalar(b0[:, 0:512], b0[:, 0:512], scal[:, 0:1],
                                        MAGIC, ALU.mult, ALU.add)
                        v.tensor_scalar(b0[:, 0:512], b0[:, 0:512], MAGIC, None,
                                        ALU.subtract)
                        v.fence(vx.h)
                    blk.vector(vec)
                    t_oq = vx.inc()

                    def tens(t, tb=tb, k=k, b0=b0, t_oq=t_oq, t_vx=vx.n):
                        t.wait_ge(vx.h, t_oq)
                        if k == 0:
                            t.wait_ge(vx.h, t_vx)
                        for dti in range(8):
                            ps = psA if dti < 4 else psB
                            t.matmul(ps[:, dti % 4, :],
                                     b0[:, 1024 + dti * 128:1024 + (dti + 1) * 128],
                                     b0[:, 0:512], start=(k == 0), stop=(k == NI - 1))
                        t.fence(mm.h)
                    blk.tensor(tens)
                    eT[ks_] = mm.inc()

                def vecu(v, tb=tb, t_mm=mm.n):
                    v.wait_ge(mm.h, t_mm)
                    for dti in range(8):
                        ps = psA if dti < 4 else psB
                        v.scalar_tensor_tensor(
                            xres[:, dti, tb * 512:(tb + 1) * 512],
                            ps[:, dti % 4, :], scal[:, 1:2],
                            xres[:, dti, tb * 512:(tb + 1) * 512],
                            ALU.mult, ALU.add)
                    v.fence(vx.h)
                blk.vector(vecu)
                vx.inc()
            barrier()

            for nm in ("up", "gate", "qu", "qg", "qt", "up2"):
                tnm = nm + "0_" + tag
                if tnm in tap:
                    src = "up" if nm == "up2" else nm
                    def ldt(sy, src=src, t_mm=mm.n):
                        sy.dma_start(out=stg[:, 0:2048],
                                     in_=dr[src][0, :, :]).then_inc(ld.h, 16)
                    blk.sync(ldt)
                    mktap(tnm, stg[:, 0:2048], [128, 2048], wait=(ld, ld.inc(16)))

        def attention():
            layer_norm("a")
            def gp(g, t_vx=vx.n):
                g.wait_ge(vx.h, t_vx)
                g.dma_start(out=dr["ag_in"].rearrange("c p t -> p c t"),
                            in_=xln[:, :, :]).then_inc(sp.h, 16)
                g.wait_ge(sp.h, sp.inc(16))
                g.collective_compute(
                    "AllGather", ALU.bypass,
                    replica_groups=[[0, 1], [2, 3], [4, 5], [6, 7]],
                    ins=[dr["ag_in"].rearrange("c p t -> (c p t)").opt()],
                    outs=[dr["ag_out"].rearrange("h c p t -> (h c p t)").opt()],
                ).then_inc(cc.h, 1)
                g.wait_ge(cc.h, cc.inc())
                g.sem_inc(gx.h, 1)
            blk.gpsimd(gp)
            gx.inc()
            barrier()

            def gpi(g):
                g.iota(ropeC[:, :, 0:1], [[128, 32], [0, 1]], base=0,
                       channel_multiplier=1, allow_small_or_imprecise_dtypes=True)
                g.fence(gx.h)
            blk.gpsimd(gpi)
            gx.inc()

            def veci(v, t_gx=gx.n):
                v.wait_ge(gx.h, t_gx)
                v.memset(sc1[0:1, 256:257], 1.0)
                ss(v)
                for i in range(1, 32):
                    v.tensor_scalar(sc1[0:1, 256 + i:257 + i], sc1[0:1, 255 + i:256 + i],
                                    RINV, None, ALU.mult)
                    ss(v)
                v.tensor_copy(sc1[0:1, 379:380], sc1[0:1, 256:257])
                v.fence(vx.h)
            blk.vector(veci)
            vx.inc()

            def gpb(g, t_vx=vx.n):
                g.wait_ge(vx.h, t_vx)
                g.dma_start(out=dr["br"][0:16] if False else dr["ar_in"][0:32],
                            in_=sc1[0:1, 256:288]).then_inc(sp.h, 16)
                g.wait_ge(sp.h, sp.inc(16))
                g.dma_start(out=stg[:, 0:32],
                            in_=dr["ar_in"][0:32].unsqueeze(0).to_broadcast([128, 32])).then_inc(sp.h, 16)
                g.wait_ge(sp.h, sp.inc(16))
                g.sem_inc(gx.h, 1)
            blk.gpsimd(gpb)
            gx.inc()

            def vecr(v, t_gx=gx.n):
                v.wait_ge(gx.h, t_gx)
                for j in range(32):
                    v.tensor_tensor(ropeS[:, j, :],
                                    ropeC[:, j, 0:1].to_broadcast([128, 32]),
                                    stg[:, 0:32], ALU.mult)
                ss(v)
                v.tensor_scalar(ropeC[:, :, :], ropeS[:, :, :], INV_2PI, MAGIC,
                                ALU.mult, ALU.add)
                ss(v)
                v.tensor_scalar(ropeC[:, :, :], ropeC[:, :, :], MAGIC, None,
                                ALU.subtract)
                ss(v)
                v.scalar_tensor_tensor(ropeS[:, :, :], ropeC[:, :, :], -CW1,
                                       ropeS[:, :, :], ALU.mult, ALU.add)
                ss(v)
                v.scalar_tensor_tensor(ropeS[:, :, :], ropeC[:, :, :], -CW2,
                                       ropeS[:, :, :], ALU.mult, ALU.add)
                ss(v)
                v.tensor_scalar(ropeS[:, :, :], ropeS[:, :, :], PI_F, -PI_F,
                                ALU.min, ALU.max)
                v.fence(vx.h)
            blk.vector(vecr)
            vx.inc()

            def scs(s, t_vx=vx.n):
                s.wait_ge(vx.h, t_vx)
                s.activation(stg2[:, 0:1024].rearrange("p (a b) -> p a b", b=32),
                             ropeS[:, :, :], AF.Sin, bias=cbias[:, 2:3],
                             scale=1.0).then_inc(ev.h, 1)
            blk.scalar(scs)
            ev.inc()

            def vecc(v, t_ev=ev.n):
                v.wait_ge(ev.h, t_ev)
                v.tensor_scalar(ropeS[:, :, :], ropeS[:, :, :], HALF_PI, None,
                                ALU.add)
                ss(v)
                v.tensor_scalar(stg[:, 32:1056].rearrange("p (a b) -> p a b", b=32),
                                ropeS[:, :, :], PI_F, TWO_PI, ALU.is_gt, ALU.mult)
                ss(v)
                v.tensor_tensor(ropeS[:, :, :], ropeS[:, :, :],
                                stg[:, 32:1056].rearrange("p (a b) -> p a b", b=32),
                                ALU.subtract)
                ss(v)
                v.tensor_scalar(ropeS[:, :, :], ropeS[:, :, :], PI_F, -PI_F,
                                ALU.min, ALU.max)
                v.fence(vx.h)
            blk.vector(vecc)
            vx.inc()

            def scc(s, t_vx=vx.n):
                s.wait_ge(vx.h, t_vx)
                s.activation(ropeC[:, :, :], ropeS[:, :, :], AF.Sin,
                             bias=cbias[:, 2:3], scale=1.0).then_inc(ev.h, 1)
            blk.scalar(scc)
            ev.inc()

            def vecc2(v, t_ev=ev.n):
                v.wait_ge(ev.h, t_ev)
                v.tensor_copy(ropeS[:, :, :],
                              stg2[:, 0:1024].rearrange("p (a b) -> p a b", b=32))
                v.fence(vx.h)
            blk.vector(vecc2)
            vx.inc()
            barrier()

            def ldw(sy):
                sy.dma_start(out=kfeat.rearrange("p (c n) -> p c n", c=NC_D),
                             in_=dr["wqT"].rearrange("c p n -> p c n")).then_inc(ld.h, 16)
                sy.dma_start(out=qfeat.rearrange("p (c n) -> p c n", c=NC_D),
                             in_=dr["wkT"].rearrange("c p n -> p c n")).then_inc(ld.h, 16)
                sy.dma_start(out=vbufF[:, 0:4096].rearrange("p (c n) -> p c n", c=NC_D),
                             in_=dr["wvT"].rearrange("c p n -> p c n")).then_inc(ld.h, 16)
            blk.sync(ldw)
            t_w = ld.inc(48)

            qT = {"mm": {}, "ev": {}}
            for tt in range(32):
                xa = stg if tt % 2 == 0 else stg2
                pq = psA if tt % 2 == 0 else psB
                half, lt = tt // 16, tt % 16

                def ldf(sy, xa=xa, half=half, lt=lt, t_mm=mm.n):
                    sy.wait_ge(mm.h, t_mm)
                    sy.dma_start(out=xa[:, 0:1024].rearrange("p (c t) -> p c t", c=NC_D),
                                 in_=dr["ag_out"][half, :, :, lt * 128:(lt + 1) * 128]
                                 .rearrange("c p t -> p c t")).then_inc(ld.h, 16)
                blk.sync(ldf)
                t_ld = ld.inc(16)

                def tens(t, xa=xa, pq=pq, t_ld=t_ld, t_w=t_w, t_ev=ev.n):
                    t.wait_ge(ld.h, max(t_ld, t_w))
                    t.wait_ge(ev.h, t_ev)
                    for c in range(NC_D):
                        t.matmul(pq[:, 0, :], xa[:, c * 128:(c + 1) * 128],
                                 kfeat[:, c * 512:(c + 1) * 512],
                                 start=(c == 0), stop=(c == 7))
                        t.matmul(pq[:, 1, :], xa[:, c * 128:(c + 1) * 128],
                                 qfeat[:, c * 512:(c + 1) * 512],
                                 start=(c == 0), stop=(c == 7))
                        t.matmul(pq[:, 2, :], xa[:, c * 128:(c + 1) * 128],
                                 vbufF[:, c * 512:(c + 1) * 512],
                                 start=(c == 0), stop=(c == 7))
                    t.fence(mm.h)
                blk.tensor(tens)
                t_mm = mm.inc()
                qT["mm"][tt] = t_mm

                def sc(v, pq=pq, t_mm=t_mm, t_sp=sp.n):
                    v.wait_ge(mm.h, t_mm)
                    v.wait_ge(sp.h, t_sp)
                    v.tensor_copy(ptile[:, 0, :], pq[:, 0, :])
                    v.tensor_copy(ptile[:, 1, :], pq[:, 1, :])
                    v.tensor_copy(ptile[:, 2, :], pq[:, 2, :]).then_inc(ev.h, 1)
                blk.vector(sc)
                t_ev = ev.inc()
                qT["ev"][tt] = t_ev

                def vec(v, tt=tt, t_ev=t_ev):
                    v.wait_ge(ev.h, t_ev)
                    co = ropeC[:, tt, :]
                    si = ropeS[:, tt, :]
                    for which in (0, 1):
                        q4 = ptile[:, which, :].rearrange("p (h f two) -> p h f two",
                                                          h=8, two=2)
                        r4 = ptile[:, 3, :].rearrange("p (h f two) -> p h f two",
                                                      h=8, two=2)
                        for h in range(8):
                            v.scalar_tensor_tensor(r4[:, h, :, 0], q4[:, h, :, 1],
                                                   -1.0, si, ALU.mult, ALU.mult)
                            v.tensor_tensor(r4[:, h, :, 1], q4[:, h, :, 0], si,
                                            ALU.mult)
                            ss(v)
                            v.tensor_tensor(q4[:, h, :, :], q4[:, h, :, :],
                                            co.unsqueeze(2).to_broadcast([128, 32, 2]),
                                            ALU.mult)
                        ss(v)
                        v.tensor_tensor(ptile[:, which, :], ptile[:, which, :],
                                        ptile[:, 3, :], ALU.add)
                        if which == 0:
                            ss(v)
                    v.fence(vx.h)
                blk.vector(vec)
                t_vx = vx.inc()

                def gp2(g, tt=tt, t_vx=t_vx):
                    g.wait_ge(vx.h, t_vx)
                    g.dma_start(out=dr["qsp"][tt, :, :], in_=ptile[:, 0, :]).then_inc(sp.h, 16)
                    g.dma_start(out=dr["ksp"][tt, :, :], in_=ptile[:, 1, :]).then_inc(sp.h, 16)
                    g.dma_start(out=dr["vsp"][tt, :, :], in_=ptile[:, 2, :]).then_inc(sp.h, 16)
                blk.gpsimd(gp2)
                sp.inc(48)
            barrier()

            def vecb(v):
                v.memset(bmask[:, :], 1.0)
                v.memset(vbuf[:, :, 64:65], 1.0)
                v.memset(vbuf[:, :, 129:130], 1.0)
                v.fence(vx.h)
            blk.vector(vecb)
            vx.inc()

            def gpm(g, t_vx=vx.n):
                g.wait_ge(vx.h, t_vx)
                g.affine_select(bmask[:, :], bmask[:, :], [[1, 896]], ALU.is_ge,
                                0.0, base=-384, channel_multiplier=-1)
                g.fence(gx.h)
            blk.gpsimd(gpm)
            gx.inc()
            barrier()

            for pair in range(4):
                for tt in range(32):
                    def ldf(sy, tt=tt, pair=pair, t_mm=mm.n, t_vx=vx.n):
                        sy.wait_ge(mm.h, t_mm)
                        sy.wait_ge(vx.h, t_vx)
                        sy.dma_start(out=stg[:, (tt % 2) * 128:(tt % 2) * 128 + 128],
                                     in_=dr["ksp"][tt, :, pair * 128:(pair + 1) * 128]
                                     ).then_inc(ld.h, 16)
                        sy.dma_start(out=stg[:, 256 + (tt % 2) * 128:256 + (tt % 2) * 128 + 128],
                                     in_=dr["qsp"][tt, :, pair * 128:(pair + 1) * 128]
                                     ).then_inc(ld.h, 16)
                        sy.dma_start(out=stg2[:, (tt % 4) * 128:(tt % 4) * 128 + 128],
                                     in_=dr["vsp"][tt, :, pair * 128:(pair + 1) * 128]
                                     ).then_inc(ld.h, 16)
                    blk.sync(ldf)
                    t_ld = ld.inc(48)

                    def tens(t, tt=tt, t_ld=t_ld, t_ev=ev.n):
                        t.wait_ge(ld.h, t_ld)
                        t.wait_ge(ev.h, t_ev)
                        t.transpose(psB[:, tt % 2, 0:128],
                                    stg[:, (tt % 2) * 128:(tt % 2) * 128 + 128],
                                    ident[:, :])
                        t.transpose(psB[:, 2 + tt % 2, 0:128],
                                    stg[:, 256 + (tt % 2) * 128:256 + (tt % 2) * 128 + 128],
                                    ident[:, :]).then_inc(mm.h, 1)
                    blk.tensor(tens)
                    t_mm = mm.inc()

                    def sc(v, tt=tt, t_mm=t_mm):
                        v.wait_ge(mm.h, t_mm)
                        v.tensor_copy(kfeat[:, tt * 128:(tt + 1) * 128],
                                      psB[:, tt % 2, 0:128])
                        v.tensor_copy(qfeat[:, tt * 128:(tt + 1) * 128],
                                      psB[:, 2 + tt % 2, 0:128]).then_inc(ev.h, 1)
                    blk.vector(sc)
                    ev.inc()

                    def vecv(v, tt=tt, t_ld=t_ld):
                        v.wait_ge(ld.h, t_ld)
                        v.tensor_copy(vbuf[:, tt, 0:64],
                                      stg2[:, (tt % 4) * 128:(tt % 4) * 128 + 64])
                        v.tensor_copy(vbuf[:, tt, 65:129],
                                      stg2[:, (tt % 4) * 128 + 64:(tt % 4) * 128 + 128])
                        v.fence(vx.h)
                    blk.vector(vecv)
                    vx.inc()
                barrier()

                prev_last_exp = 0
                prev_sc_o = 0
                for hh in range(2):
                    hloc = pair * 2 + hh
                    for qb in range(8):
                        nk = 4 * qb + 4
                        exp_tgt = {}
                        pv_tgt = {}

                        def emit_scores(j, hh=hh, qb=qb):
                            def tens_j(t, j=j, hh=hh, qb=qb,
                                       w_ev=exp_tgt.get(j - 2),
                                       w_prev=prev_last_exp):
                                if w_ev is not None:
                                    t.wait_ge(ev.h, w_ev)
                                elif w_prev:
                                    t.wait_ge(ev.h, w_prev)
                                t.matmul(psB[:, j % 2, :],
                                         kfeat[hh * 64:(hh + 1) * 64,
                                               j * 128:(j + 1) * 128],
                                         qfeat[hh * 64:(hh + 1) * 64,
                                               qb * 512:(qb + 1) * 512],
                                         start=True, stop=True).then_inc(mm.h, 1)
                            blk.tensor(tens_j)
                            t_mm = mm.inc()

                            def sc_j(s, j=j, t_mm=t_mm, w_pv=pv_tgt.get(j - 4)):
                                s.wait_ge(mm.h, t_mm)
                                if w_pv is not None:
                                    s.wait_ge(mm.h, w_pv)
                                s.activation(ptile[:, j % 4, :], psB[:, j % 2, :],
                                             AF.Exp, bias=cbias[:, 0:1],
                                             scale=0.125).then_inc(ev.h, 1)
                            blk.scalar(sc_j)
                            exp_tgt[j] = ev.inc()

                            if j >= 4 * qb:
                                def vec_j(v, j=j, qb=qb, t_exp=exp_tgt[j]):
                                    v.wait_ge(ev.h, t_exp)
                                    off = 384 + qb * 512 - j * 128
                                    v.tensor_tensor(ptile[:, j % 4, :],
                                                    ptile[:, j % 4, :],
                                                    bmask[:, off:off + 512], ALU.mult)
                                    v.fence(vx.h)
                                blk.vector(vec_j)
                                return ("vx", vx.inc())
                            return ("ev", exp_tgt[j])

                        def emit_pv(j, rdy, hh=hh, qb=qb, nk=nk):
                            def tens_pv(t, j=j, hh=hh, rdy=rdy, nk=nk,
                                        w_evac=prev_sc_o):
                                t.wait_ge({"vx": vx, "ev": ev}[rdy[0]].h, rdy[1])
                                if j == 0 and w_evac:
                                    t.wait_ge(ev.h, w_evac)
                                t.matmul(psA[0:65, 0, :],
                                         vbuf[:, j, hh * 65:(hh + 1) * 65],
                                         ptile[:, j % 4, :],
                                         start=(j == 0),
                                         stop=(j == nk - 1)).then_inc(mm.h, 1)
                            blk.tensor(tens_pv)
                            pv_tgt[j] = mm.inc()

                        rdys = {}
                        if FLASH_PIPE:
                            for j in range(nk):
                                rdys[j] = emit_scores(j)
                                if j >= 1:
                                    emit_pv(j - 1, rdys[j - 1])
                            emit_pv(nk - 1, rdys[nk - 1])
                        else:
                            for j in range(nk):
                                rdys[j] = emit_scores(j)
                                emit_pv(j, rdys[j])
                        prev_last_exp = exp_tgt[nk - 1]

                        def vec_e(v, t_mm=pv_tgt[nk - 1], t_sp=sp.n):
                            v.wait_ge(mm.h, t_mm)
                            v.wait_ge(sp.h, t_sp)
                            v.tensor_copy(orow[0:65, 0:512],
                                          psA[0:65, 0, :]).then_inc(ev.h, 1)
                        blk.vector(vec_e)
                        t_pe = ev.inc()
                        prev_sc_o = t_pe

                        sub_tgts = []
                        for sub_ in range(4):
                            def tr_s(t, sub_=sub_, t_pe=t_pe,
                                     w_vx=(sub_tgts[sub_ - 2] if sub_ >= 2 else 0)):
                                if sub_ == 0:
                                    t.wait_ge(ev.h, t_pe)
                                if w_vx:
                                    t.wait_ge(vx.h, w_vx)
                                t.transpose(psA[:, 1 + sub_ % 2, 0:65],
                                            orow[0:65, sub_ * 128:(sub_ + 1) * 128],
                                            ident[0:65, 0:65]).then_inc(mm.h, 1)
                            blk.tensor(tr_s)
                            t_tr = mm.inc()

                            def vec_s(v, sub_=sub_, t_tr=t_tr):
                                v.wait_ge(mm.h, t_tr)
                                v.reciprocal(scal[:, 3 + sub_:4 + sub_],
                                             psA[:, 1 + sub_ % 2, 64:65]).then_inc(sw.h, 1)
                                v.wait_ge(sw.h, sw.inc())
                                v.tensor_scalar(orow[:, 512 + sub_ * 64:512 + (sub_ + 1) * 64],
                                                psA[:, 1 + sub_ % 2, 0:64],
                                                scal[:, 3 + sub_:4 + sub_], None,
                                                ALU.mult)
                                v.fence(vx.h)
                            blk.vector(vec_s)
                            sub_tgts.append(vx.inc())

                        def gp_o(g, hloc=hloc, qb=qb, t_eo=sub_tgts[-1]):
                            g.wait_ge(vx.h, t_eo)
                            g.dma_start(
                                out=dr["amine"][qb * 4:(qb + 1) * 4, :,
                                                hloc * 64:(hloc + 1) * 64]
                                .rearrange("s p d -> p s d"),
                                in_=orow[:, 512:768].rearrange("p (s d) -> p s d", d=64),
                            ).then_inc(sp.h, 16)
                        blk.gpsimd(gp_o)
                        sp.inc(16)
                barrier()

            def ldwo(sy, t_sp=sp.n):
                sy.wait_ge(sp.h, t_sp)
                sy.dma_start(out=wta[:, :].rearrange("p (c n) -> p c n", c=1),
                             in_=dr["woT"][0:1, :, :].rearrange("c p n -> p c n")
                             ).then_inc(ld.h, 16)
                sy.dma_start(out=wtb[:, :].rearrange("p (c n) -> p c n", c=1),
                             in_=dr["woT"][1:2, :, :].rearrange("c p n -> p c n")
                             ).then_inc(ld.h, 16)
                sy.dma_start(out=orow[:, :].rearrange("p (c n) -> p c n", c=1),
                             in_=dr["woT"][2:3, :, :].rearrange("c p n -> p c n")
                             ).then_inc(ld.h, 16)
                sy.dma_start(out=bmask[:, 0:896],
                             in_=dr["woT"][3, :, 0:896]).then_inc(ld.h, 16)
                sy.dma_start(out=ptile[:, 0, 0:128],
                             in_=dr["woT"][3, :, 896:1024]).then_inc(ld.h, 16)
            blk.sync(ldwo)
            t_wo = ld.inc(80)
            wchunk = [wta[:, :], wtb[:, :], orow[:, :], None]

            for tt in range(32):
                buf = stg if tt % 2 == 0 else stg2

                def ldf(sy, tt=tt, buf=buf, t_mm=mm.n):
                    sy.wait_ge(mm.h, t_mm)
                    sy.dma_start(out=buf[:, 0:512],
                                 in_=dr["amine"][tt, :, :]).then_inc(ld.h, 16)
                blk.sync(ldf)
                t_ld = ld.inc(16)

                for c in range(4):
                    def tens(t, tt=tt, buf=buf, c=c, t_ld=t_ld, t_ev=ev.n):
                        if c == 0:
                            t.wait_ge(ld.h, t_ld)
                        t.wait_ge(ev.h, t_ev)
                        t.transpose(psB[:, c % 2, 0:128],
                                    buf[:, c * 128:(c + 1) * 128],
                                    ident[:, :]).then_inc(mm.h, 1)
                    blk.tensor(tens)
                    t_mm = mm.inc()

                    def sc(v, buf=buf, c=c, t_mm=t_mm):
                        v.wait_ge(mm.h, t_mm)
                        v.tensor_copy(buf[:, 512 + c * 128:512 + (c + 1) * 128],
                                      psB[:, c % 2, 0:128]).then_inc(ev.h, 1)
                    blk.vector(sc)
                    ev.inc()

                def tens2(t, tt=tt, buf=buf, t_wo=t_wo, t_evp=ev.n, t_vx=vx.n):
                    t.wait_ge(ev.h, t_evp)
                    t.wait_ge(ld.h, t_wo)
                    t.wait_ge(vx.h, t_vx)
                    for c in range(4):
                        w = wchunk[c]
                        for os_ in range(2):
                            if c < 3:
                                rhs = w[:, os_ * 512:(os_ + 1) * 512]
                                t.matmul(psA[:, os_, :],
                                         buf[:, 512 + c * 128:512 + (c + 1) * 128],
                                         rhs, start=(c == 0), stop=(c == 3))
                            else:
                                if os_ == 0:
                                    t.matmul(psA[:, 0, :],
                                             buf[:, 512 + c * 128:512 + (c + 1) * 128],
                                             bmask[:, 0:512], start=False, stop=True)
                                else:
                                    t.matmul(psA[:, 1, 0:384],
                                             buf[:, 512 + c * 128:512 + (c + 1) * 128],
                                             bmask[:, 512:896], start=False, stop=False)
                                    t.matmul(psA[:, 1, 384:512],
                                             buf[:, 512 + c * 128:512 + (c + 1) * 128],
                                             ptile[:, 0, 0:128], start=False, stop=True)
                    t.fence(mm.h)
                blk.tensor(tens2)
                t_mm2 = mm.inc()

                def vec2(v, tt=tt, buf=buf, t_mm2=t_mm2, t_sp=sp.n):
                    v.wait_ge(mm.h, t_mm2)
                    v.wait_ge(sp.h, t_sp)
                    v.tensor_copy(buf[:, 1024:1536], psA[:, 0, :])
                    v.tensor_copy(buf[:, 1536:2048], psA[:, 1, :])
                    v.fence(vx.h)
                blk.vector(vec2)
                t_vy = vx.inc()

                def gp_y(g, tt=tt, buf=buf, t_vy=t_vy):
                    g.wait_ge(vx.h, t_vy)
                    g.dma_start(out=dr["rs_in"][tt * 128:(tt + 1) * 128, :],
                                in_=buf[:, 1024:2048]).then_inc(sp.h, 16)
                blk.gpsimd(gp_y)
                sp.inc(16)

            def gpr(g, t_sp=sp.n):
                g.wait_ge(sp.h, t_sp)
                g.collective_compute(
                    "ReduceScatter", ALU.add,
                    replica_groups=[[0, 1], [2, 3], [4, 5], [6, 7]],
                    ins=[dr["rs_in"].rearrange("t d -> (t d)").opt()],
                    outs=[dr["rs_out"].rearrange("t d -> (t d)").opt()],
                ).then_inc(cc.h, 1)
                g.wait_ge(cc.h, cc.inc())
                g.sem_inc(gx.h, 1)
            blk.gpsimd(gpr)
            gx.inc()
            barrier()

            for tt in range(16):
                buf = stg if tt % 2 == 0 else stg2

                def ldf2(sy, tt=tt, buf=buf, t_mm=mm.n):
                    sy.wait_ge(mm.h, t_mm)
                    sy.dma_start(out=buf[:, 0:1024],
                                 in_=dr["rs_out"][tt * 128:(tt + 1) * 128, :]
                                 ).then_inc(ld.h, 16)
                blk.sync(ldf2)
                t_ld = ld.inc(16)

                for c in range(NC_D):
                    def tens(t, buf=buf, c=c, t_ld=t_ld, t_vx=vx.n):
                        if c == 0:
                            t.wait_ge(ld.h, t_ld)
                        t.wait_ge(vx.h, t_vx)
                        t.transpose(psB[:, c % 4, 0:128],
                                    buf[:, c * 128:(c + 1) * 128],
                                    ident[:, :]).then_inc(mm.h, 1)
                    blk.tensor(tens)
                    t_mm = mm.inc()

                    def vec(v, tt=tt, c=c, t_mm=t_mm):
                        v.wait_ge(mm.h, t_mm)
                        v.scalar_tensor_tensor(xres[:, c, tt * 128:(tt + 1) * 128],
                                               psB[:, c % 4, 0:128], 1.0,
                                               xres[:, c, tt * 128:(tt + 1) * 128],
                                               ALU.mult, ALU.add)
                        v.fence(vx.h)
                    blk.vector(vec)
                    vx.inc()
            barrier()

        def store_out():
            for tt in range(16):
                for c in range(NC_D):
                    def tens(t, tt=tt, c=c, t_ev=ev.n):
                        t.wait_ge(ev.h, t_ev)
                        t.transpose(psB[:, c % 4, 0:128],
                                    xres[:, c, tt * 128:(tt + 1) * 128],
                                    ident[:, :]).then_inc(mm.h, 1)
                    blk.tensor(tens)
                    t_mm = mm.inc()

                    def sc(v, tt=tt, c=c, t_mm=t_mm, t_sp=sp.n):
                        v.wait_ge(mm.h, t_mm)
                        v.wait_ge(sp.h, t_sp)
                        v.tensor_copy(stg[:, c * 128:(c + 1) * 128],
                                      psB[:, c % 4, 0:128]).then_inc(ev.h, 1)
                    blk.vector(sc)
                    ev.inc()

                def gp(g, tt=tt, t_ev=ev.n):
                    g.wait_ge(ev.h, t_ev)
                    g.dma_start(out=out[tt * 128:(tt + 1) * 128, :],
                                in_=stg[:, 0:1024]).then_inc(sp.h, 16)
                blk.gpsimd(gp)
                sp.inc(16)

        p0_consts()
        p0_input()
        if stage >= 1:
            p0_weights()
        barrier()
        if "wuT0" in tap:
            def ldt(sy, t_mm=mm.n):
                sy.dma_start(out=stg[:, 0:3072],
                             in_=dr["wuT"][0, :, :]).then_inc(ld.h, 16)
            blk.sync(ldt)
            mktap("wuT0", stg[:, 0:3072], [128, 3072], wait=(ld, ld.inc(16)))
        if stage >= 2:
            ffn_call("f1")
        if stage >= 3:
            attention()
        if stage >= 4:
            ffn_call("f2")
        store_out()

        def fin(g, t_sp=sp.n):
            g.wait_ge(sp.h, t_sp)
        blk.gpsimd(fin)

    return nc, tap_outs


def kernel(**inputs):
    x = np.ascontiguousarray(np.asarray(inputs["L0_attn_x"], np.float32))
    wq = np.asarray(inputs["L0_attn_wq"], np.float32)
    wk = np.asarray(inputs["L0_attn_wk"], np.float32)
    wv = np.asarray(inputs["L0_attn_wv"], np.float32)
    wo = np.ascontiguousarray(np.asarray(inputs["L0_attn_wo"], np.float32))
    wg = np.ascontiguousarray(np.asarray(inputs["L0_ffn_w_gate"], np.float32))
    wu = np.ascontiguousarray(np.asarray(inputs["L0_ffn_w_up"], np.float32))
    wd = np.ascontiguousarray(np.asarray(inputs["L0_ffn_w_down"], np.float32))

    nc, _ = build_program(stage=STAGE)
    in_maps = []
    for core in range(8):
        b, h = core // 2, core % 2
        in_maps.append({
            "x": np.ascontiguousarray(x[b, h * T:(h + 1) * T, :]),
            "wq": np.ascontiguousarray(wq[h * 512:(h + 1) * 512, :]),
            "wk": np.ascontiguousarray(wk[h * 512:(h + 1) * 512, :]),
            "wv": np.ascontiguousarray(wv[h * 512:(h + 1) * 512, :]),
            "wo": np.ascontiguousarray(wo[:, h * 512:(h + 1) * 512]),
            "wg": wg, "wu": wu, "wd": wd,
        })
    res = run_bass_kernel_spmd(nc, in_maps, core_ids=list(range(8)))
    outp = np.empty((4, S, D), np.float32)
    for core in range(8):
        b, h = core // 2, core % 2
        outp[b, h * T:(h + 1) * T, :] = res.results[core]["out"]
    return outp


# version 08
# speedup vs baseline: 1.0726x; 1.0726x over previous
import numpy as np
from contextlib import ExitStack

import concourse.bass as bass
import concourse.mybir as mybir
from concourse.bass_utils import run_bass_kernel_spmd

F32 = mybir.dt.float32
AF = mybir.ActivationFunctionType
ALU = mybir.AluOpType
AX = mybir.AxisListType.X
AXY = mybir.AxisListType.XY

MAGIC = 12582912.0
QMAX = 32767.0
C_EXP = 14.0
EPS_LN = 1e-5
LN10K = 9.210340371976184
TWO_PI = 6.283185307179586
CW1 = 6.28125
CW2 = float(np.float32(TWO_PI - 6.28125))
INV_2PI = 0.15915494309189535
HALF_PI = float(np.float32(np.pi / 2))
PI_F = float(np.float32(np.pi))
R7A = float(np.float32(1.0 / 7.0))
R7B = float(np.float32(1.0 / 7.0 - float(np.float32(1.0 / 7.0))))
RINV = float(np.float32(np.exp(-LN10K / 32.0)))

T = 2048
D = 1024
NC_D = 8
I3 = 3072
NI = 24
S = 4096
STAGE = 4
FFN_SUB = 7
FLASH_PIPE = True


class Sem:
    def __init__(self, ctx, nc, name):
        self.h = ctx.enter_context(nc.semaphore(name))
        self.n = 0

    def inc(self, k=1):
        self.n += k
        return self.n


_SEQ_ONLY = {"wait_ge", "wait_op", "sem_inc", "nop", "maybe_wait_ge"}


class EW:

    def __init__(self, eng):
        self._e = eng
        self._last = None

    def __getattr__(self, k):
        a = getattr(self._e, k)
        if callable(a) and k not in _SEQ_ONLY and not k.startswith("_"):
            def w(*args, **kw):
                r = a(*args, **kw)
                if k not in _SEQ_ONLY:
                    self._last = r
                return r
            return w
        return a

    def wait_ge(self, sem, val):
        return self._e.wait_ge(sem, val)

    def sem_inc(self, sem, val):
        return self._e.sem_inc(sem, val)

    def fence(self, sem, n=1):
        assert self._last is not None, "fence with no prior datapath op"
        self._last.then_inc(sem, n)


def build_program(stage=STAGE, taps=()):
    nc = bass.Bass("TRN2", target_bir_lowering=False,
                   detect_race_conditions=False)
    tap = set(taps)

    x_in = nc.dram_tensor("x", [T, D], F32, kind="ExternalInput").ap()
    wq_in = nc.dram_tensor("wq", [512, D], F32, kind="ExternalInput").ap()
    wk_in = nc.dram_tensor("wk", [512, D], F32, kind="ExternalInput").ap()
    wv_in = nc.dram_tensor("wv", [512, D], F32, kind="ExternalInput").ap()
    wo_in = nc.dram_tensor("wo", [D, 512], F32, kind="ExternalInput").ap()
    wg_in = nc.dram_tensor("wg", [I3, D], F32, kind="ExternalInput").ap()
    wu_in = nc.dram_tensor("wu", [I3, D], F32, kind="ExternalInput").ap()
    wd_in = nc.dram_tensor("wd", [D, I3], F32, kind="ExternalInput").ap()
    out = nc.dram_tensor("out", [T, D], F32, kind="ExternalOutput").ap()

    tap_outs = {}

    dr = {}
    dr["wqT"] = nc.dram_tensor("wqT", [NC_D, 128, 512], F32).ap()
    dr["wkT"] = nc.dram_tensor("wkT", [NC_D, 128, 512], F32).ap()
    dr["wvT"] = nc.dram_tensor("wvT", [NC_D, 128, 512], F32).ap()
    dr["woT"] = nc.dram_tensor("woT", [4, 128, D], F32).ap()
    dr["wgT"] = nc.dram_tensor("wgT", [NC_D, 128, I3], F32).ap()
    dr["wuT"] = nc.dram_tensor("wuT", [NC_D, 128, I3], F32).ap()
    dr["wdT"] = nc.dram_tensor("wdT", [NI, 128, D], F32).ap()
    dr["up"] = nc.dram_tensor("up_sp", [NI, 128, T], F32).ap()
    dr["gate"] = nc.dram_tensor("gate_sp", [NI, 128, T], F32).ap()
    dr["qu"] = nc.dram_tensor("qu_sp", [NI, 128, T], F32).ap()
    dr["qg"] = nc.dram_tensor("qg_sp", [NI, 128, T], F32).ap()
    dr["qt"] = nc.dram_tensor("qt_sp", [NI, 128, T], F32).ap()
    dr["ag_in"] = nc.dram_tensor("ag_in", [NC_D, 128, T], F32).ap()
    dr["ag_out"] = nc.dram_tensor("ag_out", [2, NC_D, 128, T], F32).ap()
    dr["qsp"] = nc.dram_tensor("q_sp", [32, 128, 512], F32).ap()
    dr["ksp"] = nc.dram_tensor("k_sp", [32, 128, 512], F32).ap()
    dr["vsp"] = nc.dram_tensor("v_sp", [32, 128, 512], F32).ap()
    dr["amine"] = nc.dram_tensor("amine", [32, 128, 512], F32).ap()
    dr["rs_in"] = nc.dram_tensor("rs_in", [S, D], F32).ap()
    dr["rs_out"] = nc.dram_tensor("rs_out", [T, D], F32).ap()
    dr["ar_in"] = nc.dram_tensor("ar_in", [256], F32).ap()
    dr["ar_out"] = nc.dram_tensor("ar_out", [256], F32).ap()
    dr["br"] = nc.dram_tensor("br", [16], F32).ap()
    dr["row"] = nc.dram_tensor("rowb", [4096], F32).ap()

    with ExitStack() as ctx:
        e = ctx.enter_context

        xres = e(nc.sbuf_tensor("xres", [128, NC_D, T], F32))
        xlnT = e(nc.sbuf_tensor("xln", [128, NC_D * T], F32))
        stg = e(nc.sbuf_tensor("stg", [128, 4096], F32))
        stg2 = e(nc.sbuf_tensor("stg2", [128, 4096], F32))
        pool = e(nc.sbuf_tensor("pool", [128, 4096], F32))
        wta = e(nc.sbuf_tensor("wta", [128, 1024], F32))
        wtb = e(nc.sbuf_tensor("wtb", [128, 1024], F32))
        ident = e(nc.sbuf_tensor("ident", [128, 128], F32))
        cbias = e(nc.sbuf_tensor("cbias", [128, 8], F32))
        scal = e(nc.sbuf_tensor("scal", [128, 16], F32))
        sc1 = e(nc.sbuf_tensor("sc1", [1, 384], F32))
        mcol = e(nc.sbuf_tensor("mcol", [128, 224], F32))
        bscale = e(nc.sbuf_tensor("bscale", [128, 96], F32))

        xln = xlnT[:, :].rearrange("p (c t) -> p c t", c=NC_D)
        kfeat = xlnT[:, 0:4096]
        qfeat = xlnT[:, 4096:8192]
        vbufF = xlnT[:, 8192:12352]
        vbuf = vbufF.rearrange("p (a b) -> p a b", b=130)
        bmask = xlnT[:, 12352:13248]
        orow = xlnT[:, 13248:14272]
        mubc = pool[:, 0:2048]
        rsbc = pool[:, 2048:4096]
        ropeC = pool[:, 0:1024].rearrange("p (a b) -> p a b", b=32)
        ropeS = pool[:, 1024:2048].rearrange("p (a b) -> p a b", b=32)
        ptile = pool[:, 2048:4096].rearrange("p (a b) -> p a b", b=512)

        psA = e(nc.psum_tensor("psA", [128, 4, 512], F32))
        psB = e(nc.psum_tensor("psB", [128, 4, 512], F32))

        ld = Sem(ctx, nc, "ld")
        sp = Sem(ctx, nc, "sp")
        mm = Sem(ctx, nc, "mm")
        ev = Sem(ctx, nc, "ev")
        vx = Sem(ctx, nc, "vx")
        gx = Sem(ctx, nc, "gx")
        cc = Sem(ctx, nc, "cc")
        bs = Sem(ctx, nc, "bs")
        sw = Sem(ctx, nc, "sw")
        blk0 = e(nc.Block())

        def vrecip(v, out_ap, in_ap):
            v.reciprocal(out_ap, in_ap).then_inc(sw.h, 1)
            v.wait_ge(sw.h, sw.inc())

        def ss(v):
            v._last.then_inc(sw.h, 1)
            v.wait_ge(sw.h, sw.inc())

        class BlkW:
            def __getattr__(self, name):
                def reg(f):
                    getattr(blk0, name)(lambda eng: f(EW(eng)))
                return reg
        blk = BlkW()

        def barrier():
            tgt = bs.n + 1
            bs.n = tgt

            def gpf(en, targets=(ld.n, sp.n, cc.n, mm.n, ev.n, vx.n, gx.n),
                    t=tgt):
                for sem, val in zip((ld, sp, cc, mm, ev, vx, gx), targets):
                    if val:
                        en.wait_ge(sem.h, val)
                en.sem_inc(bs.h, 1)
            blk.gpsimd(gpf)
            for name in ("sync", "vector", "scalar", "tensor", "gpsimd"):
                def f(en, t=tgt):
                    en.wait_ge(bs.h, t)
                getattr(blk, name)(f)

        def mktap(name, src_ap, shape, wait=None):
            if name not in tap:
                return
            tp = nc.dram_tensor("tap_" + name, shape, F32, kind="ExternalOutput").ap()
            tap_outs[name] = tp

            def gp(g):
                if wait is not None:
                    g.wait_ge(wait[0].h, wait[1])
                g.dma_start(out=tp, in_=src_ap).then_inc(sp.h, 16)
            blk.gpsimd(gp)
            sp.inc(16)
            barrier()

        def p0_consts():
            def vec(v):
                v.memset(cbias[:, :], 0.0)
                ss(v)
                v.memset(cbias[:, 0:1], -C_EXP)
                v.memset(cbias[:, 1:2], EPS_LN)
                v.memset(scal[:, 15:16], 1.0)
                v.memset(stg[:, 0:128], 1.0)
                v.fence(vx.h)
            blk.vector(vec)
            vx.inc()

            def gp(g, t=vx.n):
                g.wait_ge(vx.h, t)
                g.affine_select(ident[:, :], stg[:, 0:128], [[1, 128]],
                                ALU.is_equal, 0.0, base=0, channel_multiplier=-1)
                g.fence(gx.h)
            blk.gpsimd(gp)
            gx.inc()

        def p0_input():
            for tt in range(16):
                buf = stg if tt % 2 == 0 else stg2

                def ldf(sy, tt=tt, buf=buf, t_mm=mm.n):
                    sy.wait_ge(mm.h, t_mm)
                    sy.dma_start(out=buf[:, 0:1024],
                                 in_=x_in[tt * 128:(tt + 1) * 128, :]).then_inc(ld.h, 16)
                blk.sync(ldf)
                t_ld = ld.inc(16)

                for c in range(NC_D):
                    def tens(t, tt=tt, buf=buf, c=c, t_ld=t_ld, t_ev=ev.n):
                        if c == 0:
                            t.wait_ge(ld.h, t_ld)
                        t.wait_ge(ev.h, t_ev)
                        t.transpose(psB[:, c % 4, 0:128],
                                    buf[:, c * 128:(c + 1) * 128],
                                    ident[:, :]).then_inc(mm.h, 1)
                    blk.tensor(tens)
                    t_mm = mm.inc()

                    def sc(v, tt=tt, c=c, t_mm=t_mm):
                        v.wait_ge(mm.h, t_mm)
                        v.tensor_copy(xres[:, c, tt * 128:(tt + 1) * 128],
                                      psB[:, c % 4, 0:128]).then_inc(ev.h, 1)
                    blk.vector(sc)
                    ev.inc()

        def prep_weight(w_ap, rows, dst, lpbq):
            cols = w_ap.shape[1]
            ncol = cols // 128
            wrow = pool[:, 0:cols]
            for r in range(rows // 128):
                def ldf(sy, r=r, t_mm=mm.n):
                    sy.wait_ge(mm.h, t_mm)
                    sy.dma_start(out=wrow[:, :],
                                 in_=w_ap[r * 128:(r + 1) * 128, :]).then_inc(ld.h, 16)
                blk.sync(ldf)
                t_ld = ld.inc(16)

                if lpbq:
                    nb = cols // 32

                    def vec(v, t_ld=t_ld, nb=nb):
                        v.wait_ge(ld.h, t_ld)
                        wv = wrow.rearrange("p (b k) -> p b k", k=32)
                        v.tensor_reduce(bscale[:, 0:nb], wv, AX, ALU.max,
                                        apply_absolute_value=True)
                        ss(v)
                        v.tensor_scalar(mcol[:, 0:nb], bscale[:, 0:nb], R7A,
                                        None, ALU.mult)
                        ss(v)
                        v.scalar_tensor_tensor(mcol[:, 0:nb], bscale[:, 0:nb],
                                               R7B, mcol[:, 0:nb],
                                               ALU.mult, ALU.add)
                        ss(v)
                        vrecip(v, bscale[:, 0:nb], mcol[:, 0:nb])
                        v.tensor_tensor(wv, wv,
                                        bscale[:, 0:nb].unsqueeze(2)
                                        .to_broadcast([128, nb, 32]), ALU.mult)
                        ss(v)
                        v.tensor_scalar(wv, wv, MAGIC, MAGIC, ALU.add, ALU.subtract)
                        ss(v)
                        v.tensor_tensor(wv, wv,
                                        mcol[:, 0:nb].unsqueeze(2)
                                        .to_broadcast([128, nb, 32]), ALU.mult)
                        v.fence(vx.h)
                    blk.vector(vec)
                    rdy = ("vx", vx.inc())
                else:
                    rdy = ("ld", t_ld)

                for c in range(ncol):
                    def tens(t, c=c, rdy=rdy, t_ev=ev.n):
                        if c == 0:
                            t.wait_ge({"vx": vx, "ld": ld}[rdy[0]].h, rdy[1])
                        t.wait_ge(ev.h, t_ev)
                        t.transpose(psB[:, c % 4, 0:128],
                                    wrow[:, c * 128:(c + 1) * 128],
                                    ident[:, :]).then_inc(mm.h, 1)
                    blk.tensor(tens)
                    t_mm = mm.inc()

                    def sc(v, c=c, t_mm=t_mm, t_sp=sp.n):
                        v.wait_ge(mm.h, t_mm)
                        v.wait_ge(sp.h, t_sp)
                        v.tensor_copy(stg2[:, (c % 8) * 128:(c % 8) * 128 + 128],
                                      psB[:, c % 4, 0:128]).then_inc(ev.h, 1)
                    blk.vector(sc)
                    t_ev = ev.inc()

                    def gp(g, c=c, r=r, t_ev=t_ev):
                        g.wait_ge(ev.h, t_ev)
                        g.dma_start(out=dst[c, :, r * 128:(r + 1) * 128],
                                    in_=stg2[:, (c % 8) * 128:(c % 8) * 128 + 128]
                                    ).then_inc(sp.h, 16)
                    blk.gpsimd(gp)
                    sp.inc(16)

        def p0_weights():
            prep_weight(wq_in, 512, dr["wqT"], False)
            prep_weight(wk_in, 512, dr["wkT"], False)
            prep_weight(wv_in, 512, dr["wvT"], False)
            prep_weight(wo_in, D, dr["woT"], False)
            prep_weight(wu_in, I3, dr["wuT"], True)
            prep_weight(wg_in, I3, dr["wgT"], True)
            prep_weight(wd_in, D, dr["wdT"], True)

        def layer_norm(tag):
            for g4 in range(4):
                sl = slice(g4 * 512, (g4 + 1) * 512)

                def sc(v, sl=sl, t_vx=vx.n):
                    v.wait_ge(vx.h, t_vx)
                    v.tensor_tensor(stg[:, :].rearrange("p (c t) -> p c t", c=NC_D),
                                    xres[:, :, sl], xres[:, :, sl],
                                    ALU.mult).then_inc(ev.h, 1)
                blk.vector(sc)
                t_sq = ev.inc()

                def tens(t, sl=sl, t_sq=t_sq, t_vx=vx.n):
                    t.wait_ge(ev.h, t_sq)
                    t.wait_ge(vx.h, t_vx)
                    for c in range(NC_D):
                        t.matmul(psB[0:1, 0, 0:512], scal[:, 15:16],
                                 xres[:, c, sl], start=(c == 0), stop=(c == 7))
                    for c in range(NC_D):
                        t.matmul(psB[0:1, 1, 0:512], scal[:, 15:16],
                                 stg[:, c * 512:(c + 1) * 512],
                                 start=(c == 0), stop=(c == 7))
                    t.fence(mm.h)
                blk.tensor(tens)
                t_mm = mm.inc()

                def vec(v, sl=sl, t_mm=t_mm):
                    v.wait_ge(mm.h, t_mm)
                    v.tensor_scalar(mubc[0:1, sl], psB[0:1, 0, 0:512],
                                    1.0 / 1024.0, None, ALU.mult)
                    v.tensor_scalar(rsbc[0:1, sl],
                                    psB[0:1, 1, 0:512], 1.0 / 1024.0, None, ALU.mult)
                    ss(v)
                    v.tensor_tensor(stg2[0:1, 0:512], mubc[0:1, sl], mubc[0:1, sl],
                                    ALU.mult)
                    ss(v)
                    v.tensor_tensor(rsbc[0:1, sl], rsbc[0:1, sl], stg2[0:1, 0:512],
                                    ALU.subtract)
                    v.fence(vx.h)
                blk.vector(vec)
                t_vx = vx.inc()

                def sc2(s, sl=sl, t_vx=t_vx):
                    s.wait_ge(vx.h, t_vx)
                    s.activation(rsbc[0:1, sl], rsbc[0:1, sl], AF.Ln,
                                 bias=cbias[0:1, 1:2], scale=1.0).then_inc(sw.h, 1)
                    s.wait_ge(sw.h, sw.inc())
                    s.activation(rsbc[0:1, sl], rsbc[0:1, sl], AF.Exp,
                                 bias=cbias[0:1, 2:3], scale=-0.5).then_inc(ev.h, 1)
                blk.scalar(sc2)
                t_ev = ev.inc()

                def vec2(v, sl=sl, t_ev=t_ev):
                    v.wait_ge(ev.h, t_ev)
                    v.fence(vx.h) if False else v.tensor_copy(
                        sc1[0:1, 380:381], rsbc[0:1, sl.start:sl.start + 1])
                    v.fence(vx.h)
                blk.vector(vec2)
                vx.inc()

            def gp(g, t_vx=vx.n):
                g.wait_ge(vx.h, t_vx)
                g.dma_start(out=dr["row"][0:2048], in_=mubc[0:1, :]).then_inc(sp.h, 16)
                g.dma_start(out=dr["row"][2048:4096], in_=rsbc[0:1, :]).then_inc(sp.h, 16)
                g.wait_ge(sp.h, sp.inc(32))
                g.dma_start(out=mubc[:, :],
                            in_=dr["row"][0:2048].unsqueeze(0).to_broadcast([128, 2048])).then_inc(sp.h, 16)
                g.dma_start(out=rsbc[:, :],
                            in_=dr["row"][2048:4096].unsqueeze(0).to_broadcast([128, 2048])).then_inc(sp.h, 16)
                g.wait_ge(sp.h, sp.inc(32))
                g.sem_inc(gx.h, 1)
            blk.gpsimd(gp)
            gx.inc()

            def vec3(v, t_gx=gx.n):
                v.wait_ge(gx.h, t_gx)
                for c in range(NC_D):
                    v.tensor_tensor(xln[:, c, :], xres[:, c, :], mubc[:, :],
                                    ALU.subtract)
                    ss(v)
                    v.tensor_tensor(xln[:, c, :], xln[:, c, :], rsbc[:, :],
                                    ALU.mult)
                v.fence(vx.h)
            blk.vector(vec3)
            vx.inc()
            mktap("xln_" + tag, xlnT[:, :], [128, NC_D * T], wait=(vx, vx.n))

        def allreduce_max(n_vals, compute_scalars, nbc):
            def gp(g, t_vx=vx.n, n_vals=n_vals):
                g.wait_ge(vx.h, t_vx)
                for j in range(n_vals):
                    g.dma_start(out=dr["ar_in"][j * 128:(j + 1) * 128],
                                in_=mcol[:, 200 + j:201 + j]).then_inc(sp.h, 16)
                g.wait_ge(sp.h, sp.inc(16 * n_vals))
                g.collective_compute(
                    "AllReduce", ALU.max,
                    replica_groups=[[0, 1, 2, 3, 4, 5, 6, 7]],
                    ins=[dr["ar_in"][0:n_vals * 128].opt()],
                    outs=[dr["ar_out"][0:n_vals * 128].opt()],
                ).then_inc(cc.h, 1)
                g.wait_ge(cc.h, cc.inc())
                g.dma_start(out=sc1[0:1, 0:n_vals * 128],
                            in_=dr["ar_out"][0:n_vals * 128]).then_inc(sp.h, 16)
                g.wait_ge(sp.h, sp.inc(16))
                g.sem_inc(gx.h, 1)
            blk.gpsimd(gp)
            gx.inc()

            def vec(v, t_gx=gx.n, n_vals=n_vals):
                v.wait_ge(gx.h, t_gx)
                for j in range(n_vals):
                    r = v.tensor_reduce(sc1[0:1, 320 + j:321 + j],
                                        sc1[0:1, j * 128:(j + 1) * 128], AX, ALU.max)
                r.then_inc(sw.h, 1)
                v.wait_ge(sw.h, sw.inc())
                compute_scalars(v)
                v.fence(vx.h)
            blk.vector(vec)
            vx.inc()

            def gp2(g, t_vx=vx.n, nbc=nbc):
                g.wait_ge(vx.h, t_vx)
                g.dma_start(out=dr["br"][0:nbc], in_=sc1[0:1, 340:340 + nbc]).then_inc(sp.h, 16)
                g.wait_ge(sp.h, sp.inc(16))
                g.dma_start(out=scal[:, 0:nbc],
                            in_=dr["br"][0:nbc].unsqueeze(0).to_broadcast([128, nbc])).then_inc(sp.h, 16)
                g.wait_ge(sp.h, sp.inc(16))
                g.sem_inc(gx.h, 1)
            blk.gpsimd(gp2)
            gx.inc()
            barrier()

        def ffn_call(tag):
            sub = FFN_SUB
            layer_norm(tag)
            if sub < 2:
                return

            def vecm(v, t_vx=vx.n):
                v.wait_ge(vx.h, t_vx)
                v.tensor_reduce(mcol[:, 200:201],
                                xlnT[:, :], AX, ALU.max,
                                apply_absolute_value=True)
                v.fence(vx.h)
            blk.vector(vecm)
            vx.inc()

            def scalars1(v):
                vrecip(v, sc1[0:1, 340:341], sc1[0:1, 340:341] if False else sc1[0:1, 320:321])
                v.tensor_scalar(sc1[0:1, 340:341], sc1[0:1, 340:341], QMAX, None,
                                ALU.mult)
                ss(v)
                v.tensor_scalar(sc1[0:1, 300:301], sc1[0:1, 320:321], 1.0 / QMAX,
                                None, ALU.mult)
            allreduce_max(1, scalars1, 1)
            mktap("sc1_" + tag, sc1[0:1, :], [1, 384], wait=(gx, gx.n))
            mktap("scal_" + tag, scal[:, :], [128, 16], wait=(gx, gx.n))
            mktap("mcol_" + tag, mcol[:, :], [128, 224], wait=(gx, gx.n))
            if sub < 3:
                return

            def vecq(v):
                v.tensor_scalar(xln[:, :, :], xln[:, :, :], scal[:, 0:1], MAGIC,
                                ALU.mult, ALU.add)
                ss(v)
                v.tensor_scalar(xln[:, :, :], xln[:, :, :], MAGIC, None,
                                ALU.subtract)
                v.fence(vx.h)
            blk.vector(vecq)
            vx.inc()
            mktap("qx_" + tag, xlnT[:, :], [128, NC_D * T], wait=(vx, vx.n))
            barrier()
            if sub < 4:
                return

            bT = {"mm": {}, "ev": {}, "vx": {}, "sp": {}}
            step = [0]
            for which, wsrc, spdst, mbase in (("u", dr["wuT"], dr["up"], 8),
                                              ("g", dr["wgT"], dr["gate"], 40)):
                for i in range(NI):
                    k_ = step[0]
                    par = k_ % 2
                    wbuf = wta if par == 0 else wtb
                    ps = psA if par == 0 else psB
                    sbuf = stg if par == 0 else stg2

                    def ldf(sy, i=i, wbuf=wbuf, wsrc=wsrc,
                            t_mm=bT["mm"].get(k_ - 2, 0)):
                        if t_mm:
                            sy.wait_ge(mm.h, t_mm)
                        sy.dma_start(out=wbuf[:, :].rearrange("p (c n) -> p c n", c=NC_D),
                                     in_=wsrc[:, :, i * 128:(i + 1) * 128]
                                     .rearrange("c p n -> p c n")).then_inc(ld.h, 16)
                    blk.sync(ldf)
                    t_ld = ld.inc(16)

                    def tens(t, wbuf=wbuf, ps=ps, t_ld=t_ld,
                             t_ev=bT["ev"].get(k_ - 2, 0)):
                        t.wait_ge(ld.h, t_ld)
                        if t_ev:
                            t.wait_ge(ev.h, t_ev)
                        for c in range(NC_D):
                            for tb in range(4):
                                t.matmul(ps[:, tb, :], wbuf[:, c * 128:(c + 1) * 128],
                                         xln[:, c, tb * 512:(tb + 1) * 512],
                                         start=(c == 0), stop=(c == 7))
                        t.fence(mm.h)
                    blk.tensor(tens)
                    t_mm = mm.inc()
                    bT["mm"][k_] = t_mm

                    def sc(v, ps=ps, sbuf=sbuf, t_mm=t_mm,
                           t_sp=bT["sp"].get(k_ - 2, 0),
                           t_vx=bT["vx"].get(k_ - 2, 0)):
                        v.wait_ge(mm.h, t_mm)
                        if t_sp:
                            v.wait_ge(sp.h, t_sp)
                        if t_vx:
                            v.wait_ge(vx.h, t_vx)
                        v.tensor_copy(sbuf[:, 0:2048],
                                      ps[:, :, :].rearrange("p a b -> p (a b)")
                                      ).then_inc(ev.h, 1)
                    blk.vector(sc)
                    t_ev = ev.inc()
                    bT["ev"][k_] = t_ev

                    def vecr(v, i=i, sbuf=sbuf, mbase=mbase, t_ev=t_ev):
                        v.wait_ge(ev.h, t_ev)
                        v.tensor_reduce(mcol[:, mbase + i:mbase + i + 1],
                                        sbuf[:, 0:2048], AX, ALU.max,
                                        apply_absolute_value=True)
                        v.fence(vx.h)
                    blk.vector(vecr)
                    bT["vx"][k_] = vx.inc()

                    def gp(g, i=i, sbuf=sbuf, spdst=spdst, t_ev=t_ev):
                        g.wait_ge(ev.h, t_ev)
                        g.dma_start(out=spdst[i, :, :],
                                    in_=sbuf[:, 0:2048]).then_inc(sp.h, 16)
                    blk.gpsimd(gp)
                    bT["sp"][k_] = sp.inc(16)
                    step[0] += 1

                def vecf(v, mbase=mbase, which=which, t_vx=vx.n):
                    v.wait_ge(vx.h, t_vx)
                    col = 200 if which == "u" else 201
                    v.tensor_reduce(mcol[:, col:col + 1],
                                    mcol[:, mbase:mbase + NI], AX, ALU.max)
                    v.fence(vx.h)
                blk.vector(vecf)
                vx.inc()

            def scalars2(v):
                v.reciprocal(sc1[0:1, 340:341], sc1[0:1, 320:321])
                vrecip(v, sc1[0:1, 341:342], sc1[0:1, 321:322])
                v.tensor_scalar(sc1[0:1, 340:342], sc1[0:1, 340:342], QMAX, None,
                                ALU.mult)
                ss(v)
                v.tensor_scalar(sc1[0:1, 301:303], sc1[0:1, 320:322], 1.0 / QMAX,
                                None, ALU.mult)
                ss(v)
                v.tensor_tensor(sc1[0:1, 301:303], sc1[0:1, 301:303],
                                sc1[0:1, 300:301].to_broadcast([1, 2]), ALU.mult)
                ss(v)
                v.tensor_copy(sc1[0:1, 342:343], sc1[0:1, 302:303])
                ss(v)
                v.tensor_scalar(sc1[0:1, 344:345], sc1[0:1, 302:303], -1.0, None,
                                ALU.mult)
            allreduce_max(2, scalars2, 5)
            if ("upE_" + tag) in tap:
                def ldt(sy, t_mm=mm.n):
                    sy.dma_start(out=stg[:, 0:2048],
                                 in_=dr["up"][0, :, :]).then_inc(ld.h, 16)
                blk.sync(ldt)
                mktap("upE_" + tag, stg[:, 0:2048], [128, 2048],
                      wait=(ld, ld.inc(16)))
            if sub < 5:
                return

            sgb = pool[:, 0:2048]
            for i in range(NI):
                b0 = stg if i % 2 == 0 else stg2

                def ldf(sy, i=i, b0=b0, t_vx=vx.n, t_sp=sp.n):
                    sy.wait_ge(vx.h, t_vx)
                    sy.wait_ge(sp.h, t_sp)
                    sy.dma_start(out=b0[:, 0:2048], in_=dr["up"][i, :, :]).then_inc(ld.h, 16)
                    sy.dma_start(out=b0[:, 2048:4096], in_=dr["gate"][i, :, :]).then_inc(ld.h, 16)
                blk.sync(ldf)
                t_ld = ld.inc(32)

                def vec(v, b0=b0, t_ld=t_ld):
                    v.wait_ge(ld.h, t_ld)
                    v.tensor_scalar(b0[:, 0:2048], b0[:, 0:2048], scal[:, 0:1],
                                    MAGIC, ALU.mult, ALU.add)
                    ss(v)
                    v.tensor_scalar(b0[:, 0:2048], b0[:, 0:2048], MAGIC, None,
                                    ALU.subtract)
                    v.tensor_scalar(b0[:, 2048:4096], b0[:, 2048:4096], scal[:, 1:2],
                                    MAGIC, ALU.mult, ALU.add)
                    ss(v)
                    v.tensor_scalar(b0[:, 2048:4096], b0[:, 2048:4096], MAGIC, None,
                                    ALU.subtract)
                    v.fence(vx.h)
                blk.vector(vec)
                t_q = vx.inc()

                def gp(g, i=i, b0=b0, t_q=t_q):
                    g.wait_ge(vx.h, t_q)
                    g.dma_start(out=dr["qu"][i, :, :], in_=b0[:, 0:2048]).then_inc(sp.h, 16)
                    g.dma_start(out=dr["qg"][i, :, :], in_=b0[:, 2048:4096]).then_inc(sp.h, 16)
                blk.gpsimd(gp)
                sp.inc(32)

                def sc(s, b0=b0, t_q=t_q, t_vx2=vx.n):
                    s.wait_ge(vx.h, t_q)
                    s.activation(sgb, b0[:, 2048:4096], AF.Sigmoid,
                                 bias=cbias[:, 2:3], scale=scal[:, 2:3]).then_inc(ev.h, 1)
                blk.scalar(sc)
                t_sg = ev.inc()

                def vec2(v, i=i, b0=b0, t_sg=t_sg):
                    v.wait_ge(ev.h, t_sg)
                    v.tensor_scalar(sgb, sgb, 65536.0, MAGIC, ALU.mult, ALU.add)
                    ss(v)
                    v.tensor_scalar(sgb, sgb, MAGIC, None, ALU.subtract)
                    ss(v)
                    v.tensor_tensor(sgb, b0[:, 2048:4096], sgb, ALU.mult)
                    ss(v)
                    v.tensor_reduce(mcol[:, 72 + i:73 + i], sgb, AX, ALU.max,
                                    apply_absolute_value=True)
                    v.fence(vx.h)
                blk.vector(vec2)
                vx.inc()

            def vecf2(v, t_vx=vx.n):
                v.wait_ge(vx.h, t_vx)
                v.tensor_reduce(mcol[:, 200:201], mcol[:, 72:72 + NI], AX, ALU.max)
                v.fence(vx.h)
            blk.vector(vecf2)
            vx.inc()

            def scalars3(v):
                vrecip(v, sc1[0:1, 340:341], sc1[0:1, 320:321])
                v.tensor_scalar(sc1[0:1, 340:341], sc1[0:1, 340:341], QMAX, None,
                                ALU.mult)
                ss(v)
                v.tensor_scalar(sc1[0:1, 303:304], sc1[0:1, 320:321],
                                1.0 / 65536.0 / QMAX, None, ALU.mult)
                ss(v)
                v.tensor_tensor(sc1[0:1, 303:304], sc1[0:1, 303:304],
                                sc1[0:1, 302:303], ALU.mult)
                ss(v)
                v.tensor_copy(sc1[0:1, 341:342], sc1[0:1, 302:303])
                ss(v)
                v.tensor_scalar(sc1[0:1, 344:345], sc1[0:1, 302:303], -1.0, None,
                                ALU.mult)
            allreduce_max(1, scalars3, 5)
            if sub < 6:
                return

            for i in range(NI):
                b0 = stg if i % 2 == 0 else stg2

                def ldf(sy, i=i, b0=b0, t_vx=vx.n):
                    sy.wait_ge(vx.h, t_vx)
                    sy.dma_start(out=b0[:, 0:2048], in_=dr["qu"][i, :, :]).then_inc(ld.h, 16)
                    sy.dma_start(out=b0[:, 2048:4096], in_=dr["qg"][i, :, :]).then_inc(ld.h, 16)
                blk.sync(ldf)
                t_ld = ld.inc(32)

                def sc(s, b0=b0, t_ld=t_ld, t_sp=sp.n, t_vx=vx.n):
                    s.wait_ge(ld.h, t_ld)
                    s.wait_ge(sp.h, t_sp)
                    s.wait_ge(vx.h, t_vx)
                    s.activation(sgb, b0[:, 2048:4096], AF.Sigmoid,
                                 bias=cbias[:, 2:3], scale=scal[:, 1:2]).then_inc(ev.h, 1)
                blk.scalar(sc)
                t_sg = ev.inc()

                def vec(v, i=i, b0=b0, t_sg=t_sg):
                    v.wait_ge(ev.h, t_sg)
                    v.tensor_scalar(sgb, sgb, 65536.0, MAGIC, ALU.mult, ALU.add)
                    v.tensor_scalar(sgb, sgb, MAGIC, None, ALU.subtract)
                    v.tensor_tensor(sgb, b0[:, 2048:4096], sgb, ALU.mult)
                    v.tensor_scalar(sgb, sgb, scal[:, 0:1], MAGIC, ALU.mult, ALU.add)
                    v.tensor_scalar(sgb, sgb, MAGIC, None, ALU.subtract)
                    v.tensor_tensor(b0[:, 2048:4096], sgb, b0[:, 0:2048], ALU.mult)
                    v.tensor_reduce(mcol[:, 104 + i:105 + i], b0[:, 2048:4096],
                                    AX, ALU.max, apply_absolute_value=True)
                    v.fence(vx.h)
                blk.vector(vec)
                t_q = vx.inc()

                def gp(g, i=i, t_q=t_q):
                    g.wait_ge(vx.h, t_q)
                    g.dma_start(out=dr["qt"][i, :, :], in_=sgb).then_inc(sp.h, 16)
                blk.gpsimd(gp)
                sp.inc(16)

            def vecf3(v, t_vx=vx.n):
                v.wait_ge(vx.h, t_vx)
                v.tensor_reduce(mcol[:, 200:201], mcol[:, 104:104 + NI], AX, ALU.max)
                v.fence(vx.h)
            blk.vector(vecf3)
            vx.inc()

            def scalars4(v):
                vrecip(v, sc1[0:1, 340:341], sc1[0:1, 320:321])
                v.tensor_scalar(sc1[0:1, 340:341], sc1[0:1, 340:341], QMAX, None,
                                ALU.mult)
                ss(v)
                v.tensor_scalar(sc1[0:1, 304:305], sc1[0:1, 320:321], 1.0 / QMAX,
                                None, ALU.mult)
                ss(v)
                v.tensor_tensor(sc1[0:1, 304:305], sc1[0:1, 304:305],
                                sc1[0:1, 303:304], ALU.mult)
                ss(v)
                v.tensor_tensor(sc1[0:1, 304:305], sc1[0:1, 304:305],
                                sc1[0:1, 301:302], ALU.mult)
                ss(v)
                v.tensor_scalar(sc1[0:1, 341:342], sc1[0:1, 304:305], 0.5, None,
                                ALU.mult)
            allreduce_max(1, scalars4, 2)
            if sub < 7:
                return

            eT = {}
            for tb in range(4):
                for k in range(NI):
                    b0 = stg if k % 2 == 0 else stg2
                    ks_ = tb * NI + k

                    def ldf(sy, tb=tb, k=k, b0=b0, t_mm=eT.get(ks_ - 2, 0)):
                        if t_mm:
                            sy.wait_ge(mm.h, t_mm)
                        sy.dma_start(out=b0[:, 0:512],
                                     in_=dr["qt"][k, :, tb * 512:(tb + 1) * 512]).then_inc(ld.h, 16)
                        sy.dma_start(out=b0[:, 512:1024],
                                     in_=dr["qu"][k, :, tb * 512:(tb + 1) * 512]).then_inc(ld.h, 16)
                        sy.dma_start(out=b0[:, 1024:2048],
                                     in_=dr["wdT"][k, :, :]).then_inc(ld.h, 16)
                    blk.sync(ldf)
                    t_ld = ld.inc(48)

                    def vec(v, b0=b0, t_ld=t_ld):
                        v.wait_ge(ld.h, t_ld)
                        v.tensor_tensor(b0[:, 0:512], b0[:, 0:512], b0[:, 512:1024],
                                        ALU.mult)
                        v.tensor_scalar(b0[:, 0:512], b0[:, 0:512], scal[:, 0:1],
                                        MAGIC, ALU.mult, ALU.add)
                        v.tensor_scalar(b0[:, 0:512], b0[:, 0:512], MAGIC, None,
                                        ALU.subtract)
                        v.fence(vx.h)
                    blk.vector(vec)
                    t_oq = vx.inc()

                    def tens(t, tb=tb, k=k, b0=b0, t_oq=t_oq, t_vx=vx.n):
                        t.wait_ge(vx.h, t_oq)
                        if k == 0:
                            t.wait_ge(vx.h, t_vx)
                        for dti in range(8):
                            ps = psA if dti < 4 else psB
                            t.matmul(ps[:, dti % 4, :],
                                     b0[:, 1024 + dti * 128:1024 + (dti + 1) * 128],
                                     b0[:, 0:512], start=(k == 0), stop=(k == NI - 1))
                        t.fence(mm.h)
                    blk.tensor(tens)
                    eT[ks_] = mm.inc()

                def vecu(v, tb=tb, t_mm=mm.n):
                    v.wait_ge(mm.h, t_mm)
                    for dti in range(8):
                        ps = psA if dti < 4 else psB
                        v.scalar_tensor_tensor(
                            xres[:, dti, tb * 512:(tb + 1) * 512],
                            ps[:, dti % 4, :], scal[:, 1:2],
                            xres[:, dti, tb * 512:(tb + 1) * 512],
                            ALU.mult, ALU.add)
                    v.fence(vx.h)
                blk.vector(vecu)
                vx.inc()
            barrier()

            for nm in ("up", "gate", "qu", "qg", "qt", "up2"):
                tnm = nm + "0_" + tag
                if tnm in tap:
                    src = "up" if nm == "up2" else nm
                    def ldt(sy, src=src, t_mm=mm.n):
                        sy.dma_start(out=stg[:, 0:2048],
                                     in_=dr[src][0, :, :]).then_inc(ld.h, 16)
                    blk.sync(ldt)
                    mktap(tnm, stg[:, 0:2048], [128, 2048], wait=(ld, ld.inc(16)))

        def attention():
            layer_norm("a")
            def gp(g, t_vx=vx.n):
                g.wait_ge(vx.h, t_vx)
                g.dma_start(out=dr["ag_in"].rearrange("c p t -> p c t"),
                            in_=xln[:, :, :]).then_inc(sp.h, 16)
                g.wait_ge(sp.h, sp.inc(16))
                g.collective_compute(
                    "AllGather", ALU.bypass,
                    replica_groups=[[0, 1], [2, 3], [4, 5], [6, 7]],
                    ins=[dr["ag_in"].rearrange("c p t -> (c p t)").opt()],
                    outs=[dr["ag_out"].rearrange("h c p t -> (h c p t)").opt()],
                ).then_inc(cc.h, 1)
                g.wait_ge(cc.h, cc.inc())
                g.sem_inc(gx.h, 1)
            blk.gpsimd(gp)
            gx.inc()
            barrier()

            def gpi(g):
                g.iota(ropeC[:, :, 0:1], [[128, 32], [0, 1]], base=0,
                       channel_multiplier=1, allow_small_or_imprecise_dtypes=True)
                g.fence(gx.h)
            blk.gpsimd(gpi)
            gx.inc()

            def veci(v, t_gx=gx.n):
                v.wait_ge(gx.h, t_gx)
                v.memset(sc1[0:1, 256:257], 1.0)
                ss(v)
                for i in range(1, 32):
                    v.tensor_scalar(sc1[0:1, 256 + i:257 + i], sc1[0:1, 255 + i:256 + i],
                                    RINV, None, ALU.mult)
                    ss(v)
                v.tensor_copy(sc1[0:1, 379:380], sc1[0:1, 256:257])
                v.fence(vx.h)
            blk.vector(veci)
            vx.inc()

            def gpb(g, t_vx=vx.n):
                g.wait_ge(vx.h, t_vx)
                g.dma_start(out=dr["br"][0:16] if False else dr["ar_in"][0:32],
                            in_=sc1[0:1, 256:288]).then_inc(sp.h, 16)
                g.wait_ge(sp.h, sp.inc(16))
                g.dma_start(out=stg[:, 0:32],
                            in_=dr["ar_in"][0:32].unsqueeze(0).to_broadcast([128, 32])).then_inc(sp.h, 16)
                g.wait_ge(sp.h, sp.inc(16))
                g.sem_inc(gx.h, 1)
            blk.gpsimd(gpb)
            gx.inc()

            def vecr(v, t_gx=gx.n):
                v.wait_ge(gx.h, t_gx)
                for j in range(32):
                    v.tensor_tensor(ropeS[:, j, :],
                                    ropeC[:, j, 0:1].to_broadcast([128, 32]),
                                    stg[:, 0:32], ALU.mult)
                ss(v)
                v.tensor_scalar(ropeC[:, :, :], ropeS[:, :, :], INV_2PI, MAGIC,
                                ALU.mult, ALU.add)
                ss(v)
                v.tensor_scalar(ropeC[:, :, :], ropeC[:, :, :], MAGIC, None,
                                ALU.subtract)
                ss(v)
                v.scalar_tensor_tensor(ropeS[:, :, :], ropeC[:, :, :], -CW1,
                                       ropeS[:, :, :], ALU.mult, ALU.add)
                ss(v)
                v.scalar_tensor_tensor(ropeS[:, :, :], ropeC[:, :, :], -CW2,
                                       ropeS[:, :, :], ALU.mult, ALU.add)
                ss(v)
                v.tensor_scalar(ropeS[:, :, :], ropeS[:, :, :], PI_F, -PI_F,
                                ALU.min, ALU.max)
                v.fence(vx.h)
            blk.vector(vecr)
            vx.inc()

            def scs(s, t_vx=vx.n):
                s.wait_ge(vx.h, t_vx)
                s.activation(stg2[:, 0:1024].rearrange("p (a b) -> p a b", b=32),
                             ropeS[:, :, :], AF.Sin, bias=cbias[:, 2:3],
                             scale=1.0).then_inc(ev.h, 1)
            blk.scalar(scs)
            ev.inc()

            def vecc(v, t_ev=ev.n):
                v.wait_ge(ev.h, t_ev)
                v.tensor_scalar(ropeS[:, :, :], ropeS[:, :, :], HALF_PI, None,
                                ALU.add)
                ss(v)
                v.tensor_scalar(stg[:, 32:1056].rearrange("p (a b) -> p a b", b=32),
                                ropeS[:, :, :], PI_F, TWO_PI, ALU.is_gt, ALU.mult)
                ss(v)
                v.tensor_tensor(ropeS[:, :, :], ropeS[:, :, :],
                                stg[:, 32:1056].rearrange("p (a b) -> p a b", b=32),
                                ALU.subtract)
                ss(v)
                v.tensor_scalar(ropeS[:, :, :], ropeS[:, :, :], PI_F, -PI_F,
                                ALU.min, ALU.max)
                v.fence(vx.h)
            blk.vector(vecc)
            vx.inc()

            def scc(s, t_vx=vx.n):
                s.wait_ge(vx.h, t_vx)
                s.activation(ropeC[:, :, :], ropeS[:, :, :], AF.Sin,
                             bias=cbias[:, 2:3], scale=1.0).then_inc(ev.h, 1)
            blk.scalar(scc)
            ev.inc()

            def vecc2(v, t_ev=ev.n):
                v.wait_ge(ev.h, t_ev)
                v.tensor_copy(ropeS[:, :, :],
                              stg2[:, 0:1024].rearrange("p (a b) -> p a b", b=32))
                v.fence(vx.h)
            blk.vector(vecc2)
            vx.inc()
            barrier()

            def ldw(sy):
                sy.dma_start(out=kfeat.rearrange("p (c n) -> p c n", c=NC_D),
                             in_=dr["wqT"].rearrange("c p n -> p c n")).then_inc(ld.h, 16)
                sy.dma_start(out=qfeat.rearrange("p (c n) -> p c n", c=NC_D),
                             in_=dr["wkT"].rearrange("c p n -> p c n")).then_inc(ld.h, 16)
                sy.dma_start(out=vbufF[:, 0:4096].rearrange("p (c n) -> p c n", c=NC_D),
                             in_=dr["wvT"].rearrange("c p n -> p c n")).then_inc(ld.h, 16)
            blk.sync(ldw)
            t_w = ld.inc(48)

            qT = {"mm": {}, "ev": {}}
            for tt in range(32):
                xa = stg if tt % 2 == 0 else stg2
                pq = psA if tt % 2 == 0 else psB
                half, lt = tt // 16, tt % 16

                def ldf(sy, xa=xa, half=half, lt=lt, t_mm=mm.n):
                    sy.wait_ge(mm.h, t_mm)
                    sy.dma_start(out=xa[:, 0:1024].rearrange("p (c t) -> p c t", c=NC_D),
                                 in_=dr["ag_out"][half, :, :, lt * 128:(lt + 1) * 128]
                                 .rearrange("c p t -> p c t")).then_inc(ld.h, 16)
                blk.sync(ldf)
                t_ld = ld.inc(16)

                def tens(t, xa=xa, pq=pq, t_ld=t_ld, t_w=t_w, t_ev=ev.n):
                    t.wait_ge(ld.h, max(t_ld, t_w))
                    t.wait_ge(ev.h, t_ev)
                    for c in range(NC_D):
                        t.matmul(pq[:, 0, :], xa[:, c * 128:(c + 1) * 128],
                                 kfeat[:, c * 512:(c + 1) * 512],
                                 start=(c == 0), stop=(c == 7))
                        t.matmul(pq[:, 1, :], xa[:, c * 128:(c + 1) * 128],
                                 qfeat[:, c * 512:(c + 1) * 512],
                                 start=(c == 0), stop=(c == 7))
                        t.matmul(pq[:, 2, :], xa[:, c * 128:(c + 1) * 128],
                                 vbufF[:, c * 512:(c + 1) * 512],
                                 start=(c == 0), stop=(c == 7))
                    t.fence(mm.h)
                blk.tensor(tens)
                t_mm = mm.inc()
                qT["mm"][tt] = t_mm

                def sc(v, pq=pq, t_mm=t_mm, t_sp=sp.n):
                    v.wait_ge(mm.h, t_mm)
                    v.wait_ge(sp.h, t_sp)
                    v.tensor_copy(ptile[:, 0, :], pq[:, 0, :])
                    v.tensor_copy(ptile[:, 1, :], pq[:, 1, :])
                    v.tensor_copy(ptile[:, 2, :], pq[:, 2, :]).then_inc(ev.h, 1)
                blk.vector(sc)
                t_ev = ev.inc()
                qT["ev"][tt] = t_ev

                def vec(v, tt=tt, t_ev=t_ev):
                    v.wait_ge(ev.h, t_ev)
                    co = ropeC[:, tt, :]
                    si = ropeS[:, tt, :]
                    for which in (0, 1):
                        q4 = ptile[:, which, :].rearrange("p (h f two) -> p h f two",
                                                          h=8, two=2)
                        r4 = ptile[:, 3, :].rearrange("p (h f two) -> p h f two",
                                                      h=8, two=2)
                        for h in range(8):
                            v.scalar_tensor_tensor(r4[:, h, :, 0], q4[:, h, :, 1],
                                                   -1.0, si, ALU.mult, ALU.mult)
                            v.tensor_tensor(r4[:, h, :, 1], q4[:, h, :, 0], si,
                                            ALU.mult)
                            ss(v)
                            v.tensor_tensor(q4[:, h, :, :], q4[:, h, :, :],
                                            co.unsqueeze(2).to_broadcast([128, 32, 2]),
                                            ALU.mult)
                        ss(v)
                        v.tensor_tensor(ptile[:, which, :], ptile[:, which, :],
                                        ptile[:, 3, :], ALU.add)
                        if which == 0:
                            ss(v)
                    v.fence(vx.h)
                blk.vector(vec)
                t_vx = vx.inc()

                def gp2(g, tt=tt, t_vx=t_vx):
                    g.wait_ge(vx.h, t_vx)
                    g.dma_start(out=dr["qsp"][tt, :, :], in_=ptile[:, 0, :]).then_inc(sp.h, 16)
                    g.dma_start(out=dr["ksp"][tt, :, :], in_=ptile[:, 1, :]).then_inc(sp.h, 16)
                    g.dma_start(out=dr["vsp"][tt, :, :], in_=ptile[:, 2, :]).then_inc(sp.h, 16)
                blk.gpsimd(gp2)
                sp.inc(48)
            barrier()

            def vecb(v):
                v.memset(bmask[:, :], 1.0)
                v.memset(vbuf[:, :, 64:65], 1.0)
                v.memset(vbuf[:, :, 129:130], 1.0)
                v.fence(vx.h)
            blk.vector(vecb)
            vx.inc()

            def gpm(g, t_vx=vx.n):
                g.wait_ge(vx.h, t_vx)
                g.affine_select(bmask[:, :], bmask[:, :], [[1, 896]], ALU.is_ge,
                                0.0, base=-384, channel_multiplier=-1)
                g.fence(gx.h)
            blk.gpsimd(gpm)
            gx.inc()
            barrier()

            for pair in range(4):
                for tt in range(32):
                    def ldf(sy, tt=tt, pair=pair, t_mm=mm.n, t_vx=vx.n):
                        sy.wait_ge(mm.h, t_mm)
                        sy.wait_ge(vx.h, t_vx)
                        sy.dma_start(out=stg[:, (tt % 2) * 128:(tt % 2) * 128 + 128],
                                     in_=dr["ksp"][tt, :, pair * 128:(pair + 1) * 128]
                                     ).then_inc(ld.h, 16)
                        sy.dma_start(out=stg[:, 256 + (tt % 2) * 128:256 + (tt % 2) * 128 + 128],
                                     in_=dr["qsp"][tt, :, pair * 128:(pair + 1) * 128]
                                     ).then_inc(ld.h, 16)
                        sy.dma_start(out=stg2[:, (tt % 4) * 128:(tt % 4) * 128 + 128],
                                     in_=dr["vsp"][tt, :, pair * 128:(pair + 1) * 128]
                                     ).then_inc(ld.h, 16)
                    blk.sync(ldf)
                    t_ld = ld.inc(48)

                    def tens(t, tt=tt, t_ld=t_ld, t_ev=ev.n):
                        t.wait_ge(ld.h, t_ld)
                        t.wait_ge(ev.h, t_ev)
                        t.transpose(psB[:, tt % 2, 0:128],
                                    stg[:, (tt % 2) * 128:(tt % 2) * 128 + 128],
                                    ident[:, :])
                        t.transpose(psB[:, 2 + tt % 2, 0:128],
                                    stg[:, 256 + (tt % 2) * 128:256 + (tt % 2) * 128 + 128],
                                    ident[:, :]).then_inc(mm.h, 1)
                    blk.tensor(tens)
                    t_mm = mm.inc()

                    def sc(v, tt=tt, t_mm=t_mm):
                        v.wait_ge(mm.h, t_mm)
                        v.tensor_copy(kfeat[:, tt * 128:(tt + 1) * 128],
                                      psB[:, tt % 2, 0:128])
                        v.tensor_copy(qfeat[:, tt * 128:(tt + 1) * 128],
                                      psB[:, 2 + tt % 2, 0:128]).then_inc(ev.h, 1)
                    blk.vector(sc)
                    ev.inc()

                    def vecv(v, tt=tt, t_ld=t_ld):
                        v.wait_ge(ld.h, t_ld)
                        v.tensor_copy(vbuf[:, tt, 0:64],
                                      stg2[:, (tt % 4) * 128:(tt % 4) * 128 + 64])
                        v.tensor_copy(vbuf[:, tt, 65:129],
                                      stg2[:, (tt % 4) * 128 + 64:(tt % 4) * 128 + 128])
                        v.fence(vx.h)
                    blk.vector(vecv)
                    vx.inc()
                barrier()

                prev_last_exp = 0
                prev_sc_o = 0
                for hh in range(2):
                    hloc = pair * 2 + hh
                    for qb in range(8):
                        nk = 4 * qb + 4
                        exp_tgt = {}
                        pv_tgt = {}

                        def emit_scores(j, hh=hh, qb=qb):
                            def tens_j(t, j=j, hh=hh, qb=qb,
                                       w_ev=exp_tgt.get(j - 2),
                                       w_prev=prev_last_exp):
                                if w_ev is not None:
                                    t.wait_ge(ev.h, w_ev)
                                elif w_prev:
                                    t.wait_ge(ev.h, w_prev)
                                t.matmul(psB[:, j % 2, :],
                                         kfeat[hh * 64:(hh + 1) * 64,
                                               j * 128:(j + 1) * 128],
                                         qfeat[hh * 64:(hh + 1) * 64,
                                               qb * 512:(qb + 1) * 512],
                                         start=True, stop=True).then_inc(mm.h, 1)
                            blk.tensor(tens_j)
                            t_mm = mm.inc()

                            def sc_j(s, j=j, t_mm=t_mm, w_pv=pv_tgt.get(j - 4)):
                                s.wait_ge(mm.h, t_mm)
                                if w_pv is not None:
                                    s.wait_ge(mm.h, w_pv)
                                s.activation(ptile[:, j % 4, :], psB[:, j % 2, :],
                                             AF.Exp, bias=cbias[:, 0:1],
                                             scale=0.125).then_inc(ev.h, 1)
                            blk.scalar(sc_j)
                            exp_tgt[j] = ev.inc()

                            if j >= 4 * qb:
                                def vec_j(v, j=j, qb=qb, t_exp=exp_tgt[j]):
                                    v.wait_ge(ev.h, t_exp)
                                    off = 384 + qb * 512 - j * 128
                                    v.tensor_tensor(ptile[:, j % 4, :],
                                                    ptile[:, j % 4, :],
                                                    bmask[:, off:off + 512], ALU.mult)
                                    v.fence(vx.h)
                                blk.vector(vec_j)
                                return ("vx", vx.inc())
                            return ("ev", exp_tgt[j])

                        def emit_pv(j, rdy, hh=hh, qb=qb, nk=nk):
                            def tens_pv(t, j=j, hh=hh, rdy=rdy, nk=nk,
                                        w_evac=prev_sc_o):
                                t.wait_ge({"vx": vx, "ev": ev}[rdy[0]].h, rdy[1])
                                if j == 0 and w_evac:
                                    t.wait_ge(ev.h, w_evac)
                                t.matmul(psA[0:65, 0, :],
                                         vbuf[:, j, hh * 65:(hh + 1) * 65],
                                         ptile[:, j % 4, :],
                                         start=(j == 0),
                                         stop=(j == nk - 1)).then_inc(mm.h, 1)
                            blk.tensor(tens_pv)
                            pv_tgt[j] = mm.inc()

                        rdys = {}
                        if FLASH_PIPE:
                            for j in range(nk):
                                rdys[j] = emit_scores(j)
                                if j >= 1:
                                    emit_pv(j - 1, rdys[j - 1])
                            emit_pv(nk - 1, rdys[nk - 1])
                        else:
                            for j in range(nk):
                                rdys[j] = emit_scores(j)
                                emit_pv(j, rdys[j])
                        prev_last_exp = exp_tgt[nk - 1]

                        def vec_e(v, t_mm=pv_tgt[nk - 1], t_sp=sp.n):
                            v.wait_ge(mm.h, t_mm)
                            v.wait_ge(sp.h, t_sp)
                            v.tensor_copy(orow[0:65, 0:512],
                                          psA[0:65, 0, :]).then_inc(ev.h, 1)
                        blk.vector(vec_e)
                        t_pe = ev.inc()
                        prev_sc_o = t_pe

                        sub_tgts = []
                        for sub_ in range(4):
                            def tr_s(t, sub_=sub_, t_pe=t_pe,
                                     w_vx=(sub_tgts[sub_ - 2] if sub_ >= 2 else 0)):
                                if sub_ == 0:
                                    t.wait_ge(ev.h, t_pe)
                                if w_vx:
                                    t.wait_ge(vx.h, w_vx)
                                t.transpose(psA[:, 1 + sub_ % 2, 0:65],
                                            orow[0:65, sub_ * 128:(sub_ + 1) * 128],
                                            ident[0:65, 0:65]).then_inc(mm.h, 1)
                            blk.tensor(tr_s)
                            t_tr = mm.inc()

                            def vec_s(v, sub_=sub_, t_tr=t_tr):
                                v.wait_ge(mm.h, t_tr)
                                v.reciprocal(scal[:, 3 + sub_:4 + sub_],
                                             psA[:, 1 + sub_ % 2, 64:65]).then_inc(sw.h, 1)
                                v.wait_ge(sw.h, sw.inc())
                                v.tensor_scalar(orow[:, 512 + sub_ * 64:512 + (sub_ + 1) * 64],
                                                psA[:, 1 + sub_ % 2, 0:64],
                                                scal[:, 3 + sub_:4 + sub_], None,
                                                ALU.mult)
                                v.fence(vx.h)
                            blk.vector(vec_s)
                            sub_tgts.append(vx.inc())

                        def gp_o(g, hloc=hloc, qb=qb, t_eo=sub_tgts[-1]):
                            g.wait_ge(vx.h, t_eo)
                            g.dma_start(
                                out=dr["amine"][qb * 4:(qb + 1) * 4, :,
                                                hloc * 64:(hloc + 1) * 64]
                                .rearrange("s p d -> p s d"),
                                in_=orow[:, 512:768].rearrange("p (s d) -> p s d", d=64),
                            ).then_inc(sp.h, 16)
                        blk.gpsimd(gp_o)
                        sp.inc(16)
                barrier()

            def ldwo(sy, t_sp=sp.n):
                sy.wait_ge(sp.h, t_sp)
                sy.dma_start(out=wta[:, :].rearrange("p (c n) -> p c n", c=1),
                             in_=dr["woT"][0:1, :, :].rearrange("c p n -> p c n")
                             ).then_inc(ld.h, 16)
                sy.dma_start(out=wtb[:, :].rearrange("p (c n) -> p c n", c=1),
                             in_=dr["woT"][1:2, :, :].rearrange("c p n -> p c n")
                             ).then_inc(ld.h, 16)
                sy.dma_start(out=orow[:, :].rearrange("p (c n) -> p c n", c=1),
                             in_=dr["woT"][2:3, :, :].rearrange("c p n -> p c n")
                             ).then_inc(ld.h, 16)
                sy.dma_start(out=bmask[:, 0:896],
                             in_=dr["woT"][3, :, 0:896]).then_inc(ld.h, 16)
                sy.dma_start(out=ptile[:, 0, 0:128],
                             in_=dr["woT"][3, :, 896:1024]).then_inc(ld.h, 16)
            blk.sync(ldwo)
            t_wo = ld.inc(80)
            wchunk = [wta[:, :], wtb[:, :], orow[:, :], None]

            for tt in range(32):
                buf = stg if tt % 2 == 0 else stg2

                def ldf(sy, tt=tt, buf=buf, t_mm=mm.n):
                    sy.wait_ge(mm.h, t_mm)
                    sy.dma_start(out=buf[:, 0:512],
                                 in_=dr["amine"][tt, :, :]).then_inc(ld.h, 16)
                blk.sync(ldf)
                t_ld = ld.inc(16)

                for c in range(4):
                    def tens(t, tt=tt, buf=buf, c=c, t_ld=t_ld, t_ev=ev.n):
                        if c == 0:
                            t.wait_ge(ld.h, t_ld)
                        t.wait_ge(ev.h, t_ev)
                        t.transpose(psB[:, c % 2, 0:128],
                                    buf[:, c * 128:(c + 1) * 128],
                                    ident[:, :]).then_inc(mm.h, 1)
                    blk.tensor(tens)
                    t_mm = mm.inc()

                    def sc(v, buf=buf, c=c, t_mm=t_mm):
                        v.wait_ge(mm.h, t_mm)
                        v.tensor_copy(buf[:, 512 + c * 128:512 + (c + 1) * 128],
                                      psB[:, c % 2, 0:128]).then_inc(ev.h, 1)
                    blk.vector(sc)
                    ev.inc()

                def tens2(t, tt=tt, buf=buf, t_wo=t_wo, t_evp=ev.n, t_vx=vx.n):
                    t.wait_ge(ev.h, t_evp)
                    t.wait_ge(ld.h, t_wo)
                    t.wait_ge(vx.h, t_vx)
                    for c in range(4):
                        w = wchunk[c]
                        for os_ in range(2):
                            if c < 3:
                                rhs = w[:, os_ * 512:(os_ + 1) * 512]
                                t.matmul(psA[:, os_, :],
                                         buf[:, 512 + c * 128:512 + (c + 1) * 128],
                                         rhs, start=(c == 0), stop=(c == 3))
                            else:
                                if os_ == 0:
                                    t.matmul(psA[:, 0, :],
                                             buf[:, 512 + c * 128:512 + (c + 1) * 128],
                                             bmask[:, 0:512], start=False, stop=True)
                                else:
                                    t.matmul(psA[:, 1, 0:384],
                                             buf[:, 512 + c * 128:512 + (c + 1) * 128],
                                             bmask[:, 512:896], start=False, stop=False)
                                    t.matmul(psA[:, 1, 384:512],
                                             buf[:, 512 + c * 128:512 + (c + 1) * 128],
                                             ptile[:, 0, 0:128], start=False, stop=True)
                    t.fence(mm.h)
                blk.tensor(tens2)
                t_mm2 = mm.inc()

                def vec2(v, tt=tt, buf=buf, t_mm2=t_mm2, t_sp=sp.n):
                    v.wait_ge(mm.h, t_mm2)
                    v.wait_ge(sp.h, t_sp)
                    v.tensor_copy(buf[:, 1024:1536], psA[:, 0, :])
                    v.tensor_copy(buf[:, 1536:2048], psA[:, 1, :])
                    v.fence(vx.h)
                blk.vector(vec2)
                t_vy = vx.inc()

                def gp_y(g, tt=tt, buf=buf, t_vy=t_vy):
                    g.wait_ge(vx.h, t_vy)
                    g.dma_start(out=dr["rs_in"][tt * 128:(tt + 1) * 128, :],
                                in_=buf[:, 1024:2048]).then_inc(sp.h, 16)
                blk.gpsimd(gp_y)
                sp.inc(16)

            def gpr(g, t_sp=sp.n):
                g.wait_ge(sp.h, t_sp)
                g.collective_compute(
                    "ReduceScatter", ALU.add,
                    replica_groups=[[0, 1], [2, 3], [4, 5], [6, 7]],
                    ins=[dr["rs_in"].rearrange("t d -> (t d)").opt()],
                    outs=[dr["rs_out"].rearrange("t d -> (t d)").opt()],
                ).then_inc(cc.h, 1)
                g.wait_ge(cc.h, cc.inc())
                g.sem_inc(gx.h, 1)
            blk.gpsimd(gpr)
            gx.inc()
            barrier()

            for tt in range(16):
                buf = stg if tt % 2 == 0 else stg2

                def ldf2(sy, tt=tt, buf=buf, t_mm=mm.n):
                    sy.wait_ge(mm.h, t_mm)
                    sy.dma_start(out=buf[:, 0:1024],
                                 in_=dr["rs_out"][tt * 128:(tt + 1) * 128, :]
                                 ).then_inc(ld.h, 16)
                blk.sync(ldf2)
                t_ld = ld.inc(16)

                for c in range(NC_D):
                    def tens(t, buf=buf, c=c, t_ld=t_ld, t_vx=vx.n):
                        if c == 0:
                            t.wait_ge(ld.h, t_ld)
                        t.wait_ge(vx.h, t_vx)
                        t.transpose(psB[:, c % 4, 0:128],
                                    buf[:, c * 128:(c + 1) * 128],
                                    ident[:, :]).then_inc(mm.h, 1)
                    blk.tensor(tens)
                    t_mm = mm.inc()

                    def vec(v, tt=tt, c=c, t_mm=t_mm):
                        v.wait_ge(mm.h, t_mm)
                        v.scalar_tensor_tensor(xres[:, c, tt * 128:(tt + 1) * 128],
                                               psB[:, c % 4, 0:128], 1.0,
                                               xres[:, c, tt * 128:(tt + 1) * 128],
                                               ALU.mult, ALU.add)
                        v.fence(vx.h)
                    blk.vector(vec)
                    vx.inc()
            barrier()

        def store_out():
            for tt in range(16):
                for c in range(NC_D):
                    def tens(t, tt=tt, c=c, t_ev=ev.n):
                        t.wait_ge(ev.h, t_ev)
                        t.transpose(psB[:, c % 4, 0:128],
                                    xres[:, c, tt * 128:(tt + 1) * 128],
                                    ident[:, :]).then_inc(mm.h, 1)
                    blk.tensor(tens)
                    t_mm = mm.inc()

                    def sc(v, tt=tt, c=c, t_mm=t_mm, t_sp=sp.n):
                        v.wait_ge(mm.h, t_mm)
                        v.wait_ge(sp.h, t_sp)
                        v.tensor_copy(stg[:, c * 128:(c + 1) * 128],
                                      psB[:, c % 4, 0:128]).then_inc(ev.h, 1)
                    blk.vector(sc)
                    ev.inc()

                def gp(g, tt=tt, t_ev=ev.n):
                    g.wait_ge(ev.h, t_ev)
                    g.dma_start(out=out[tt * 128:(tt + 1) * 128, :],
                                in_=stg[:, 0:1024]).then_inc(sp.h, 16)
                blk.gpsimd(gp)
                sp.inc(16)

        p0_consts()
        p0_input()
        if stage >= 1:
            p0_weights()
        barrier()
        if "wuT0" in tap:
            def ldt(sy, t_mm=mm.n):
                sy.dma_start(out=stg[:, 0:3072],
                             in_=dr["wuT"][0, :, :]).then_inc(ld.h, 16)
            blk.sync(ldt)
            mktap("wuT0", stg[:, 0:3072], [128, 3072], wait=(ld, ld.inc(16)))
        if stage >= 2:
            ffn_call("f1")
        if stage >= 3:
            attention()
        if stage >= 4:
            ffn_call("f2")
        store_out()

        def fin(g, t_sp=sp.n):
            g.wait_ge(sp.h, t_sp)
        blk.gpsimd(fin)

    return nc, tap_outs


def kernel(**inputs):
    x = np.ascontiguousarray(np.asarray(inputs["L0_attn_x"], np.float32))
    wq = np.asarray(inputs["L0_attn_wq"], np.float32)
    wk = np.asarray(inputs["L0_attn_wk"], np.float32)
    wv = np.asarray(inputs["L0_attn_wv"], np.float32)
    wo = np.ascontiguousarray(np.asarray(inputs["L0_attn_wo"], np.float32))
    wg = np.ascontiguousarray(np.asarray(inputs["L0_ffn_w_gate"], np.float32))
    wu = np.ascontiguousarray(np.asarray(inputs["L0_ffn_w_up"], np.float32))
    wd = np.ascontiguousarray(np.asarray(inputs["L0_ffn_w_down"], np.float32))

    nc, _ = build_program(stage=STAGE)
    in_maps = []
    for core in range(8):
        b, h = core // 2, core % 2
        in_maps.append({
            "x": np.ascontiguousarray(x[b, h * T:(h + 1) * T, :]),
            "wq": np.ascontiguousarray(wq[h * 512:(h + 1) * 512, :]),
            "wk": np.ascontiguousarray(wk[h * 512:(h + 1) * 512, :]),
            "wv": np.ascontiguousarray(wv[h * 512:(h + 1) * 512, :]),
            "wo": np.ascontiguousarray(wo[:, h * 512:(h + 1) * 512]),
            "wg": wg, "wu": wu, "wd": wd,
        })
    res = run_bass_kernel_spmd(nc, in_maps, core_ids=list(range(8)))
    outp = np.empty((4, S, D), np.float32)
    for core in range(8):
        b, h = core // 2, core % 2
        outp[b, h * T:(h + 1) * T, :] = res.results[core]["out"]
    return outp


# version 09
# speedup vs baseline: 1.0955x; 1.0213x over previous
import numpy as np
from contextlib import ExitStack

import concourse.bass as bass
import concourse.mybir as mybir
from concourse.bass_utils import run_bass_kernel_spmd

F32 = mybir.dt.float32
AF = mybir.ActivationFunctionType
ALU = mybir.AluOpType
AX = mybir.AxisListType.X
AXY = mybir.AxisListType.XY

MAGIC = 12582912.0
QMAX = 32767.0
C_EXP = 14.0
EPS_LN = 1e-5
LN10K = 9.210340371976184
TWO_PI = 6.283185307179586
CW1 = 6.28125
CW2 = float(np.float32(TWO_PI - 6.28125))
INV_2PI = 0.15915494309189535
HALF_PI = float(np.float32(np.pi / 2))
PI_F = float(np.float32(np.pi))
R7A = float(np.float32(1.0 / 7.0))
R7B = float(np.float32(1.0 / 7.0 - float(np.float32(1.0 / 7.0))))
RINV = float(np.float32(np.exp(-LN10K / 32.0)))

T = 2048
D = 1024
NC_D = 8
I3 = 3072
NI = 24
S = 4096
STAGE = 4
FFN_SUB = 7
FLASH_PIPE = True


class Sem:
    def __init__(self, ctx, nc, name):
        self.h = ctx.enter_context(nc.semaphore(name))
        self.n = 0

    def inc(self, k=1):
        self.n += k
        return self.n


_SEQ_ONLY = {"wait_ge", "wait_op", "sem_inc", "nop", "maybe_wait_ge"}


class EW:

    def __init__(self, eng):
        self._e = eng
        self._last = None

    def __getattr__(self, k):
        a = getattr(self._e, k)
        if callable(a) and k not in _SEQ_ONLY and not k.startswith("_"):
            def w(*args, **kw):
                r = a(*args, **kw)
                if k not in _SEQ_ONLY:
                    self._last = r
                return r
            return w
        return a

    def wait_ge(self, sem, val):
        return self._e.wait_ge(sem, val)

    def sem_inc(self, sem, val):
        return self._e.sem_inc(sem, val)

    def fence(self, sem, n=1):
        assert self._last is not None, "fence with no prior datapath op"
        self._last.then_inc(sem, n)


def build_program(stage=STAGE, taps=()):
    nc = bass.Bass("TRN2", target_bir_lowering=False,
                   detect_race_conditions=False)
    tap = set(taps)

    x_in = nc.dram_tensor("x", [T, D], F32, kind="ExternalInput").ap()
    wq_in = nc.dram_tensor("wq", [512, D], F32, kind="ExternalInput").ap()
    wk_in = nc.dram_tensor("wk", [512, D], F32, kind="ExternalInput").ap()
    wv_in = nc.dram_tensor("wv", [512, D], F32, kind="ExternalInput").ap()
    wo_in = nc.dram_tensor("wo", [D, 512], F32, kind="ExternalInput").ap()
    wg_in = nc.dram_tensor("wg", [I3, D], F32, kind="ExternalInput").ap()
    wu_in = nc.dram_tensor("wu", [I3, D], F32, kind="ExternalInput").ap()
    wd_in = nc.dram_tensor("wd", [D, I3], F32, kind="ExternalInput").ap()
    out = nc.dram_tensor("out", [T, D], F32, kind="ExternalOutput").ap()

    tap_outs = {}

    dr = {}
    dr["wqT"] = nc.dram_tensor("wqT", [NC_D, 128, 512], F32).ap()
    dr["wkT"] = nc.dram_tensor("wkT", [NC_D, 128, 512], F32).ap()
    dr["wvT"] = nc.dram_tensor("wvT", [NC_D, 128, 512], F32).ap()
    dr["woT"] = nc.dram_tensor("woT", [4, 128, D], F32).ap()
    dr["wgT"] = nc.dram_tensor("wgT", [NC_D, 128, I3], F32).ap()
    dr["wuT"] = nc.dram_tensor("wuT", [NC_D, 128, I3], F32).ap()
    dr["wdT"] = nc.dram_tensor("wdT", [NI, 128, D], F32).ap()
    dr["up"] = nc.dram_tensor("up_sp", [NI, 128, T], F32).ap()
    dr["gate"] = nc.dram_tensor("gate_sp", [NI, 128, T], F32).ap()
    dr["qu"] = nc.dram_tensor("qu_sp", [NI, 128, T], F32).ap()
    dr["qg"] = nc.dram_tensor("qg_sp", [NI, 128, T], F32).ap()
    dr["qt"] = nc.dram_tensor("qt_sp", [NI, 128, T], F32).ap()
    dr["ag_in"] = nc.dram_tensor("ag_in", [NC_D, 128, T], F32).ap()
    dr["ag_out"] = nc.dram_tensor("ag_out", [2, NC_D, 128, T], F32).ap()
    dr["qsp"] = nc.dram_tensor("q_sp", [32, 128, 512], F32).ap()
    dr["ksp"] = nc.dram_tensor("k_sp", [32, 128, 512], F32).ap()
    dr["vsp"] = nc.dram_tensor("v_sp", [32, 128, 512], F32).ap()
    dr["amine"] = nc.dram_tensor("amine", [32, 128, 512], F32).ap()
    dr["rs_in"] = nc.dram_tensor("rs_in", [S, D], F32).ap()
    dr["rs_out"] = nc.dram_tensor("rs_out", [T, D], F32).ap()
    dr["ar_in"] = nc.dram_tensor("ar_in", [256], F32).ap()
    dr["ar_out"] = nc.dram_tensor("ar_out", [256], F32).ap()
    dr["br"] = nc.dram_tensor("br", [16], F32).ap()
    dr["row"] = nc.dram_tensor("rowb", [4096], F32).ap()

    with ExitStack() as ctx:
        e = ctx.enter_context

        xres = e(nc.sbuf_tensor("xres", [128, NC_D, T], F32))
        xlnT = e(nc.sbuf_tensor("xln", [128, NC_D * T], F32))
        stg = e(nc.sbuf_tensor("stg", [128, 4096], F32))
        stg2 = e(nc.sbuf_tensor("stg2", [128, 4096], F32))
        pool = e(nc.sbuf_tensor("pool", [128, 4096], F32))
        wta = e(nc.sbuf_tensor("wta", [128, 1024], F32))
        wtb = e(nc.sbuf_tensor("wtb", [128, 1024], F32))
        ident = e(nc.sbuf_tensor("ident", [128, 128], F32))
        cbias = e(nc.sbuf_tensor("cbias", [128, 8], F32))
        scal = e(nc.sbuf_tensor("scal", [128, 16], F32))
        sc1 = e(nc.sbuf_tensor("sc1", [1, 384], F32))
        mcol = e(nc.sbuf_tensor("mcol", [128, 224], F32))
        bscale = e(nc.sbuf_tensor("bscale", [128, 96], F32))

        xln = xlnT[:, :].rearrange("p (c t) -> p c t", c=NC_D)
        kfeat = xlnT[:, 0:4096]
        qfeat = xlnT[:, 4096:8192]
        vbufF = xlnT[:, 8192:12352]
        vbuf = vbufF.rearrange("p (a b) -> p a b", b=130)
        bmask = xlnT[:, 12352:13248]
        orow = xlnT[:, 13248:14272]
        mubc = pool[:, 0:2048]
        rsbc = pool[:, 2048:4096]
        ropeC = pool[:, 0:1024].rearrange("p (a b) -> p a b", b=32)
        ropeS = pool[:, 1024:2048].rearrange("p (a b) -> p a b", b=32)
        ptile = pool[:, 2048:4096].rearrange("p (a b) -> p a b", b=512)

        psA = e(nc.psum_tensor("psA", [128, 4, 512], F32))
        psB = e(nc.psum_tensor("psB", [128, 4, 512], F32))

        ld = Sem(ctx, nc, "ld")
        sp = Sem(ctx, nc, "sp")
        mm = Sem(ctx, nc, "mm")
        ev = Sem(ctx, nc, "ev")
        vx = Sem(ctx, nc, "vx")
        gx = Sem(ctx, nc, "gx")
        cc = Sem(ctx, nc, "cc")
        bs = Sem(ctx, nc, "bs")
        sw = Sem(ctx, nc, "sw")
        blk0 = e(nc.Block())

        def vrecip(v, out_ap, in_ap):
            v.reciprocal(out_ap, in_ap).then_inc(sw.h, 1)
            v.wait_ge(sw.h, sw.inc())

        def ss(v):
            v._last.then_inc(sw.h, 1)
            v.wait_ge(sw.h, sw.inc())

        class BlkW:
            def __getattr__(self, name):
                def reg(f):
                    getattr(blk0, name)(lambda eng: f(EW(eng)))
                return reg
        blk = BlkW()

        def barrier():
            tgt = bs.n + 1
            bs.n = tgt

            def gpf(en, targets=(ld.n, sp.n, cc.n, mm.n, ev.n, vx.n, gx.n),
                    t=tgt):
                for sem, val in zip((ld, sp, cc, mm, ev, vx, gx), targets):
                    if val:
                        en.wait_ge(sem.h, val)
                en.sem_inc(bs.h, 1)
            blk.gpsimd(gpf)
            for name in ("sync", "vector", "scalar", "tensor", "gpsimd"):
                def f(en, t=tgt):
                    en.wait_ge(bs.h, t)
                getattr(blk, name)(f)

        def mktap(name, src_ap, shape, wait=None):
            if name not in tap:
                return
            tp = nc.dram_tensor("tap_" + name, shape, F32, kind="ExternalOutput").ap()
            tap_outs[name] = tp

            def gp(g):
                if wait is not None:
                    g.wait_ge(wait[0].h, wait[1])
                g.dma_start(out=tp, in_=src_ap).then_inc(sp.h, 16)
            blk.gpsimd(gp)
            sp.inc(16)
            barrier()

        def p0_consts():
            def vec(v):
                v.memset(cbias[:, :], 0.0)
                ss(v)
                v.memset(cbias[:, 0:1], -C_EXP)
                v.memset(cbias[:, 1:2], EPS_LN)
                v.memset(scal[:, 15:16], 1.0)
                v.memset(stg[:, 0:128], 1.0)
                v.fence(vx.h)
            blk.vector(vec)
            vx.inc()

            def gp(g, t=vx.n):
                g.wait_ge(vx.h, t)
                g.affine_select(ident[:, :], stg[:, 0:128], [[1, 128]],
                                ALU.is_equal, 0.0, base=0, channel_multiplier=-1)
                g.fence(gx.h)
            blk.gpsimd(gp)
            gx.inc()

        def p0_input():
            for tt in range(16):
                buf = stg if tt % 2 == 0 else stg2

                def ldf(sy, tt=tt, buf=buf, t_mm=mm.n):
                    sy.wait_ge(mm.h, t_mm)
                    sy.dma_start(out=buf[:, 0:1024],
                                 in_=x_in[tt * 128:(tt + 1) * 128, :]).then_inc(ld.h, 16)
                blk.sync(ldf)
                t_ld = ld.inc(16)

                for c in range(NC_D):
                    def tens(t, tt=tt, buf=buf, c=c, t_ld=t_ld, t_ev=ev.n):
                        if c == 0:
                            t.wait_ge(ld.h, t_ld)
                        t.wait_ge(ev.h, t_ev)
                        t.transpose(psB[:, c % 4, 0:128],
                                    buf[:, c * 128:(c + 1) * 128],
                                    ident[:, :]).then_inc(mm.h, 1)
                    blk.tensor(tens)
                    t_mm = mm.inc()

                    def sc(v, tt=tt, c=c, t_mm=t_mm):
                        v.wait_ge(mm.h, t_mm)
                        v.tensor_copy(xres[:, c, tt * 128:(tt + 1) * 128],
                                      psB[:, c % 4, 0:128]).then_inc(ev.h, 1)
                    blk.vector(sc)
                    ev.inc()

        def prep_weight(w_ap, rows, dst, lpbq):
            cols = w_ap.shape[1]
            ncol = cols // 128
            wrow = pool[:, 0:cols]
            for r in range(rows // 128):
                def ldf(sy, r=r, t_mm=mm.n):
                    sy.wait_ge(mm.h, t_mm)
                    sy.dma_start(out=wrow[:, :],
                                 in_=w_ap[r * 128:(r + 1) * 128, :]).then_inc(ld.h, 16)
                blk.sync(ldf)
                t_ld = ld.inc(16)

                if lpbq:
                    nb = cols // 32

                    def vec(v, t_ld=t_ld, nb=nb):
                        v.wait_ge(ld.h, t_ld)
                        wv = wrow.rearrange("p (b k) -> p b k", k=32)
                        v.tensor_reduce(bscale[:, 0:nb], wv, AX, ALU.max,
                                        apply_absolute_value=True)
                        ss(v)
                        v.tensor_scalar(mcol[:, 0:nb], bscale[:, 0:nb], R7A,
                                        None, ALU.mult)
                        ss(v)
                        v.scalar_tensor_tensor(mcol[:, 0:nb], bscale[:, 0:nb],
                                               R7B, mcol[:, 0:nb],
                                               ALU.mult, ALU.add)
                        ss(v)
                        vrecip(v, bscale[:, 0:nb], mcol[:, 0:nb])
                        v.tensor_tensor(wv, wv,
                                        bscale[:, 0:nb].unsqueeze(2)
                                        .to_broadcast([128, nb, 32]), ALU.mult)
                        ss(v)
                        v.tensor_scalar(wv, wv, MAGIC, MAGIC, ALU.add, ALU.subtract)
                        ss(v)
                        v.tensor_tensor(wv, wv,
                                        mcol[:, 0:nb].unsqueeze(2)
                                        .to_broadcast([128, nb, 32]), ALU.mult)
                        v.fence(vx.h)
                    blk.vector(vec)
                    rdy = ("vx", vx.inc())
                else:
                    rdy = ("ld", t_ld)

                for c in range(ncol):
                    def tens(t, c=c, rdy=rdy, t_ev=ev.n):
                        if c == 0:
                            t.wait_ge({"vx": vx, "ld": ld}[rdy[0]].h, rdy[1])
                        t.wait_ge(ev.h, t_ev)
                        t.transpose(psB[:, c % 4, 0:128],
                                    wrow[:, c * 128:(c + 1) * 128],
                                    ident[:, :]).then_inc(mm.h, 1)
                    blk.tensor(tens)
                    t_mm = mm.inc()

                    def sc(v, c=c, t_mm=t_mm, t_sp=sp.n):
                        v.wait_ge(mm.h, t_mm)
                        v.wait_ge(sp.h, t_sp)
                        v.tensor_copy(stg2[:, (c % 8) * 128:(c % 8) * 128 + 128],
                                      psB[:, c % 4, 0:128]).then_inc(ev.h, 1)
                    blk.vector(sc)
                    t_ev = ev.inc()

                    def gp(g, c=c, r=r, t_ev=t_ev):
                        g.wait_ge(ev.h, t_ev)
                        g.dma_start(out=dst[c, :, r * 128:(r + 1) * 128],
                                    in_=stg2[:, (c % 8) * 128:(c % 8) * 128 + 128]
                                    ).then_inc(sp.h, 16)
                    blk.gpsimd(gp)
                    sp.inc(16)

        def p0_weights():
            prep_weight(wq_in, 512, dr["wqT"], False)
            prep_weight(wk_in, 512, dr["wkT"], False)
            prep_weight(wv_in, 512, dr["wvT"], False)
            prep_weight(wo_in, D, dr["woT"], False)
            prep_weight(wu_in, I3, dr["wuT"], True)
            prep_weight(wg_in, I3, dr["wgT"], True)
            prep_weight(wd_in, D, dr["wdT"], True)

        def layer_norm(tag):
            for g4 in range(4):
                sl = slice(g4 * 512, (g4 + 1) * 512)

                def sc(v, sl=sl, t_vx=vx.n):
                    v.wait_ge(vx.h, t_vx)
                    v.tensor_tensor(stg[:, :].rearrange("p (c t) -> p c t", c=NC_D),
                                    xres[:, :, sl], xres[:, :, sl],
                                    ALU.mult).then_inc(ev.h, 1)
                blk.vector(sc)
                t_sq = ev.inc()

                def tens(t, sl=sl, t_sq=t_sq, t_vx=vx.n):
                    t.wait_ge(ev.h, t_sq)
                    t.wait_ge(vx.h, t_vx)
                    for c in range(NC_D):
                        t.matmul(psB[0:1, 0, 0:512], scal[:, 15:16],
                                 xres[:, c, sl], start=(c == 0), stop=(c == 7))
                    for c in range(NC_D):
                        t.matmul(psB[0:1, 1, 0:512], scal[:, 15:16],
                                 stg[:, c * 512:(c + 1) * 512],
                                 start=(c == 0), stop=(c == 7))
                    t.fence(mm.h)
                blk.tensor(tens)
                t_mm = mm.inc()

                def vec(v, sl=sl, t_mm=t_mm):
                    v.wait_ge(mm.h, t_mm)
                    v.tensor_scalar(mubc[0:1, sl], psB[0:1, 0, 0:512],
                                    1.0 / 1024.0, None, ALU.mult)
                    v.tensor_scalar(rsbc[0:1, sl],
                                    psB[0:1, 1, 0:512], 1.0 / 1024.0, None, ALU.mult)
                    ss(v)
                    v.tensor_tensor(stg2[0:1, 0:512], mubc[0:1, sl], mubc[0:1, sl],
                                    ALU.mult)
                    ss(v)
                    v.tensor_tensor(rsbc[0:1, sl], rsbc[0:1, sl], stg2[0:1, 0:512],
                                    ALU.subtract)
                    v.fence(vx.h)
                blk.vector(vec)
                t_vx = vx.inc()

                def sc2(s, sl=sl, t_vx=t_vx):
                    s.wait_ge(vx.h, t_vx)
                    s.activation(rsbc[0:1, sl], rsbc[0:1, sl], AF.Ln,
                                 bias=cbias[0:1, 1:2], scale=1.0).then_inc(sw.h, 1)
                    s.wait_ge(sw.h, sw.inc())
                    s.activation(rsbc[0:1, sl], rsbc[0:1, sl], AF.Exp,
                                 bias=cbias[0:1, 2:3], scale=-0.5).then_inc(ev.h, 1)
                blk.scalar(sc2)
                t_ev = ev.inc()

                def vec2(v, sl=sl, t_ev=t_ev):
                    v.wait_ge(ev.h, t_ev)
                    v.fence(vx.h) if False else v.tensor_copy(
                        sc1[0:1, 380:381], rsbc[0:1, sl.start:sl.start + 1])
                    v.fence(vx.h)
                blk.vector(vec2)
                vx.inc()

            def gp(g, t_vx=vx.n):
                g.wait_ge(vx.h, t_vx)
                g.dma_start(out=dr["row"][0:2048], in_=mubc[0:1, :]).then_inc(sp.h, 16)
                g.dma_start(out=dr["row"][2048:4096], in_=rsbc[0:1, :]).then_inc(sp.h, 16)
                g.wait_ge(sp.h, sp.inc(32))
                g.dma_start(out=mubc[:, :],
                            in_=dr["row"][0:2048].unsqueeze(0).to_broadcast([128, 2048])).then_inc(sp.h, 16)
                g.dma_start(out=rsbc[:, :],
                            in_=dr["row"][2048:4096].unsqueeze(0).to_broadcast([128, 2048])).then_inc(sp.h, 16)
                g.wait_ge(sp.h, sp.inc(32))
                g.sem_inc(gx.h, 1)
            blk.gpsimd(gp)
            gx.inc()

            def vec3(v, t_gx=gx.n):
                v.wait_ge(gx.h, t_gx)
                for c in range(NC_D):
                    v.tensor_tensor(xln[:, c, :], xres[:, c, :], mubc[:, :],
                                    ALU.subtract)
                    ss(v)
                    v.tensor_tensor(xln[:, c, :], xln[:, c, :], rsbc[:, :],
                                    ALU.mult)
                v.fence(vx.h)
            blk.vector(vec3)
            vx.inc()
            mktap("xln_" + tag, xlnT[:, :], [128, NC_D * T], wait=(vx, vx.n))

        def allreduce_max(n_vals, compute_scalars, nbc):
            def gp(g, t_vx=vx.n, n_vals=n_vals):
                g.wait_ge(vx.h, t_vx)
                for j in range(n_vals):
                    g.dma_start(out=dr["ar_in"][j * 128:(j + 1) * 128],
                                in_=mcol[:, 200 + j:201 + j]).then_inc(sp.h, 16)
                g.wait_ge(sp.h, sp.inc(16 * n_vals))
                g.collective_compute(
                    "AllReduce", ALU.max,
                    replica_groups=[[0, 1, 2, 3, 4, 5, 6, 7]],
                    ins=[dr["ar_in"][0:n_vals * 128].opt()],
                    outs=[dr["ar_out"][0:n_vals * 128].opt()],
                ).then_inc(cc.h, 1)
                g.wait_ge(cc.h, cc.inc())
                g.dma_start(out=sc1[0:1, 0:n_vals * 128],
                            in_=dr["ar_out"][0:n_vals * 128]).then_inc(sp.h, 16)
                g.wait_ge(sp.h, sp.inc(16))
                g.sem_inc(gx.h, 1)
            blk.gpsimd(gp)
            gx.inc()

            def vec(v, t_gx=gx.n, n_vals=n_vals):
                v.wait_ge(gx.h, t_gx)
                for j in range(n_vals):
                    r = v.tensor_reduce(sc1[0:1, 320 + j:321 + j],
                                        sc1[0:1, j * 128:(j + 1) * 128], AX, ALU.max)
                r.then_inc(sw.h, 1)
                v.wait_ge(sw.h, sw.inc())
                compute_scalars(v)
                v.fence(vx.h)
            blk.vector(vec)
            vx.inc()

            def gp2(g, t_vx=vx.n, nbc=nbc):
                g.wait_ge(vx.h, t_vx)
                g.dma_start(out=dr["br"][0:nbc], in_=sc1[0:1, 340:340 + nbc]).then_inc(sp.h, 16)
                g.wait_ge(sp.h, sp.inc(16))
                g.dma_start(out=scal[:, 0:nbc],
                            in_=dr["br"][0:nbc].unsqueeze(0).to_broadcast([128, nbc])).then_inc(sp.h, 16)
                g.wait_ge(sp.h, sp.inc(16))
                g.sem_inc(gx.h, 1)
            blk.gpsimd(gp2)
            gx.inc()
            barrier()

        def ffn_call(tag):
            sub = FFN_SUB
            layer_norm(tag)
            if sub < 2:
                return

            def vecm(v, t_vx=vx.n):
                v.wait_ge(vx.h, t_vx)
                v.tensor_reduce(mcol[:, 200:201],
                                xlnT[:, :], AX, ALU.max,
                                apply_absolute_value=True)
                v.fence(vx.h)
            blk.vector(vecm)
            vx.inc()

            def scalars1(v):
                vrecip(v, sc1[0:1, 340:341], sc1[0:1, 340:341] if False else sc1[0:1, 320:321])
                v.tensor_scalar(sc1[0:1, 340:341], sc1[0:1, 340:341], QMAX, None,
                                ALU.mult)
                ss(v)
                v.tensor_scalar(sc1[0:1, 300:301], sc1[0:1, 320:321], 1.0 / QMAX,
                                None, ALU.mult)
            allreduce_max(1, scalars1, 1)
            mktap("sc1_" + tag, sc1[0:1, :], [1, 384], wait=(gx, gx.n))
            mktap("scal_" + tag, scal[:, :], [128, 16], wait=(gx, gx.n))
            mktap("mcol_" + tag, mcol[:, :], [128, 224], wait=(gx, gx.n))
            if sub < 3:
                return

            def vecq(v):
                v.tensor_scalar(xln[:, :, :], xln[:, :, :], scal[:, 0:1], MAGIC,
                                ALU.mult, ALU.add)
                ss(v)
                v.tensor_scalar(xln[:, :, :], xln[:, :, :], MAGIC, None,
                                ALU.subtract)
                v.fence(vx.h)
            blk.vector(vecq)
            vx.inc()
            mktap("qx_" + tag, xlnT[:, :], [128, NC_D * T], wait=(vx, vx.n))
            barrier()
            if sub < 4:
                return

            bT = {"mm": {}, "ev": {}, "vx": {}, "sp": {}}
            step = [0]
            for which, wsrc, spdst, mbase in (("u", dr["wuT"], dr["up"], 8),
                                              ("g", dr["wgT"], dr["gate"], 40)):
                for i in range(NI):
                    k_ = step[0]
                    par = k_ % 2
                    wbuf = wta if par == 0 else wtb
                    ps = psA if par == 0 else psB
                    sbuf = stg if par == 0 else stg2

                    def ldf(sy, i=i, wbuf=wbuf, wsrc=wsrc,
                            t_mm=bT["mm"].get(k_ - 2, 0)):
                        if t_mm:
                            sy.wait_ge(mm.h, t_mm)
                        sy.dma_start(out=wbuf[:, :].rearrange("p (c n) -> p c n", c=NC_D),
                                     in_=wsrc[:, :, i * 128:(i + 1) * 128]
                                     .rearrange("c p n -> p c n")).then_inc(ld.h, 16)
                    blk.sync(ldf)
                    t_ld = ld.inc(16)

                    def tens(t, wbuf=wbuf, ps=ps, t_ld=t_ld,
                             t_ev=bT["ev"].get(k_ - 2, 0)):
                        t.wait_ge(ld.h, t_ld)
                        if t_ev:
                            t.wait_ge(ev.h, t_ev)
                        for c in range(NC_D):
                            for tb in range(4):
                                t.matmul(ps[:, tb, :], wbuf[:, c * 128:(c + 1) * 128],
                                         xln[:, c, tb * 512:(tb + 1) * 512],
                                         start=(c == 0), stop=(c == 7))
                        t.fence(mm.h)
                    blk.tensor(tens)
                    t_mm = mm.inc()
                    bT["mm"][k_] = t_mm

                    def sc(v, ps=ps, sbuf=sbuf, t_mm=t_mm,
                           t_sp=bT["sp"].get(k_ - 2, 0),
                           t_vx=bT["vx"].get(k_ - 2, 0)):
                        v.wait_ge(mm.h, t_mm)
                        if t_sp:
                            v.wait_ge(sp.h, t_sp)
                        if t_vx:
                            v.wait_ge(vx.h, t_vx)
                        v.tensor_copy(sbuf[:, 0:2048],
                                      ps[:, :, :].rearrange("p a b -> p (a b)")
                                      ).then_inc(ev.h, 1)
                    blk.vector(sc)
                    t_ev = ev.inc()
                    bT["ev"][k_] = t_ev

                    def vecr(v, i=i, sbuf=sbuf, mbase=mbase, t_ev=t_ev):
                        v.wait_ge(ev.h, t_ev)
                        v.tensor_reduce(mcol[:, mbase + i:mbase + i + 1],
                                        sbuf[:, 0:2048], AX, ALU.max,
                                        apply_absolute_value=True)
                        v.fence(vx.h)
                    blk.vector(vecr)
                    bT["vx"][k_] = vx.inc()

                    def gp(g, i=i, sbuf=sbuf, spdst=spdst, t_ev=t_ev):
                        g.wait_ge(ev.h, t_ev)
                        g.dma_start(out=spdst[i, :, :],
                                    in_=sbuf[:, 0:2048]).then_inc(sp.h, 16)
                    blk.gpsimd(gp)
                    bT["sp"][k_] = sp.inc(16)
                    step[0] += 1

                def vecf(v, mbase=mbase, which=which, t_vx=vx.n):
                    v.wait_ge(vx.h, t_vx)
                    col = 200 if which == "u" else 201
                    v.tensor_reduce(mcol[:, col:col + 1],
                                    mcol[:, mbase:mbase + NI], AX, ALU.max)
                    v.fence(vx.h)
                blk.vector(vecf)
                vx.inc()

            def scalars2(v):
                v.reciprocal(sc1[0:1, 340:341], sc1[0:1, 320:321])
                vrecip(v, sc1[0:1, 341:342], sc1[0:1, 321:322])
                v.tensor_scalar(sc1[0:1, 340:342], sc1[0:1, 340:342], QMAX, None,
                                ALU.mult)
                ss(v)
                v.tensor_scalar(sc1[0:1, 301:303], sc1[0:1, 320:322], 1.0 / QMAX,
                                None, ALU.mult)
                ss(v)
                v.tensor_tensor(sc1[0:1, 301:303], sc1[0:1, 301:303],
                                sc1[0:1, 300:301].to_broadcast([1, 2]), ALU.mult)
                ss(v)
                v.tensor_copy(sc1[0:1, 342:343], sc1[0:1, 302:303])
                ss(v)
                v.tensor_scalar(sc1[0:1, 344:345], sc1[0:1, 302:303], -1.0, None,
                                ALU.mult)
            allreduce_max(2, scalars2, 5)
            if ("upE_" + tag) in tap:
                def ldt(sy, t_mm=mm.n):
                    sy.dma_start(out=stg[:, 0:2048],
                                 in_=dr["up"][0, :, :]).then_inc(ld.h, 16)
                blk.sync(ldt)
                mktap("upE_" + tag, stg[:, 0:2048], [128, 2048],
                      wait=(ld, ld.inc(16)))
            if sub < 5:
                return

            sgb = pool[:, 0:2048]
            cT = {"vx": {}, "sp": {}}
            for i in range(NI):
                b0 = stg if i % 2 == 0 else stg2

                def ldf(sy, i=i, b0=b0, t_vx=cT["vx"].get(i - 2, 0),
                        t_sp=cT["sp"].get(i - 2, 0)):
                    if t_vx:
                        sy.wait_ge(vx.h, t_vx)
                    if t_sp:
                        sy.wait_ge(sp.h, t_sp)
                    sy.dma_start(out=b0[:, 0:2048], in_=dr["up"][i, :, :]).then_inc(ld.h, 16)
                    sy.dma_start(out=b0[:, 2048:4096], in_=dr["gate"][i, :, :]).then_inc(ld.h, 16)
                blk.sync(ldf)
                t_ld = ld.inc(32)

                def vec(v, b0=b0, t_ld=t_ld):
                    v.wait_ge(ld.h, t_ld)
                    v.tensor_scalar(b0[:, 0:2048], b0[:, 0:2048], scal[:, 0:1],
                                    MAGIC, ALU.mult, ALU.add)
                    ss(v)
                    v.tensor_scalar(b0[:, 0:2048], b0[:, 0:2048], MAGIC, None,
                                    ALU.subtract)
                    v.tensor_scalar(b0[:, 2048:4096], b0[:, 2048:4096], scal[:, 1:2],
                                    MAGIC, ALU.mult, ALU.add)
                    ss(v)
                    v.tensor_scalar(b0[:, 2048:4096], b0[:, 2048:4096], MAGIC, None,
                                    ALU.subtract)
                    v.fence(vx.h)
                blk.vector(vec)
                t_q = vx.inc()

                def gp(g, i=i, b0=b0, t_q=t_q):
                    g.wait_ge(vx.h, t_q)
                    g.dma_start(out=dr["qu"][i, :, :], in_=b0[:, 0:2048]).then_inc(sp.h, 16)
                    g.dma_start(out=dr["qg"][i, :, :], in_=b0[:, 2048:4096]).then_inc(sp.h, 16)
                blk.gpsimd(gp)
                cT["sp"][i] = sp.inc(32)

                def sc(s, b0=b0, t_q=t_q, t_vx2=vx.n):
                    s.wait_ge(vx.h, t_q)
                    s.activation(sgb, b0[:, 2048:4096], AF.Sigmoid,
                                 bias=cbias[:, 2:3], scale=scal[:, 2:3]).then_inc(ev.h, 1)
                blk.scalar(sc)
                t_sg = ev.inc()

                def vec2(v, i=i, b0=b0, t_sg=t_sg):
                    v.wait_ge(ev.h, t_sg)
                    v.tensor_scalar(sgb, sgb, 65536.0, MAGIC, ALU.mult, ALU.add)
                    ss(v)
                    v.tensor_scalar(sgb, sgb, MAGIC, None, ALU.subtract)
                    ss(v)
                    v.tensor_tensor(sgb, b0[:, 2048:4096], sgb, ALU.mult)
                    ss(v)
                    v.tensor_reduce(mcol[:, 72 + i:73 + i], sgb, AX, ALU.max,
                                    apply_absolute_value=True)
                    v.fence(vx.h)
                blk.vector(vec2)
                cT["vx"][i] = vx.inc()

            def vecf2(v, t_vx=vx.n):
                v.wait_ge(vx.h, t_vx)
                v.tensor_reduce(mcol[:, 200:201], mcol[:, 72:72 + NI], AX, ALU.max)
                v.fence(vx.h)
            blk.vector(vecf2)
            vx.inc()

            def scalars3(v):
                vrecip(v, sc1[0:1, 340:341], sc1[0:1, 320:321])
                v.tensor_scalar(sc1[0:1, 340:341], sc1[0:1, 340:341], QMAX, None,
                                ALU.mult)
                ss(v)
                v.tensor_scalar(sc1[0:1, 303:304], sc1[0:1, 320:321],
                                1.0 / 65536.0 / QMAX, None, ALU.mult)
                ss(v)
                v.tensor_tensor(sc1[0:1, 303:304], sc1[0:1, 303:304],
                                sc1[0:1, 302:303], ALU.mult)
                ss(v)
                v.tensor_copy(sc1[0:1, 341:342], sc1[0:1, 302:303])
                ss(v)
                v.tensor_scalar(sc1[0:1, 344:345], sc1[0:1, 302:303], -1.0, None,
                                ALU.mult)
            allreduce_max(1, scalars3, 5)
            if sub < 6:
                return

            dT = {}
            for i in range(NI):
                b0 = stg if i % 2 == 0 else stg2

                def ldf(sy, i=i, b0=b0, t_vx=dT.get(i - 2, 0)):
                    if t_vx:
                        sy.wait_ge(vx.h, t_vx)
                    sy.dma_start(out=b0[:, 0:2048], in_=dr["qu"][i, :, :]).then_inc(ld.h, 16)
                    sy.dma_start(out=b0[:, 2048:4096], in_=dr["qg"][i, :, :]).then_inc(ld.h, 16)
                blk.sync(ldf)
                t_ld = ld.inc(32)

                def sc(s, b0=b0, t_ld=t_ld, t_sp=sp.n, t_vx=vx.n):
                    s.wait_ge(ld.h, t_ld)
                    s.wait_ge(sp.h, t_sp)
                    s.wait_ge(vx.h, t_vx)
                    s.activation(sgb, b0[:, 2048:4096], AF.Sigmoid,
                                 bias=cbias[:, 2:3], scale=scal[:, 1:2]).then_inc(ev.h, 1)
                blk.scalar(sc)
                t_sg = ev.inc()

                def vec(v, i=i, b0=b0, t_sg=t_sg):
                    v.wait_ge(ev.h, t_sg)
                    v.tensor_scalar(sgb, sgb, 65536.0, MAGIC, ALU.mult, ALU.add)
                    v.tensor_scalar(sgb, sgb, MAGIC, None, ALU.subtract)
                    v.tensor_tensor(sgb, b0[:, 2048:4096], sgb, ALU.mult)
                    v.tensor_scalar(sgb, sgb, scal[:, 0:1], MAGIC, ALU.mult, ALU.add)
                    v.tensor_scalar(sgb, sgb, MAGIC, None, ALU.subtract)
                    v.tensor_tensor(b0[:, 2048:4096], sgb, b0[:, 0:2048], ALU.mult)
                    v.tensor_reduce(mcol[:, 104 + i:105 + i], b0[:, 2048:4096],
                                    AX, ALU.max, apply_absolute_value=True)
                    v.fence(vx.h)
                blk.vector(vec)
                t_q = vx.inc()
                dT[i] = t_q

                def gp(g, i=i, t_q=t_q):
                    g.wait_ge(vx.h, t_q)
                    g.dma_start(out=dr["qt"][i, :, :], in_=sgb).then_inc(sp.h, 16)
                blk.gpsimd(gp)
                sp.inc(16)

            def vecf3(v, t_vx=vx.n):
                v.wait_ge(vx.h, t_vx)
                v.tensor_reduce(mcol[:, 200:201], mcol[:, 104:104 + NI], AX, ALU.max)
                v.fence(vx.h)
            blk.vector(vecf3)
            vx.inc()

            def scalars4(v):
                vrecip(v, sc1[0:1, 340:341], sc1[0:1, 320:321])
                v.tensor_scalar(sc1[0:1, 340:341], sc1[0:1, 340:341], QMAX, None,
                                ALU.mult)
                ss(v)
                v.tensor_scalar(sc1[0:1, 304:305], sc1[0:1, 320:321], 1.0 / QMAX,
                                None, ALU.mult)
                ss(v)
                v.tensor_tensor(sc1[0:1, 304:305], sc1[0:1, 304:305],
                                sc1[0:1, 303:304], ALU.mult)
                ss(v)
                v.tensor_tensor(sc1[0:1, 304:305], sc1[0:1, 304:305],
                                sc1[0:1, 301:302], ALU.mult)
                ss(v)
                v.tensor_scalar(sc1[0:1, 341:342], sc1[0:1, 304:305], 0.5, None,
                                ALU.mult)
            allreduce_max(1, scalars4, 2)
            if sub < 7:
                return

            eT = {}
            for tb in range(4):
                for k in range(NI):
                    b0 = stg if k % 2 == 0 else stg2
                    ks_ = tb * NI + k

                    def ldf(sy, tb=tb, k=k, b0=b0, t_mm=eT.get(ks_ - 2, 0)):
                        if t_mm:
                            sy.wait_ge(mm.h, t_mm)
                        sy.dma_start(out=b0[:, 0:512],
                                     in_=dr["qt"][k, :, tb * 512:(tb + 1) * 512]).then_inc(ld.h, 16)
                        sy.dma_start(out=b0[:, 512:1024],
                                     in_=dr["qu"][k, :, tb * 512:(tb + 1) * 512]).then_inc(ld.h, 16)
                        sy.dma_start(out=b0[:, 1024:2048],
                                     in_=dr["wdT"][k, :, :]).then_inc(ld.h, 16)
                    blk.sync(ldf)
                    t_ld = ld.inc(48)

                    def vec(v, b0=b0, t_ld=t_ld):
                        v.wait_ge(ld.h, t_ld)
                        v.tensor_tensor(b0[:, 0:512], b0[:, 0:512], b0[:, 512:1024],
                                        ALU.mult)
                        v.tensor_scalar(b0[:, 0:512], b0[:, 0:512], scal[:, 0:1],
                                        MAGIC, ALU.mult, ALU.add)
                        v.tensor_scalar(b0[:, 0:512], b0[:, 0:512], MAGIC, None,
                                        ALU.subtract)
                        v.fence(vx.h)
                    blk.vector(vec)
                    t_oq = vx.inc()

                    def tens(t, tb=tb, k=k, b0=b0, t_oq=t_oq, t_vx=vx.n):
                        t.wait_ge(vx.h, t_oq)
                        if k == 0:
                            t.wait_ge(vx.h, t_vx)
                        for dti in range(8):
                            ps = psA if dti < 4 else psB
                            t.matmul(ps[:, dti % 4, :],
                                     b0[:, 1024 + dti * 128:1024 + (dti + 1) * 128],
                                     b0[:, 0:512], start=(k == 0), stop=(k == NI - 1))
                        t.fence(mm.h)
                    blk.tensor(tens)
                    eT[ks_] = mm.inc()

                def vecu(v, tb=tb, t_mm=mm.n):
                    v.wait_ge(mm.h, t_mm)
                    for dti in range(8):
                        ps = psA if dti < 4 else psB
                        v.scalar_tensor_tensor(
                            xres[:, dti, tb * 512:(tb + 1) * 512],
                            ps[:, dti % 4, :], scal[:, 1:2],
                            xres[:, dti, tb * 512:(tb + 1) * 512],
                            ALU.mult, ALU.add)
                    v.fence(vx.h)
                blk.vector(vecu)
                vx.inc()
            barrier()

            for nm in ("up", "gate", "qu", "qg", "qt", "up2"):
                tnm = nm + "0_" + tag
                if tnm in tap:
                    src = "up" if nm == "up2" else nm
                    def ldt(sy, src=src, t_mm=mm.n):
                        sy.dma_start(out=stg[:, 0:2048],
                                     in_=dr[src][0, :, :]).then_inc(ld.h, 16)
                    blk.sync(ldt)
                    mktap(tnm, stg[:, 0:2048], [128, 2048], wait=(ld, ld.inc(16)))

        def attention():
            layer_norm("a")
            def gp(g, t_vx=vx.n):
                g.wait_ge(vx.h, t_vx)
                g.dma_start(out=dr["ag_in"].rearrange("c p t -> p c t"),
                            in_=xln[:, :, :]).then_inc(sp.h, 16)
                g.wait_ge(sp.h, sp.inc(16))
                g.collective_compute(
                    "AllGather", ALU.bypass,
                    replica_groups=[[0, 1], [2, 3], [4, 5], [6, 7]],
                    ins=[dr["ag_in"].rearrange("c p t -> (c p t)").opt()],
                    outs=[dr["ag_out"].rearrange("h c p t -> (h c p t)").opt()],
                ).then_inc(cc.h, 1)
                g.wait_ge(cc.h, cc.inc())
                g.sem_inc(gx.h, 1)
            blk.gpsimd(gp)
            gx.inc()
            barrier()

            def gpi(g):
                g.iota(ropeC[:, :, 0:1], [[128, 32], [0, 1]], base=0,
                       channel_multiplier=1, allow_small_or_imprecise_dtypes=True)
                g.fence(gx.h)
            blk.gpsimd(gpi)
            gx.inc()

            def veci(v, t_gx=gx.n):
                v.wait_ge(gx.h, t_gx)
                v.memset(sc1[0:1, 256:257], 1.0)
                ss(v)
                for i in range(1, 32):
                    v.tensor_scalar(sc1[0:1, 256 + i:257 + i], sc1[0:1, 255 + i:256 + i],
                                    RINV, None, ALU.mult)
                    ss(v)
                v.tensor_copy(sc1[0:1, 379:380], sc1[0:1, 256:257])
                v.fence(vx.h)
            blk.vector(veci)
            vx.inc()

            def gpb(g, t_vx=vx.n):
                g.wait_ge(vx.h, t_vx)
                g.dma_start(out=dr["br"][0:16] if False else dr["ar_in"][0:32],
                            in_=sc1[0:1, 256:288]).then_inc(sp.h, 16)
                g.wait_ge(sp.h, sp.inc(16))
                g.dma_start(out=stg[:, 0:32],
                            in_=dr["ar_in"][0:32].unsqueeze(0).to_broadcast([128, 32])).then_inc(sp.h, 16)
                g.wait_ge(sp.h, sp.inc(16))
                g.sem_inc(gx.h, 1)
            blk.gpsimd(gpb)
            gx.inc()

            def vecr(v, t_gx=gx.n):
                v.wait_ge(gx.h, t_gx)
                for j in range(32):
                    v.tensor_tensor(ropeS[:, j, :],
                                    ropeC[:, j, 0:1].to_broadcast([128, 32]),
                                    stg[:, 0:32], ALU.mult)
                ss(v)
                v.tensor_scalar(ropeC[:, :, :], ropeS[:, :, :], INV_2PI, MAGIC,
                                ALU.mult, ALU.add)
                ss(v)
                v.tensor_scalar(ropeC[:, :, :], ropeC[:, :, :], MAGIC, None,
                                ALU.subtract)
                ss(v)
                v.scalar_tensor_tensor(ropeS[:, :, :], ropeC[:, :, :], -CW1,
                                       ropeS[:, :, :], ALU.mult, ALU.add)
                ss(v)
                v.scalar_tensor_tensor(ropeS[:, :, :], ropeC[:, :, :], -CW2,
                                       ropeS[:, :, :], ALU.mult, ALU.add)
                ss(v)
                v.tensor_scalar(ropeS[:, :, :], ropeS[:, :, :], PI_F, -PI_F,
                                ALU.min, ALU.max)
                v.fence(vx.h)
            blk.vector(vecr)
            vx.inc()

            def scs(s, t_vx=vx.n):
                s.wait_ge(vx.h, t_vx)
                s.activation(stg2[:, 0:1024].rearrange("p (a b) -> p a b", b=32),
                             ropeS[:, :, :], AF.Sin, bias=cbias[:, 2:3],
                             scale=1.0).then_inc(ev.h, 1)
            blk.scalar(scs)
            ev.inc()

            def vecc(v, t_ev=ev.n):
                v.wait_ge(ev.h, t_ev)
                v.tensor_scalar(ropeS[:, :, :], ropeS[:, :, :], HALF_PI, None,
                                ALU.add)
                ss(v)
                v.tensor_scalar(stg[:, 32:1056].rearrange("p (a b) -> p a b", b=32),
                                ropeS[:, :, :], PI_F, TWO_PI, ALU.is_gt, ALU.mult)
                ss(v)
                v.tensor_tensor(ropeS[:, :, :], ropeS[:, :, :],
                                stg[:, 32:1056].rearrange("p (a b) -> p a b", b=32),
                                ALU.subtract)
                ss(v)
                v.tensor_scalar(ropeS[:, :, :], ropeS[:, :, :], PI_F, -PI_F,
                                ALU.min, ALU.max)
                v.fence(vx.h)
            blk.vector(vecc)
            vx.inc()

            def scc(s, t_vx=vx.n):
                s.wait_ge(vx.h, t_vx)
                s.activation(ropeC[:, :, :], ropeS[:, :, :], AF.Sin,
                             bias=cbias[:, 2:3], scale=1.0).then_inc(ev.h, 1)
            blk.scalar(scc)
            ev.inc()

            def vecc2(v, t_ev=ev.n):
                v.wait_ge(ev.h, t_ev)
                v.tensor_copy(ropeS[:, :, :],
                              stg2[:, 0:1024].rearrange("p (a b) -> p a b", b=32))
                v.fence(vx.h)
            blk.vector(vecc2)
            vx.inc()
            barrier()

            def ldw(sy):
                sy.dma_start(out=kfeat.rearrange("p (c n) -> p c n", c=NC_D),
                             in_=dr["wqT"].rearrange("c p n -> p c n")).then_inc(ld.h, 16)
                sy.dma_start(out=qfeat.rearrange("p (c n) -> p c n", c=NC_D),
                             in_=dr["wkT"].rearrange("c p n -> p c n")).then_inc(ld.h, 16)
                sy.dma_start(out=vbufF[:, 0:4096].rearrange("p (c n) -> p c n", c=NC_D),
                             in_=dr["wvT"].rearrange("c p n -> p c n")).then_inc(ld.h, 16)
            blk.sync(ldw)
            t_w = ld.inc(48)

            qT = {"mm": {}, "ev": {}}
            for tt in range(32):
                xa = stg if tt % 2 == 0 else stg2
                pq = psA if tt % 2 == 0 else psB
                half, lt = tt // 16, tt % 16

                def ldf(sy, xa=xa, half=half, lt=lt, t_mm=mm.n):
                    sy.wait_ge(mm.h, t_mm)
                    sy.dma_start(out=xa[:, 0:1024].rearrange("p (c t) -> p c t", c=NC_D),
                                 in_=dr["ag_out"][half, :, :, lt * 128:(lt + 1) * 128]
                                 .rearrange("c p t -> p c t")).then_inc(ld.h, 16)
                blk.sync(ldf)
                t_ld = ld.inc(16)

                def tens(t, xa=xa, pq=pq, t_ld=t_ld, t_w=t_w, t_ev=ev.n):
                    t.wait_ge(ld.h, max(t_ld, t_w))
                    t.wait_ge(ev.h, t_ev)
                    for c in range(NC_D):
                        t.matmul(pq[:, 0, :], xa[:, c * 128:(c + 1) * 128],
                                 kfeat[:, c * 512:(c + 1) * 512],
                                 start=(c == 0), stop=(c == 7))
                        t.matmul(pq[:, 1, :], xa[:, c * 128:(c + 1) * 128],
                                 qfeat[:, c * 512:(c + 1) * 512],
                                 start=(c == 0), stop=(c == 7))
                        t.matmul(pq[:, 2, :], xa[:, c * 128:(c + 1) * 128],
                                 vbufF[:, c * 512:(c + 1) * 512],
                                 start=(c == 0), stop=(c == 7))
                    t.fence(mm.h)
                blk.tensor(tens)
                t_mm = mm.inc()
                qT["mm"][tt] = t_mm

                def sc(v, pq=pq, t_mm=t_mm, t_sp=sp.n):
                    v.wait_ge(mm.h, t_mm)
                    v.wait_ge(sp.h, t_sp)
                    v.tensor_copy(ptile[:, 0, :], pq[:, 0, :])
                    v.tensor_copy(ptile[:, 1, :], pq[:, 1, :])
                    v.tensor_copy(ptile[:, 2, :], pq[:, 2, :]).then_inc(ev.h, 1)
                blk.vector(sc)
                t_ev = ev.inc()
                qT["ev"][tt] = t_ev

                def vec(v, tt=tt, t_ev=t_ev):
                    v.wait_ge(ev.h, t_ev)
                    co = ropeC[:, tt, :]
                    si = ropeS[:, tt, :]
                    for which in (0, 1):
                        q4 = ptile[:, which, :].rearrange("p (h f two) -> p h f two",
                                                          h=8, two=2)
                        r4 = ptile[:, 3, :].rearrange("p (h f two) -> p h f two",
                                                      h=8, two=2)
                        for h in range(8):
                            v.scalar_tensor_tensor(r4[:, h, :, 0], q4[:, h, :, 1],
                                                   -1.0, si, ALU.mult, ALU.mult)
                            v.tensor_tensor(r4[:, h, :, 1], q4[:, h, :, 0], si,
                                            ALU.mult)
                            ss(v)
                            v.tensor_tensor(q4[:, h, :, :], q4[:, h, :, :],
                                            co.unsqueeze(2).to_broadcast([128, 32, 2]),
                                            ALU.mult)
                        ss(v)
                        v.tensor_tensor(ptile[:, which, :], ptile[:, which, :],
                                        ptile[:, 3, :], ALU.add)
                        if which == 0:
                            ss(v)
                    v.fence(vx.h)
                blk.vector(vec)
                t_vx = vx.inc()

                def gp2(g, tt=tt, t_vx=t_vx):
                    g.wait_ge(vx.h, t_vx)
                    g.dma_start(out=dr["qsp"][tt, :, :], in_=ptile[:, 0, :]).then_inc(sp.h, 16)
                    g.dma_start(out=dr["ksp"][tt, :, :], in_=ptile[:, 1, :]).then_inc(sp.h, 16)
                    g.dma_start(out=dr["vsp"][tt, :, :], in_=ptile[:, 2, :]).then_inc(sp.h, 16)
                blk.gpsimd(gp2)
                sp.inc(48)
            barrier()

            def vecb(v):
                v.memset(bmask[:, :], 1.0)
                v.memset(vbuf[:, :, 64:65], 1.0)
                v.memset(vbuf[:, :, 129:130], 1.0)
                v.fence(vx.h)
            blk.vector(vecb)
            vx.inc()

            def gpm(g, t_vx=vx.n):
                g.wait_ge(vx.h, t_vx)
                g.affine_select(bmask[:, :], bmask[:, :], [[1, 896]], ALU.is_ge,
                                0.0, base=-384, channel_multiplier=-1)
                g.fence(gx.h)
            blk.gpsimd(gpm)
            gx.inc()
            barrier()

            for pair in range(4):
                for tt in range(32):
                    def ldf(sy, tt=tt, pair=pair, t_mm=mm.n, t_vx=vx.n):
                        sy.wait_ge(mm.h, t_mm)
                        sy.wait_ge(vx.h, t_vx)
                        sy.dma_start(out=stg[:, (tt % 2) * 128:(tt % 2) * 128 + 128],
                                     in_=dr["ksp"][tt, :, pair * 128:(pair + 1) * 128]
                                     ).then_inc(ld.h, 16)
                        sy.dma_start(out=stg[:, 256 + (tt % 2) * 128:256 + (tt % 2) * 128 + 128],
                                     in_=dr["qsp"][tt, :, pair * 128:(pair + 1) * 128]
                                     ).then_inc(ld.h, 16)
                        sy.dma_start(out=stg2[:, (tt % 4) * 128:(tt % 4) * 128 + 128],
                                     in_=dr["vsp"][tt, :, pair * 128:(pair + 1) * 128]
                                     ).then_inc(ld.h, 16)
                    blk.sync(ldf)
                    t_ld = ld.inc(48)

                    def tens(t, tt=tt, t_ld=t_ld, t_ev=ev.n):
                        t.wait_ge(ld.h, t_ld)
                        t.wait_ge(ev.h, t_ev)
                        t.transpose(psB[:, tt % 2, 0:128],
                                    stg[:, (tt % 2) * 128:(tt % 2) * 128 + 128],
                                    ident[:, :])
                        t.transpose(psB[:, 2 + tt % 2, 0:128],
                                    stg[:, 256 + (tt % 2) * 128:256 + (tt % 2) * 128 + 128],
                                    ident[:, :]).then_inc(mm.h, 1)
                    blk.tensor(tens)
                    t_mm = mm.inc()

                    def sc(v, tt=tt, t_mm=t_mm):
                        v.wait_ge(mm.h, t_mm)
                        v.tensor_copy(kfeat[:, tt * 128:(tt + 1) * 128],
                                      psB[:, tt % 2, 0:128])
                        v.tensor_copy(qfeat[:, tt * 128:(tt + 1) * 128],
                                      psB[:, 2 + tt % 2, 0:128]).then_inc(ev.h, 1)
                    blk.vector(sc)
                    ev.inc()

                    def vecv(v, tt=tt, t_ld=t_ld):
                        v.wait_ge(ld.h, t_ld)
                        v.tensor_copy(vbuf[:, tt, 0:64],
                                      stg2[:, (tt % 4) * 128:(tt % 4) * 128 + 64])
                        v.tensor_copy(vbuf[:, tt, 65:129],
                                      stg2[:, (tt % 4) * 128 + 64:(tt % 4) * 128 + 128])
                        v.fence(vx.h)
                    blk.vector(vecv)
                    vx.inc()
                barrier()

                prev_last_exp = 0
                prev_sc_o = 0
                for hh in range(2):
                    hloc = pair * 2 + hh
                    for qb in range(8):
                        nk = 4 * qb + 4
                        exp_tgt = {}
                        pv_tgt = {}

                        def emit_scores(j, hh=hh, qb=qb):
                            def tens_j(t, j=j, hh=hh, qb=qb,
                                       w_ev=exp_tgt.get(j - 2),
                                       w_prev=prev_last_exp):
                                if w_ev is not None:
                                    t.wait_ge(ev.h, w_ev)
                                elif w_prev:
                                    t.wait_ge(ev.h, w_prev)
                                t.matmul(psB[:, j % 2, :],
                                         kfeat[hh * 64:(hh + 1) * 64,
                                               j * 128:(j + 1) * 128],
                                         qfeat[hh * 64:(hh + 1) * 64,
                                               qb * 512:(qb + 1) * 512],
                                         start=True, stop=True).then_inc(mm.h, 1)
                            blk.tensor(tens_j)
                            t_mm = mm.inc()

                            def sc_j(s, j=j, t_mm=t_mm, w_pv=pv_tgt.get(j - 4)):
                                s.wait_ge(mm.h, t_mm)
                                if w_pv is not None:
                                    s.wait_ge(mm.h, w_pv)
                                s.activation(ptile[:, j % 4, :], psB[:, j % 2, :],
                                             AF.Exp, bias=cbias[:, 0:1],
                                             scale=0.125).then_inc(ev.h, 1)
                            blk.scalar(sc_j)
                            exp_tgt[j] = ev.inc()

                            if j >= 4 * qb:
                                def vec_j(v, j=j, qb=qb, t_exp=exp_tgt[j]):
                                    v.wait_ge(ev.h, t_exp)
                                    off = 384 + qb * 512 - j * 128
                                    v.tensor_tensor(ptile[:, j % 4, :],
                                                    ptile[:, j % 4, :],
                                                    bmask[:, off:off + 512], ALU.mult)
                                    v.fence(vx.h)
                                blk.vector(vec_j)
                                return ("vx", vx.inc())
                            return ("ev", exp_tgt[j])

                        def emit_pv(j, rdy, hh=hh, qb=qb, nk=nk):
                            def tens_pv(t, j=j, hh=hh, rdy=rdy, nk=nk,
                                        w_evac=prev_sc_o):
                                t.wait_ge({"vx": vx, "ev": ev}[rdy[0]].h, rdy[1])
                                if j == 0 and w_evac:
                                    t.wait_ge(ev.h, w_evac)
                                t.matmul(psA[0:65, 0, :],
                                         vbuf[:, j, hh * 65:(hh + 1) * 65],
                                         ptile[:, j % 4, :],
                                         start=(j == 0),
                                         stop=(j == nk - 1)).then_inc(mm.h, 1)
                            blk.tensor(tens_pv)
                            pv_tgt[j] = mm.inc()

                        rdys = {}
                        if FLASH_PIPE:
                            for j in range(nk):
                                rdys[j] = emit_scores(j)
                                if j >= 1:
                                    emit_pv(j - 1, rdys[j - 1])
                            emit_pv(nk - 1, rdys[nk - 1])
                        else:
                            for j in range(nk):
                                rdys[j] = emit_scores(j)
                                emit_pv(j, rdys[j])
                        prev_last_exp = exp_tgt[nk - 1]

                        def vec_e(v, t_mm=pv_tgt[nk - 1], t_sp=sp.n):
                            v.wait_ge(mm.h, t_mm)
                            v.wait_ge(sp.h, t_sp)
                            v.tensor_copy(orow[0:65, 0:512],
                                          psA[0:65, 0, :]).then_inc(ev.h, 1)
                        blk.vector(vec_e)
                        t_pe = ev.inc()
                        prev_sc_o = t_pe

                        sub_tgts = []
                        for sub_ in range(4):
                            def tr_s(t, sub_=sub_, t_pe=t_pe,
                                     w_vx=(sub_tgts[sub_ - 2] if sub_ >= 2 else 0)):
                                if sub_ == 0:
                                    t.wait_ge(ev.h, t_pe)
                                if w_vx:
                                    t.wait_ge(vx.h, w_vx)
                                t.transpose(psA[:, 1 + sub_ % 2, 0:65],
                                            orow[0:65, sub_ * 128:(sub_ + 1) * 128],
                                            ident[0:65, 0:65]).then_inc(mm.h, 1)
                            blk.tensor(tr_s)
                            t_tr = mm.inc()

                            def vec_s(v, sub_=sub_, t_tr=t_tr):
                                v.wait_ge(mm.h, t_tr)
                                v.reciprocal(scal[:, 3 + sub_:4 + sub_],
                                             psA[:, 1 + sub_ % 2, 64:65]).then_inc(sw.h, 1)
                                v.wait_ge(sw.h, sw.inc())
                                v.tensor_scalar(orow[:, 512 + sub_ * 64:512 + (sub_ + 1) * 64],
                                                psA[:, 1 + sub_ % 2, 0:64],
                                                scal[:, 3 + sub_:4 + sub_], None,
                                                ALU.mult)
                                v.fence(vx.h)
                            blk.vector(vec_s)
                            sub_tgts.append(vx.inc())

                        def gp_o(g, hloc=hloc, qb=qb, t_eo=sub_tgts[-1]):
                            g.wait_ge(vx.h, t_eo)
                            g.dma_start(
                                out=dr["amine"][qb * 4:(qb + 1) * 4, :,
                                                hloc * 64:(hloc + 1) * 64]
                                .rearrange("s p d -> p s d"),
                                in_=orow[:, 512:768].rearrange("p (s d) -> p s d", d=64),
                            ).then_inc(sp.h, 16)
                        blk.gpsimd(gp_o)
                        sp.inc(16)
                barrier()

            def ldwo(sy, t_sp=sp.n):
                sy.wait_ge(sp.h, t_sp)
                sy.dma_start(out=wta[:, :].rearrange("p (c n) -> p c n", c=1),
                             in_=dr["woT"][0:1, :, :].rearrange("c p n -> p c n")
                             ).then_inc(ld.h, 16)
                sy.dma_start(out=wtb[:, :].rearrange("p (c n) -> p c n", c=1),
                             in_=dr["woT"][1:2, :, :].rearrange("c p n -> p c n")
                             ).then_inc(ld.h, 16)
                sy.dma_start(out=orow[:, :].rearrange("p (c n) -> p c n", c=1),
                             in_=dr["woT"][2:3, :, :].rearrange("c p n -> p c n")
                             ).then_inc(ld.h, 16)
                sy.dma_start(out=bmask[:, 0:896],
                             in_=dr["woT"][3, :, 0:896]).then_inc(ld.h, 16)
                sy.dma_start(out=ptile[:, 0, 0:128],
                             in_=dr["woT"][3, :, 896:1024]).then_inc(ld.h, 16)
            blk.sync(ldwo)
            t_wo = ld.inc(80)
            wchunk = [wta[:, :], wtb[:, :], orow[:, :], None]

            for tt in range(32):
                buf = stg if tt % 2 == 0 else stg2

                def ldf(sy, tt=tt, buf=buf, t_mm=mm.n):
                    sy.wait_ge(mm.h, t_mm)
                    sy.dma_start(out=buf[:, 0:512],
                                 in_=dr["amine"][tt, :, :]).then_inc(ld.h, 16)
                blk.sync(ldf)
                t_ld = ld.inc(16)

                for c in range(4):
                    def tens(t, tt=tt, buf=buf, c=c, t_ld=t_ld, t_ev=ev.n):
                        if c == 0:
                            t.wait_ge(ld.h, t_ld)
                        t.wait_ge(ev.h, t_ev)
                        t.transpose(psB[:, c % 2, 0:128],
                                    buf[:, c * 128:(c + 1) * 128],
                                    ident[:, :]).then_inc(mm.h, 1)
                    blk.tensor(tens)
                    t_mm = mm.inc()

                    def sc(v, buf=buf, c=c, t_mm=t_mm):
                        v.wait_ge(mm.h, t_mm)
                        v.tensor_copy(buf[:, 512 + c * 128:512 + (c + 1) * 128],
                                      psB[:, c % 2, 0:128]).then_inc(ev.h, 1)
                    blk.vector(sc)
                    ev.inc()

                def tens2(t, tt=tt, buf=buf, t_wo=t_wo, t_evp=ev.n, t_vx=vx.n):
                    t.wait_ge(ev.h, t_evp)
                    t.wait_ge(ld.h, t_wo)
                    t.wait_ge(vx.h, t_vx)
                    for c in range(4):
                        w = wchunk[c]
                        for os_ in range(2):
                            if c < 3:
                                rhs = w[:, os_ * 512:(os_ + 1) * 512]
                                t.matmul(psA[:, os_, :],
                                         buf[:, 512 + c * 128:512 + (c + 1) * 128],
                                         rhs, start=(c == 0), stop=(c == 3))
                            else:
                                if os_ == 0:
                                    t.matmul(psA[:, 0, :],
                                             buf[:, 512 + c * 128:512 + (c + 1) * 128],
                                             bmask[:, 0:512], start=False, stop=True)
                                else:
                                    t.matmul(psA[:, 1, 0:384],
                                             buf[:, 512 + c * 128:512 + (c + 1) * 128],
                                             bmask[:, 512:896], start=False, stop=False)
                                    t.matmul(psA[:, 1, 384:512],
                                             buf[:, 512 + c * 128:512 + (c + 1) * 128],
                                             ptile[:, 0, 0:128], start=False, stop=True)
                    t.fence(mm.h)
                blk.tensor(tens2)
                t_mm2 = mm.inc()

                def vec2(v, tt=tt, buf=buf, t_mm2=t_mm2, t_sp=sp.n):
                    v.wait_ge(mm.h, t_mm2)
                    v.wait_ge(sp.h, t_sp)
                    v.tensor_copy(buf[:, 1024:1536], psA[:, 0, :])
                    v.tensor_copy(buf[:, 1536:2048], psA[:, 1, :])
                    v.fence(vx.h)
                blk.vector(vec2)
                t_vy = vx.inc()

                def gp_y(g, tt=tt, buf=buf, t_vy=t_vy):
                    g.wait_ge(vx.h, t_vy)
                    g.dma_start(out=dr["rs_in"][tt * 128:(tt + 1) * 128, :],
                                in_=buf[:, 1024:2048]).then_inc(sp.h, 16)
                blk.gpsimd(gp_y)
                sp.inc(16)

            def gpr(g, t_sp=sp.n):
                g.wait_ge(sp.h, t_sp)
                g.collective_compute(
                    "ReduceScatter", ALU.add,
                    replica_groups=[[0, 1], [2, 3], [4, 5], [6, 7]],
                    ins=[dr["rs_in"].rearrange("t d -> (t d)").opt()],
                    outs=[dr["rs_out"].rearrange("t d -> (t d)").opt()],
                ).then_inc(cc.h, 1)
                g.wait_ge(cc.h, cc.inc())
                g.sem_inc(gx.h, 1)
            blk.gpsimd(gpr)
            gx.inc()
            barrier()

            for tt in range(16):
                buf = stg if tt % 2 == 0 else stg2

                def ldf2(sy, tt=tt, buf=buf, t_mm=mm.n):
                    sy.wait_ge(mm.h, t_mm)
                    sy.dma_start(out=buf[:, 0:1024],
                                 in_=dr["rs_out"][tt * 128:(tt + 1) * 128, :]
                                 ).then_inc(ld.h, 16)
                blk.sync(ldf2)
                t_ld = ld.inc(16)

                for c in range(NC_D):
                    def tens(t, buf=buf, c=c, t_ld=t_ld, t_vx=vx.n):
                        if c == 0:
                            t.wait_ge(ld.h, t_ld)
                        t.wait_ge(vx.h, t_vx)
                        t.transpose(psB[:, c % 4, 0:128],
                                    buf[:, c * 128:(c + 1) * 128],
                                    ident[:, :]).then_inc(mm.h, 1)
                    blk.tensor(tens)
                    t_mm = mm.inc()

                    def vec(v, tt=tt, c=c, t_mm=t_mm):
                        v.wait_ge(mm.h, t_mm)
                        v.scalar_tensor_tensor(xres[:, c, tt * 128:(tt + 1) * 128],
                                               psB[:, c % 4, 0:128], 1.0,
                                               xres[:, c, tt * 128:(tt + 1) * 128],
                                               ALU.mult, ALU.add)
                        v.fence(vx.h)
                    blk.vector(vec)
                    vx.inc()
            barrier()

        def store_out():
            for tt in range(16):
                for c in range(NC_D):
                    def tens(t, tt=tt, c=c, t_ev=ev.n):
                        t.wait_ge(ev.h, t_ev)
                        t.transpose(psB[:, c % 4, 0:128],
                                    xres[:, c, tt * 128:(tt + 1) * 128],
                                    ident[:, :]).then_inc(mm.h, 1)
                    blk.tensor(tens)
                    t_mm = mm.inc()

                    def sc(v, tt=tt, c=c, t_mm=t_mm, t_sp=sp.n):
                        v.wait_ge(mm.h, t_mm)
                        v.wait_ge(sp.h, t_sp)
                        v.tensor_copy(stg[:, c * 128:(c + 1) * 128],
                                      psB[:, c % 4, 0:128]).then_inc(ev.h, 1)
                    blk.vector(sc)
                    ev.inc()

                def gp(g, tt=tt, t_ev=ev.n):
                    g.wait_ge(ev.h, t_ev)
                    g.dma_start(out=out[tt * 128:(tt + 1) * 128, :],
                                in_=stg[:, 0:1024]).then_inc(sp.h, 16)
                blk.gpsimd(gp)
                sp.inc(16)

        p0_consts()
        p0_input()
        if stage >= 1:
            p0_weights()
        barrier()
        if "wuT0" in tap:
            def ldt(sy, t_mm=mm.n):
                sy.dma_start(out=stg[:, 0:3072],
                             in_=dr["wuT"][0, :, :]).then_inc(ld.h, 16)
            blk.sync(ldt)
            mktap("wuT0", stg[:, 0:3072], [128, 3072], wait=(ld, ld.inc(16)))
        if stage >= 2:
            ffn_call("f1")
        if stage >= 3:
            attention()
        if stage >= 4:
            ffn_call("f2")
        store_out()

        def fin(g, t_sp=sp.n):
            g.wait_ge(sp.h, t_sp)
        blk.gpsimd(fin)

    return nc, tap_outs


def kernel(**inputs):
    x = np.ascontiguousarray(np.asarray(inputs["L0_attn_x"], np.float32))
    wq = np.asarray(inputs["L0_attn_wq"], np.float32)
    wk = np.asarray(inputs["L0_attn_wk"], np.float32)
    wv = np.asarray(inputs["L0_attn_wv"], np.float32)
    wo = np.ascontiguousarray(np.asarray(inputs["L0_attn_wo"], np.float32))
    wg = np.ascontiguousarray(np.asarray(inputs["L0_ffn_w_gate"], np.float32))
    wu = np.ascontiguousarray(np.asarray(inputs["L0_ffn_w_up"], np.float32))
    wd = np.ascontiguousarray(np.asarray(inputs["L0_ffn_w_down"], np.float32))

    nc, _ = build_program(stage=STAGE)
    in_maps = []
    for core in range(8):
        b, h = core // 2, core % 2
        in_maps.append({
            "x": np.ascontiguousarray(x[b, h * T:(h + 1) * T, :]),
            "wq": np.ascontiguousarray(wq[h * 512:(h + 1) * 512, :]),
            "wk": np.ascontiguousarray(wk[h * 512:(h + 1) * 512, :]),
            "wv": np.ascontiguousarray(wv[h * 512:(h + 1) * 512, :]),
            "wo": np.ascontiguousarray(wo[:, h * 512:(h + 1) * 512]),
            "wg": wg, "wu": wu, "wd": wd,
        })
    res = run_bass_kernel_spmd(nc, in_maps, core_ids=list(range(8)))
    outp = np.empty((4, S, D), np.float32)
    for core in range(8):
        b, h = core // 2, core % 2
        outp[b, h * T:(h + 1) * T, :] = res.results[core]["out"]
    return outp


# version 11 (final)
# speedup vs baseline: 1.1340x; 1.0352x over previous
import numpy as np
from contextlib import ExitStack

import concourse.bass as bass
import concourse.mybir as mybir
from concourse.bass_utils import run_bass_kernel_spmd

F32 = mybir.dt.float32
F16 = mybir.dt.float16
AF = mybir.ActivationFunctionType
ALU = mybir.AluOpType
AX = mybir.AxisListType.X
AXY = mybir.AxisListType.XY

MAGIC = 12582912.0
QMAX = 32767.0
C_EXP = 14.0
EPS_LN = 1e-5
LN10K = 9.210340371976184
TWO_PI = 6.283185307179586
CW1 = 6.28125
CW2 = float(np.float32(TWO_PI - 6.28125))
INV_2PI = 0.15915494309189535
HALF_PI = float(np.float32(np.pi / 2))
PI_F = float(np.float32(np.pi))
R7A = float(np.float32(1.0 / 7.0))
R7B = float(np.float32(1.0 / 7.0 - float(np.float32(1.0 / 7.0))))
RINV = float(np.float32(np.exp(-LN10K / 32.0)))

T = 2048
D = 1024
NC_D = 8
I3 = 3072
NI = 24
S = 4096
STAGE = 4
FFN_SUB = 7
FLASH_PIPE = True


class Sem:
    def __init__(self, ctx, nc, name):
        self.h = ctx.enter_context(nc.semaphore(name))
        self.n = 0

    def inc(self, k=1):
        self.n += k
        return self.n


_SEQ_ONLY = {"wait_ge", "wait_op", "sem_inc", "nop", "maybe_wait_ge"}


class EW:

    def __init__(self, eng):
        self._e = eng
        self._last = None

    def __getattr__(self, k):
        a = getattr(self._e, k)
        if callable(a) and k not in _SEQ_ONLY and not k.startswith("_"):
            def w(*args, **kw):
                r = a(*args, **kw)
                if k not in _SEQ_ONLY:
                    self._last = r
                return r
            return w
        return a

    def wait_ge(self, sem, val):
        return self._e.wait_ge(sem, val)

    def sem_inc(self, sem, val):
        return self._e.sem_inc(sem, val)

    def fence(self, sem, n=1):
        assert self._last is not None, "fence with no prior datapath op"
        self._last.then_inc(sem, n)


def build_program(stage=STAGE, taps=()):
    nc = bass.Bass("TRN2", target_bir_lowering=False,
                   detect_race_conditions=False)
    tap = set(taps)

    x_in = nc.dram_tensor("x", [T, D], F32, kind="ExternalInput").ap()
    wq_in = nc.dram_tensor("wq", [512, D], F32, kind="ExternalInput").ap()
    wk_in = nc.dram_tensor("wk", [512, D], F32, kind="ExternalInput").ap()
    wv_in = nc.dram_tensor("wv", [512, D], F32, kind="ExternalInput").ap()
    wo_in = nc.dram_tensor("wo", [D, 512], F32, kind="ExternalInput").ap()
    wg_in = nc.dram_tensor("wg", [I3, D], F32, kind="ExternalInput").ap()
    wu_in = nc.dram_tensor("wu", [I3, D], F32, kind="ExternalInput").ap()
    wd_in = nc.dram_tensor("wd", [D, I3], F32, kind="ExternalInput").ap()
    out = nc.dram_tensor("out", [T, D], F32, kind="ExternalOutput").ap()

    tap_outs = {}

    dr = {}
    dr["wqT"] = nc.dram_tensor("wqT", [NC_D, 128, 512], F32).ap()
    dr["wkT"] = nc.dram_tensor("wkT", [NC_D, 128, 512], F32).ap()
    dr["wvT"] = nc.dram_tensor("wvT", [NC_D, 128, 512], F32).ap()
    dr["woT"] = nc.dram_tensor("woT", [4, 128, D], F32).ap()
    dr["wgT"] = nc.dram_tensor("wgT", [NC_D, 128, I3], F32).ap()
    dr["wuT"] = nc.dram_tensor("wuT", [NC_D, 128, I3], F32).ap()
    dr["wdT"] = nc.dram_tensor("wdT", [NI, 128, D], F32).ap()
    dr["up"] = nc.dram_tensor("up_sp", [NI, 128, T], F32).ap()
    dr["gate"] = nc.dram_tensor("gate_sp", [NI, 128, T], F32).ap()
    dr["qu"] = nc.dram_tensor("qu_sp", [NI, 128, T], F32).ap()
    dr["qg"] = nc.dram_tensor("qg_sp", [NI, 128, T], F32).ap()
    dr["qt"] = nc.dram_tensor("qt_sp", [NI, 128, T], F32).ap()
    dr["ag_in"] = nc.dram_tensor("ag_in", [NC_D, 128, T], F32).ap()
    dr["ag_out"] = nc.dram_tensor("ag_out", [2, NC_D, 128, T], F32).ap()
    dr["qsp"] = nc.dram_tensor("q_sp", [32, 128, 512], F32).ap()
    dr["ksp"] = nc.dram_tensor("k_sp", [32, 128, 512], F32).ap()
    dr["vsp"] = nc.dram_tensor("v_sp", [32, 128, 512], F32).ap()
    dr["amine"] = nc.dram_tensor("amine", [32, 128, 512], F32).ap()
    dr["rs_in"] = nc.dram_tensor("rs_in", [S, D], F32).ap()
    dr["rs_out"] = nc.dram_tensor("rs_out", [T, D], F32).ap()
    dr["ar_in"] = nc.dram_tensor("ar_in", [256], F32).ap()
    dr["ar_out"] = nc.dram_tensor("ar_out", [256], F32).ap()
    dr["br"] = nc.dram_tensor("br", [16], F32).ap()
    dr["row"] = nc.dram_tensor("rowb", [4096], F32).ap()

    with ExitStack() as ctx:
        e = ctx.enter_context

        xres = e(nc.sbuf_tensor("xres", [128, NC_D, T], F32))
        xlnT = e(nc.sbuf_tensor("xln", [128, NC_D * T], F32))
        stg = e(nc.sbuf_tensor("stg", [128, 4096], F32))
        stg2 = e(nc.sbuf_tensor("stg2", [128, 4096], F32))
        pool = e(nc.sbuf_tensor("pool", [128, 4096], F32))
        wta = e(nc.sbuf_tensor("wta", [128, 1024], F32))
        wtb = e(nc.sbuf_tensor("wtb", [128, 1024], F32))
        ident = e(nc.sbuf_tensor("ident", [128, 128], F32))
        cbias = e(nc.sbuf_tensor("cbias", [128, 8], F32))
        scal = e(nc.sbuf_tensor("scal", [128, 16], F32))
        sc1 = e(nc.sbuf_tensor("sc1", [1, 384], F32))
        mcol = e(nc.sbuf_tensor("mcol", [128, 224], F32))
        bscale = e(nc.sbuf_tensor("bscale", [128, 96], F32))

        xln = xlnT[:, :].rearrange("p (c t) -> p c t", c=NC_D)
        kfeat = xlnT[:, 0:4096]
        qfeat = xlnT[:, 4096:8192]
        kf16 = xlnT[:, 0:4096].bitcast(F16)
        qf16 = xlnT[:, 4096:8192].bitcast(F16)
        vbufF = xlnT[:, 8192:12352]
        vbuf = vbufF.rearrange("p (a b) -> p a b", b=130)
        bmask = xlnT[:, 12352:13248]
        orow = xlnT[:, 13248:14272]
        mubc = pool[:, 0:2048]
        rsbc = pool[:, 2048:4096]
        ropeC = pool[:, 0:1024].rearrange("p (a b) -> p a b", b=32)
        ropeS = pool[:, 1024:2048].rearrange("p (a b) -> p a b", b=32)
        ptile = pool[:, 2048:4096].rearrange("p (a b) -> p a b", b=512)

        psA = e(nc.psum_tensor("psA", [128, 4, 512], F32))
        psB = e(nc.psum_tensor("psB", [128, 4, 512], F32))

        ld = Sem(ctx, nc, "ld")
        sp = Sem(ctx, nc, "sp")
        mm = Sem(ctx, nc, "mm")
        ev = Sem(ctx, nc, "ev")
        vx = Sem(ctx, nc, "vx")
        gx = Sem(ctx, nc, "gx")
        cc = Sem(ctx, nc, "cc")
        bs = Sem(ctx, nc, "bs")
        sw = Sem(ctx, nc, "sw")
        blk0 = e(nc.Block())

        def vrecip(v, out_ap, in_ap):
            v.reciprocal(out_ap, in_ap).then_inc(sw.h, 1)
            v.wait_ge(sw.h, sw.inc())

        def ss(v):
            v._last.then_inc(sw.h, 1)
            v.wait_ge(sw.h, sw.inc())

        class BlkW:
            def __getattr__(self, name):
                def reg(f):
                    getattr(blk0, name)(lambda eng: f(EW(eng)))
                return reg
        blk = BlkW()

        def barrier():
            tgt = bs.n + 1
            bs.n = tgt

            def gpf(en, targets=(ld.n, sp.n, cc.n, mm.n, ev.n, vx.n, gx.n),
                    t=tgt):
                for sem, val in zip((ld, sp, cc, mm, ev, vx, gx), targets):
                    if val:
                        en.wait_ge(sem.h, val)
                en.sem_inc(bs.h, 1)
            blk.gpsimd(gpf)
            for name in ("sync", "vector", "scalar", "tensor", "gpsimd"):
                def f(en, t=tgt):
                    en.wait_ge(bs.h, t)
                getattr(blk, name)(f)

        def mktap(name, src_ap, shape, wait=None):
            if name not in tap:
                return
            tp = nc.dram_tensor("tap_" + name, shape, F32, kind="ExternalOutput").ap()
            tap_outs[name] = tp

            def gp(g):
                if wait is not None:
                    g.wait_ge(wait[0].h, wait[1])
                g.dma_start(out=tp, in_=src_ap).then_inc(sp.h, 16)
            blk.gpsimd(gp)
            sp.inc(16)
            barrier()

        def p0_consts():
            def vec(v):
                v.memset(cbias[:, :], 0.0)
                ss(v)
                v.memset(cbias[:, 0:1], -C_EXP)
                v.memset(cbias[:, 1:2], EPS_LN)
                v.memset(scal[:, 15:16], 1.0)
                v.memset(stg[:, 0:128], 1.0)
                v.fence(vx.h)
            blk.vector(vec)
            vx.inc()

            def gp(g, t=vx.n):
                g.wait_ge(vx.h, t)
                g.affine_select(ident[:, :], stg[:, 0:128], [[1, 128]],
                                ALU.is_equal, 0.0, base=0, channel_multiplier=-1)
                g.fence(gx.h)
            blk.gpsimd(gp)
            gx.inc()

        def p0_input():
            for tt in range(16):
                buf = stg if tt % 2 == 0 else stg2

                def ldf(sy, tt=tt, buf=buf, t_mm=mm.n):
                    sy.wait_ge(mm.h, t_mm)
                    sy.dma_start(out=buf[:, 0:1024],
                                 in_=x_in[tt * 128:(tt + 1) * 128, :]).then_inc(ld.h, 16)
                blk.sync(ldf)
                t_ld = ld.inc(16)

                for c in range(NC_D):
                    def tens(t, tt=tt, buf=buf, c=c, t_ld=t_ld, t_ev=ev.n):
                        if c == 0:
                            t.wait_ge(ld.h, t_ld)
                        t.wait_ge(ev.h, t_ev)
                        t.transpose(psB[:, c % 4, 0:128],
                                    buf[:, c * 128:(c + 1) * 128],
                                    ident[:, :]).then_inc(mm.h, 1)
                    blk.tensor(tens)
                    t_mm = mm.inc()

                    def sc(v, tt=tt, c=c, t_mm=t_mm):
                        v.wait_ge(mm.h, t_mm)
                        v.tensor_copy(xres[:, c, tt * 128:(tt + 1) * 128],
                                      psB[:, c % 4, 0:128]).then_inc(ev.h, 1)
                    blk.vector(sc)
                    ev.inc()

        def prep_weight(w_ap, rows, dst, lpbq):
            cols = w_ap.shape[1]
            ncol = cols // 128
            wrow = pool[:, 0:cols]
            for r in range(rows // 128):
                def ldf(sy, r=r, t_mm=mm.n):
                    sy.wait_ge(mm.h, t_mm)
                    sy.dma_start(out=wrow[:, :],
                                 in_=w_ap[r * 128:(r + 1) * 128, :]).then_inc(ld.h, 16)
                blk.sync(ldf)
                t_ld = ld.inc(16)

                if lpbq:
                    nb = cols // 32

                    def vec(v, t_ld=t_ld, nb=nb):
                        v.wait_ge(ld.h, t_ld)
                        wv = wrow.rearrange("p (b k) -> p b k", k=32)
                        v.tensor_reduce(bscale[:, 0:nb], wv, AX, ALU.max,
                                        apply_absolute_value=True)
                        ss(v)
                        v.tensor_scalar(mcol[:, 0:nb], bscale[:, 0:nb], R7A,
                                        None, ALU.mult)
                        ss(v)
                        v.scalar_tensor_tensor(mcol[:, 0:nb], bscale[:, 0:nb],
                                               R7B, mcol[:, 0:nb],
                                               ALU.mult, ALU.add)
                        ss(v)
                        vrecip(v, bscale[:, 0:nb], mcol[:, 0:nb])
                        v.tensor_tensor(wv, wv,
                                        bscale[:, 0:nb].unsqueeze(2)
                                        .to_broadcast([128, nb, 32]), ALU.mult)
                        ss(v)
                        v.tensor_scalar(wv, wv, MAGIC, MAGIC, ALU.add, ALU.subtract)
                        ss(v)
                        v.tensor_tensor(wv, wv,
                                        mcol[:, 0:nb].unsqueeze(2)
                                        .to_broadcast([128, nb, 32]), ALU.mult)
                        v.fence(vx.h)
                    blk.vector(vec)
                    rdy = ("vx", vx.inc())
                else:
                    rdy = ("ld", t_ld)

                for c in range(ncol):
                    def tens(t, c=c, rdy=rdy, t_ev=ev.n):
                        if c == 0:
                            t.wait_ge({"vx": vx, "ld": ld}[rdy[0]].h, rdy[1])
                        t.wait_ge(ev.h, t_ev)
                        t.transpose(psB[:, c % 4, 0:128],
                                    wrow[:, c * 128:(c + 1) * 128],
                                    ident[:, :]).then_inc(mm.h, 1)
                    blk.tensor(tens)
                    t_mm = mm.inc()

                    def sc(v, c=c, t_mm=t_mm, t_sp=sp.n):
                        v.wait_ge(mm.h, t_mm)
                        v.wait_ge(sp.h, t_sp)
                        v.tensor_copy(stg2[:, (c % 8) * 128:(c % 8) * 128 + 128],
                                      psB[:, c % 4, 0:128]).then_inc(ev.h, 1)
                    blk.vector(sc)
                    t_ev = ev.inc()

                    def gp(g, c=c, r=r, t_ev=t_ev):
                        g.wait_ge(ev.h, t_ev)
                        g.dma_start(out=dst[c, :, r * 128:(r + 1) * 128],
                                    in_=stg2[:, (c % 8) * 128:(c % 8) * 128 + 128]
                                    ).then_inc(sp.h, 16)
                    blk.gpsimd(gp)
                    sp.inc(16)

        def p0_weights():
            prep_weight(wq_in, 512, dr["wqT"], False)
            prep_weight(wk_in, 512, dr["wkT"], False)
            prep_weight(wv_in, 512, dr["wvT"], False)
            prep_weight(wo_in, D, dr["woT"], False)
            prep_weight(wu_in, I3, dr["wuT"], True)
            prep_weight(wg_in, I3, dr["wgT"], True)
            prep_weight(wd_in, D, dr["wdT"], True)

        def layer_norm(tag):
            for g4 in range(4):
                sl = slice(g4 * 512, (g4 + 1) * 512)

                def sc(v, sl=sl, t_vx=vx.n):
                    v.wait_ge(vx.h, t_vx)
                    v.tensor_tensor(stg[:, :].rearrange("p (c t) -> p c t", c=NC_D),
                                    xres[:, :, sl], xres[:, :, sl],
                                    ALU.mult).then_inc(ev.h, 1)
                blk.vector(sc)
                t_sq = ev.inc()

                def tens(t, sl=sl, t_sq=t_sq, t_vx=vx.n):
                    t.wait_ge(ev.h, t_sq)
                    t.wait_ge(vx.h, t_vx)
                    for c in range(NC_D):
                        t.matmul(psB[0:1, 0, 0:512], scal[:, 15:16],
                                 xres[:, c, sl], start=(c == 0), stop=(c == 7))
                    for c in range(NC_D):
                        t.matmul(psB[0:1, 1, 0:512], scal[:, 15:16],
                                 stg[:, c * 512:(c + 1) * 512],
                                 start=(c == 0), stop=(c == 7))
                    t.fence(mm.h)
                blk.tensor(tens)
                t_mm = mm.inc()

                def vec(v, sl=sl, t_mm=t_mm):
                    v.wait_ge(mm.h, t_mm)
                    v.tensor_scalar(mubc[0:1, sl], psB[0:1, 0, 0:512],
                                    1.0 / 1024.0, None, ALU.mult)
                    v.tensor_scalar(rsbc[0:1, sl],
                                    psB[0:1, 1, 0:512], 1.0 / 1024.0, None, ALU.mult)
                    ss(v)
                    v.tensor_tensor(stg2[0:1, 0:512], mubc[0:1, sl], mubc[0:1, sl],
                                    ALU.mult)
                    ss(v)
                    v.tensor_tensor(rsbc[0:1, sl], rsbc[0:1, sl], stg2[0:1, 0:512],
                                    ALU.subtract)
                    v.fence(vx.h)
                blk.vector(vec)
                t_vx = vx.inc()

                def sc2(s, sl=sl, t_vx=t_vx):
                    s.wait_ge(vx.h, t_vx)
                    s.activation(rsbc[0:1, sl], rsbc[0:1, sl], AF.Ln,
                                 bias=cbias[0:1, 1:2], scale=1.0).then_inc(sw.h, 1)
                    s.wait_ge(sw.h, sw.inc())
                    s.activation(rsbc[0:1, sl], rsbc[0:1, sl], AF.Exp,
                                 bias=cbias[0:1, 2:3], scale=-0.5).then_inc(ev.h, 1)
                blk.scalar(sc2)
                t_ev = ev.inc()

                def vec2(v, sl=sl, t_ev=t_ev):
                    v.wait_ge(ev.h, t_ev)
                    v.fence(vx.h) if False else v.tensor_copy(
                        sc1[0:1, 380:381], rsbc[0:1, sl.start:sl.start + 1])
                    v.fence(vx.h)
                blk.vector(vec2)
                vx.inc()

            def gp(g, t_vx=vx.n):
                g.wait_ge(vx.h, t_vx)
                g.dma_start(out=dr["row"][0:2048], in_=mubc[0:1, :]).then_inc(sp.h, 16)
                g.dma_start(out=dr["row"][2048:4096], in_=rsbc[0:1, :]).then_inc(sp.h, 16)
                g.wait_ge(sp.h, sp.inc(32))
                g.dma_start(out=mubc[:, :],
                            in_=dr["row"][0:2048].unsqueeze(0).to_broadcast([128, 2048])).then_inc(sp.h, 16)
                g.dma_start(out=rsbc[:, :],
                            in_=dr["row"][2048:4096].unsqueeze(0).to_broadcast([128, 2048])).then_inc(sp.h, 16)
                g.wait_ge(sp.h, sp.inc(32))
                g.sem_inc(gx.h, 1)
            blk.gpsimd(gp)
            gx.inc()

            def vec3(v, t_gx=gx.n):
                v.wait_ge(gx.h, t_gx)
                for c in range(NC_D):
                    v.tensor_tensor(xln[:, c, :], xres[:, c, :], mubc[:, :],
                                    ALU.subtract)
                    ss(v)
                    v.tensor_tensor(xln[:, c, :], xln[:, c, :], rsbc[:, :],
                                    ALU.mult)
                v.fence(vx.h)
            blk.vector(vec3)
            vx.inc()
            mktap("xln_" + tag, xlnT[:, :], [128, NC_D * T], wait=(vx, vx.n))

        def allreduce_max(n_vals, compute_scalars, nbc):
            def gp(g, t_vx=vx.n, n_vals=n_vals):
                g.wait_ge(vx.h, t_vx)
                for j in range(n_vals):
                    g.dma_start(out=dr["ar_in"][j * 128:(j + 1) * 128],
                                in_=mcol[:, 200 + j:201 + j]).then_inc(sp.h, 16)
                g.wait_ge(sp.h, sp.inc(16 * n_vals))
                g.collective_compute(
                    "AllReduce", ALU.max,
                    replica_groups=[[0, 1, 2, 3, 4, 5, 6, 7]],
                    ins=[dr["ar_in"][0:n_vals * 128].opt()],
                    outs=[dr["ar_out"][0:n_vals * 128].opt()],
                ).then_inc(cc.h, 1)
                g.wait_ge(cc.h, cc.inc())
                g.dma_start(out=sc1[0:1, 0:n_vals * 128],
                            in_=dr["ar_out"][0:n_vals * 128]).then_inc(sp.h, 16)
                g.wait_ge(sp.h, sp.inc(16))
                g.sem_inc(gx.h, 1)
            blk.gpsimd(gp)
            gx.inc()

            def vec(v, t_gx=gx.n, n_vals=n_vals):
                v.wait_ge(gx.h, t_gx)
                for j in range(n_vals):
                    r = v.tensor_reduce(sc1[0:1, 320 + j:321 + j],
                                        sc1[0:1, j * 128:(j + 1) * 128], AX, ALU.max)
                r.then_inc(sw.h, 1)
                v.wait_ge(sw.h, sw.inc())
                compute_scalars(v)
                v.fence(vx.h)
            blk.vector(vec)
            vx.inc()

            def gp2(g, t_vx=vx.n, nbc=nbc):
                g.wait_ge(vx.h, t_vx)
                g.dma_start(out=dr["br"][0:nbc], in_=sc1[0:1, 340:340 + nbc]).then_inc(sp.h, 16)
                g.wait_ge(sp.h, sp.inc(16))
                g.dma_start(out=scal[:, 0:nbc],
                            in_=dr["br"][0:nbc].unsqueeze(0).to_broadcast([128, nbc])).then_inc(sp.h, 16)
                g.wait_ge(sp.h, sp.inc(16))
                g.sem_inc(gx.h, 1)
            blk.gpsimd(gp2)
            gx.inc()
            barrier()

        def ffn_call(tag):
            sub = FFN_SUB
            layer_norm(tag)
            if sub < 2:
                return

            def vecm(v, t_vx=vx.n):
                v.wait_ge(vx.h, t_vx)
                v.tensor_reduce(mcol[:, 200:201],
                                xlnT[:, :], AX, ALU.max,
                                apply_absolute_value=True)
                v.fence(vx.h)
            blk.vector(vecm)
            vx.inc()

            def scalars1(v):
                vrecip(v, sc1[0:1, 340:341], sc1[0:1, 340:341] if False else sc1[0:1, 320:321])
                v.tensor_scalar(sc1[0:1, 340:341], sc1[0:1, 340:341], QMAX, None,
                                ALU.mult)
                ss(v)
                v.tensor_scalar(sc1[0:1, 300:301], sc1[0:1, 320:321], 1.0 / QMAX,
                                None, ALU.mult)
            allreduce_max(1, scalars1, 1)
            mktap("sc1_" + tag, sc1[0:1, :], [1, 384], wait=(gx, gx.n))
            mktap("scal_" + tag, scal[:, :], [128, 16], wait=(gx, gx.n))
            mktap("mcol_" + tag, mcol[:, :], [128, 224], wait=(gx, gx.n))
            if sub < 3:
                return

            def vecq(v):
                v.tensor_scalar(xln[:, :, :], xln[:, :, :], scal[:, 0:1], MAGIC,
                                ALU.mult, ALU.add)
                ss(v)
                v.tensor_scalar(xln[:, :, :], xln[:, :, :], MAGIC, None,
                                ALU.subtract)
                v.fence(vx.h)
            blk.vector(vecq)
            vx.inc()
            mktap("qx_" + tag, xlnT[:, :], [128, NC_D * T], wait=(vx, vx.n))
            barrier()
            if sub < 4:
                return

            bT = {"mm": {}, "ev": {}, "vx": {}, "sp": {}}
            step = [0]
            for which, wsrc, spdst, mbase in (("u", dr["wuT"], dr["up"], 8),
                                              ("g", dr["wgT"], dr["gate"], 40)):
                for i in range(NI):
                    k_ = step[0]
                    par = k_ % 2
                    wbuf = wta if par == 0 else wtb
                    ps = psA if par == 0 else psB
                    sbuf = stg if par == 0 else stg2

                    def ldf(sy, i=i, wbuf=wbuf, wsrc=wsrc,
                            t_mm=bT["mm"].get(k_ - 2, 0)):
                        if t_mm:
                            sy.wait_ge(mm.h, t_mm)
                        sy.dma_start(out=wbuf[:, :].rearrange("p (c n) -> p c n", c=NC_D),
                                     in_=wsrc[:, :, i * 128:(i + 1) * 128]
                                     .rearrange("c p n -> p c n")).then_inc(ld.h, 16)
                    blk.sync(ldf)
                    t_ld = ld.inc(16)

                    def tens(t, wbuf=wbuf, ps=ps, t_ld=t_ld,
                             t_ev=bT["ev"].get(k_ - 2, 0)):
                        t.wait_ge(ld.h, t_ld)
                        if t_ev:
                            t.wait_ge(ev.h, t_ev)
                        for c in range(NC_D):
                            for tb in range(4):
                                t.matmul(ps[:, tb, :], wbuf[:, c * 128:(c + 1) * 128],
                                         xln[:, c, tb * 512:(tb + 1) * 512],
                                         start=(c == 0), stop=(c == 7))
                        t.fence(mm.h)
                    blk.tensor(tens)
                    t_mm = mm.inc()
                    bT["mm"][k_] = t_mm

                    def sc(v, ps=ps, sbuf=sbuf, t_mm=t_mm,
                           t_sp=bT["sp"].get(k_ - 2, 0),
                           t_vx=bT["vx"].get(k_ - 2, 0)):
                        v.wait_ge(mm.h, t_mm)
                        if t_sp:
                            v.wait_ge(sp.h, t_sp)
                        if t_vx:
                            v.wait_ge(vx.h, t_vx)
                        v.tensor_copy(sbuf[:, 0:2048],
                                      ps[:, :, :].rearrange("p a b -> p (a b)")
                                      ).then_inc(ev.h, 1)
                    blk.vector(sc)
                    t_ev = ev.inc()
                    bT["ev"][k_] = t_ev

                    def vecr(v, i=i, sbuf=sbuf, mbase=mbase, t_ev=t_ev):
                        v.wait_ge(ev.h, t_ev)
                        v.tensor_reduce(mcol[:, mbase + i:mbase + i + 1],
                                        sbuf[:, 0:2048], AX, ALU.max,
                                        apply_absolute_value=True)
                        v.fence(vx.h)
                    blk.vector(vecr)
                    bT["vx"][k_] = vx.inc()

                    def gp(g, i=i, sbuf=sbuf, spdst=spdst, t_ev=t_ev):
                        g.wait_ge(ev.h, t_ev)
                        g.dma_start(out=spdst[i, :, :],
                                    in_=sbuf[:, 0:2048]).then_inc(sp.h, 16)
                    blk.gpsimd(gp)
                    bT["sp"][k_] = sp.inc(16)
                    step[0] += 1

                def vecf(v, mbase=mbase, which=which, t_vx=vx.n):
                    v.wait_ge(vx.h, t_vx)
                    col = 200 if which == "u" else 201
                    v.tensor_reduce(mcol[:, col:col + 1],
                                    mcol[:, mbase:mbase + NI], AX, ALU.max)
                    v.fence(vx.h)
                blk.vector(vecf)
                vx.inc()

            def scalars2(v):
                v.reciprocal(sc1[0:1, 340:341], sc1[0:1, 320:321])
                vrecip(v, sc1[0:1, 341:342], sc1[0:1, 321:322])
                v.tensor_scalar(sc1[0:1, 340:342], sc1[0:1, 340:342], QMAX, None,
                                ALU.mult)
                ss(v)
                v.tensor_scalar(sc1[0:1, 301:303], sc1[0:1, 320:322], 1.0 / QMAX,
                                None, ALU.mult)
                ss(v)
                v.tensor_tensor(sc1[0:1, 301:303], sc1[0:1, 301:303],
                                sc1[0:1, 300:301].to_broadcast([1, 2]), ALU.mult)
                ss(v)
                v.tensor_copy(sc1[0:1, 342:343], sc1[0:1, 302:303])
                ss(v)
                v.tensor_scalar(sc1[0:1, 344:345], sc1[0:1, 302:303], -1.0, None,
                                ALU.mult)
            allreduce_max(2, scalars2, 5)
            if ("upE_" + tag) in tap:
                def ldt(sy, t_mm=mm.n):
                    sy.dma_start(out=stg[:, 0:2048],
                                 in_=dr["up"][0, :, :]).then_inc(ld.h, 16)
                blk.sync(ldt)
                mktap("upE_" + tag, stg[:, 0:2048], [128, 2048],
                      wait=(ld, ld.inc(16)))
            if sub < 5:
                return

            sgb = pool[:, 0:2048]
            cT = {"vx": {}, "sp": {}}
            for i in range(NI):
                b0 = stg if i % 2 == 0 else stg2

                def ldf(sy, i=i, b0=b0, t_vx=cT["vx"].get(i - 2, 0),
                        t_sp=cT["sp"].get(i - 2, 0)):
                    if t_vx:
                        sy.wait_ge(vx.h, t_vx)
                    if t_sp:
                        sy.wait_ge(sp.h, t_sp)
                    sy.dma_start(out=b0[:, 0:2048], in_=dr["up"][i, :, :]).then_inc(ld.h, 16)
                    sy.dma_start(out=b0[:, 2048:4096], in_=dr["gate"][i, :, :]).then_inc(ld.h, 16)
                blk.sync(ldf)
                t_ld = ld.inc(32)

                def vec(v, b0=b0, t_ld=t_ld):
                    v.wait_ge(ld.h, t_ld)
                    v.tensor_scalar(b0[:, 0:2048], b0[:, 0:2048], scal[:, 0:1],
                                    MAGIC, ALU.mult, ALU.add)
                    ss(v)
                    v.tensor_scalar(b0[:, 0:2048], b0[:, 0:2048], MAGIC, None,
                                    ALU.subtract)
                    v.tensor_scalar(b0[:, 2048:4096], b0[:, 2048:4096], scal[:, 1:2],
                                    MAGIC, ALU.mult, ALU.add)
                    ss(v)
                    v.tensor_scalar(b0[:, 2048:4096], b0[:, 2048:4096], MAGIC, None,
                                    ALU.subtract)
                    v.fence(vx.h)
                blk.vector(vec)
                t_q = vx.inc()

                def gp(g, i=i, b0=b0, t_q=t_q):
                    g.wait_ge(vx.h, t_q)
                    g.dma_start(out=dr["qu"][i, :, :], in_=b0[:, 0:2048]).then_inc(sp.h, 16)
                    g.dma_start(out=dr["qg"][i, :, :], in_=b0[:, 2048:4096]).then_inc(sp.h, 16)
                blk.gpsimd(gp)
                cT["sp"][i] = sp.inc(32)

                def sc(s, b0=b0, t_q=t_q, t_vx2=vx.n):
                    s.wait_ge(vx.h, t_q)
                    s.activation(sgb, b0[:, 2048:4096], AF.Sigmoid,
                                 bias=cbias[:, 2:3], scale=scal[:, 2:3]).then_inc(ev.h, 1)
                blk.scalar(sc)
                t_sg = ev.inc()

                def vec2(v, i=i, b0=b0, t_sg=t_sg):
                    v.wait_ge(ev.h, t_sg)
                    v.tensor_scalar(sgb, sgb, 65536.0, MAGIC, ALU.mult, ALU.add)
                    ss(v)
                    v.tensor_scalar(sgb, sgb, MAGIC, None, ALU.subtract)
                    ss(v)
                    v.tensor_tensor(sgb, b0[:, 2048:4096], sgb, ALU.mult)
                    ss(v)
                    v.tensor_reduce(mcol[:, 72 + i:73 + i], sgb, AX, ALU.max,
                                    apply_absolute_value=True)
                    v.fence(vx.h)
                blk.vector(vec2)
                cT["vx"][i] = vx.inc()

            def vecf2(v, t_vx=vx.n):
                v.wait_ge(vx.h, t_vx)
                v.tensor_reduce(mcol[:, 200:201], mcol[:, 72:72 + NI], AX, ALU.max)
                v.fence(vx.h)
            blk.vector(vecf2)
            vx.inc()

            def scalars3(v):
                vrecip(v, sc1[0:1, 340:341], sc1[0:1, 320:321])
                v.tensor_scalar(sc1[0:1, 340:341], sc1[0:1, 340:341], QMAX, None,
                                ALU.mult)
                ss(v)
                v.tensor_scalar(sc1[0:1, 303:304], sc1[0:1, 320:321],
                                1.0 / 65536.0 / QMAX, None, ALU.mult)
                ss(v)
                v.tensor_tensor(sc1[0:1, 303:304], sc1[0:1, 303:304],
                                sc1[0:1, 302:303], ALU.mult)
                ss(v)
                v.tensor_copy(sc1[0:1, 341:342], sc1[0:1, 302:303])
                ss(v)
                v.tensor_scalar(sc1[0:1, 344:345], sc1[0:1, 302:303], -1.0, None,
                                ALU.mult)
            allreduce_max(1, scalars3, 5)
            if sub < 6:
                return

            dT = {}
            for i in range(NI):
                b0 = stg if i % 2 == 0 else stg2

                def ldf(sy, i=i, b0=b0, t_vx=dT.get(i - 2, 0)):
                    if t_vx:
                        sy.wait_ge(vx.h, t_vx)
                    sy.dma_start(out=b0[:, 0:2048], in_=dr["qu"][i, :, :]).then_inc(ld.h, 16)
                    sy.dma_start(out=b0[:, 2048:4096], in_=dr["qg"][i, :, :]).then_inc(ld.h, 16)
                blk.sync(ldf)
                t_ld = ld.inc(32)

                def sc(s, b0=b0, t_ld=t_ld, t_sp=sp.n, t_vx=vx.n):
                    s.wait_ge(ld.h, t_ld)
                    s.wait_ge(sp.h, t_sp)
                    s.wait_ge(vx.h, t_vx)
                    s.activation(sgb, b0[:, 2048:4096], AF.Sigmoid,
                                 bias=cbias[:, 2:3], scale=scal[:, 1:2]).then_inc(ev.h, 1)
                blk.scalar(sc)
                t_sg = ev.inc()

                def vec(v, i=i, b0=b0, t_sg=t_sg):
                    v.wait_ge(ev.h, t_sg)
                    v.tensor_scalar(sgb, sgb, 65536.0, MAGIC, ALU.mult, ALU.add)
                    v.tensor_scalar(sgb, sgb, MAGIC, None, ALU.subtract)
                    v.tensor_tensor(sgb, b0[:, 2048:4096], sgb, ALU.mult)
                    v.tensor_scalar(sgb, sgb, scal[:, 0:1], MAGIC, ALU.mult, ALU.add)
                    v.tensor_scalar(sgb, sgb, MAGIC, None, ALU.subtract)
                    v.tensor_tensor(b0[:, 2048:4096], sgb, b0[:, 0:2048], ALU.mult)
                    v.tensor_reduce(mcol[:, 104 + i:105 + i], b0[:, 2048:4096],
                                    AX, ALU.max, apply_absolute_value=True)
                    v.fence(vx.h)
                blk.vector(vec)
                t_q = vx.inc()
                dT[i] = t_q

                def gp(g, i=i, t_q=t_q):
                    g.wait_ge(vx.h, t_q)
                    g.dma_start(out=dr["qt"][i, :, :], in_=sgb).then_inc(sp.h, 16)
                blk.gpsimd(gp)
                sp.inc(16)

            def vecf3(v, t_vx=vx.n):
                v.wait_ge(vx.h, t_vx)
                v.tensor_reduce(mcol[:, 200:201], mcol[:, 104:104 + NI], AX, ALU.max)
                v.fence(vx.h)
            blk.vector(vecf3)
            vx.inc()

            def scalars4(v):
                vrecip(v, sc1[0:1, 340:341], sc1[0:1, 320:321])
                v.tensor_scalar(sc1[0:1, 340:341], sc1[0:1, 340:341], QMAX, None,
                                ALU.mult)
                ss(v)
                v.tensor_scalar(sc1[0:1, 304:305], sc1[0:1, 320:321], 1.0 / QMAX,
                                None, ALU.mult)
                ss(v)
                v.tensor_tensor(sc1[0:1, 304:305], sc1[0:1, 304:305],
                                sc1[0:1, 303:304], ALU.mult)
                ss(v)
                v.tensor_tensor(sc1[0:1, 304:305], sc1[0:1, 304:305],
                                sc1[0:1, 301:302], ALU.mult)
                ss(v)
                v.tensor_scalar(sc1[0:1, 341:342], sc1[0:1, 304:305], 0.5, None,
                                ALU.mult)
            allreduce_max(1, scalars4, 2)
            if sub < 7:
                return

            eT = {}
            for tb in range(4):
                for k in range(NI):
                    b0 = stg if k % 2 == 0 else stg2
                    ks_ = tb * NI + k

                    def ldf(sy, tb=tb, k=k, b0=b0, t_mm=eT.get(ks_ - 2, 0)):
                        if t_mm:
                            sy.wait_ge(mm.h, t_mm)
                        sy.dma_start(out=b0[:, 0:512],
                                     in_=dr["qt"][k, :, tb * 512:(tb + 1) * 512]).then_inc(ld.h, 16)
                        sy.dma_start(out=b0[:, 512:1024],
                                     in_=dr["qu"][k, :, tb * 512:(tb + 1) * 512]).then_inc(ld.h, 16)
                        sy.dma_start(out=b0[:, 1024:2048],
                                     in_=dr["wdT"][k, :, :]).then_inc(ld.h, 16)
                    blk.sync(ldf)
                    t_ld = ld.inc(48)

                    def vec(v, b0=b0, t_ld=t_ld):
                        v.wait_ge(ld.h, t_ld)
                        v.tensor_tensor(b0[:, 0:512], b0[:, 0:512], b0[:, 512:1024],
                                        ALU.mult)
                        v.tensor_scalar(b0[:, 0:512], b0[:, 0:512], scal[:, 0:1],
                                        MAGIC, ALU.mult, ALU.add)
                        v.tensor_scalar(b0[:, 0:512], b0[:, 0:512], MAGIC, None,
                                        ALU.subtract)
                        v.fence(vx.h)
                    blk.vector(vec)
                    t_oq = vx.inc()

                    def tens(t, tb=tb, k=k, b0=b0, t_oq=t_oq, t_vx=vx.n):
                        t.wait_ge(vx.h, t_oq)
                        if k == 0:
                            t.wait_ge(vx.h, t_vx)
                        for dti in range(8):
                            ps = psA if dti < 4 else psB
                            t.matmul(ps[:, dti % 4, :],
                                     b0[:, 1024 + dti * 128:1024 + (dti + 1) * 128],
                                     b0[:, 0:512], start=(k == 0), stop=(k == NI - 1))
                        t.fence(mm.h)
                    blk.tensor(tens)
                    eT[ks_] = mm.inc()

                def vecu(v, tb=tb, t_mm=mm.n):
                    v.wait_ge(mm.h, t_mm)
                    for dti in range(8):
                        ps = psA if dti < 4 else psB
                        v.scalar_tensor_tensor(
                            xres[:, dti, tb * 512:(tb + 1) * 512],
                            ps[:, dti % 4, :], scal[:, 1:2],
                            xres[:, dti, tb * 512:(tb + 1) * 512],
                            ALU.mult, ALU.add)
                    v.fence(vx.h)
                blk.vector(vecu)
                vx.inc()
            barrier()

            for nm in ("up", "gate", "qu", "qg", "qt", "up2"):
                tnm = nm + "0_" + tag
                if tnm in tap:
                    src = "up" if nm == "up2" else nm
                    def ldt(sy, src=src, t_mm=mm.n):
                        sy.dma_start(out=stg[:, 0:2048],
                                     in_=dr[src][0, :, :]).then_inc(ld.h, 16)
                    blk.sync(ldt)
                    mktap(tnm, stg[:, 0:2048], [128, 2048], wait=(ld, ld.inc(16)))

        def attention():
            layer_norm("a")
            def gp(g, t_vx=vx.n):
                g.wait_ge(vx.h, t_vx)
                g.dma_start(out=dr["ag_in"].rearrange("c p t -> p c t"),
                            in_=xln[:, :, :]).then_inc(sp.h, 16)
                g.wait_ge(sp.h, sp.inc(16))
                g.collective_compute(
                    "AllGather", ALU.bypass,
                    replica_groups=[[0, 1], [2, 3], [4, 5], [6, 7]],
                    ins=[dr["ag_in"].rearrange("c p t -> (c p t)").opt()],
                    outs=[dr["ag_out"].rearrange("h c p t -> (h c p t)").opt()],
                ).then_inc(cc.h, 1)
                g.wait_ge(cc.h, cc.inc())
                g.sem_inc(gx.h, 1)
            blk.gpsimd(gp)
            gx.inc()
            barrier()

            def gpi(g):
                g.iota(ropeC[:, :, 0:1], [[128, 32], [0, 1]], base=0,
                       channel_multiplier=1, allow_small_or_imprecise_dtypes=True)
                g.fence(gx.h)
            blk.gpsimd(gpi)
            gx.inc()

            def veci(v, t_gx=gx.n):
                v.wait_ge(gx.h, t_gx)
                v.memset(sc1[0:1, 256:257], 1.0)
                ss(v)
                for i in range(1, 32):
                    v.tensor_scalar(sc1[0:1, 256 + i:257 + i], sc1[0:1, 255 + i:256 + i],
                                    RINV, None, ALU.mult)
                    ss(v)
                v.tensor_copy(sc1[0:1, 379:380], sc1[0:1, 256:257])
                v.fence(vx.h)
            blk.vector(veci)
            vx.inc()

            def gpb(g, t_vx=vx.n):
                g.wait_ge(vx.h, t_vx)
                g.dma_start(out=dr["br"][0:16] if False else dr["ar_in"][0:32],
                            in_=sc1[0:1, 256:288]).then_inc(sp.h, 16)
                g.wait_ge(sp.h, sp.inc(16))
                g.dma_start(out=stg[:, 0:32],
                            in_=dr["ar_in"][0:32].unsqueeze(0).to_broadcast([128, 32])).then_inc(sp.h, 16)
                g.wait_ge(sp.h, sp.inc(16))
                g.sem_inc(gx.h, 1)
            blk.gpsimd(gpb)
            gx.inc()

            def vecr(v, t_gx=gx.n):
                v.wait_ge(gx.h, t_gx)
                for j in range(32):
                    v.tensor_tensor(ropeS[:, j, :],
                                    ropeC[:, j, 0:1].to_broadcast([128, 32]),
                                    stg[:, 0:32], ALU.mult)
                ss(v)
                v.tensor_scalar(ropeC[:, :, :], ropeS[:, :, :], INV_2PI, MAGIC,
                                ALU.mult, ALU.add)
                ss(v)
                v.tensor_scalar(ropeC[:, :, :], ropeC[:, :, :], MAGIC, None,
                                ALU.subtract)
                ss(v)
                v.scalar_tensor_tensor(ropeS[:, :, :], ropeC[:, :, :], -CW1,
                                       ropeS[:, :, :], ALU.mult, ALU.add)
                ss(v)
                v.scalar_tensor_tensor(ropeS[:, :, :], ropeC[:, :, :], -CW2,
                                       ropeS[:, :, :], ALU.mult, ALU.add)
                ss(v)
                v.tensor_scalar(ropeS[:, :, :], ropeS[:, :, :], PI_F, -PI_F,
                                ALU.min, ALU.max)
                v.fence(vx.h)
            blk.vector(vecr)
            vx.inc()

            def scs(s, t_vx=vx.n):
                s.wait_ge(vx.h, t_vx)
                s.activation(stg2[:, 0:1024].rearrange("p (a b) -> p a b", b=32),
                             ropeS[:, :, :], AF.Sin, bias=cbias[:, 2:3],
                             scale=1.0).then_inc(ev.h, 1)
            blk.scalar(scs)
            ev.inc()

            def vecc(v, t_ev=ev.n):
                v.wait_ge(ev.h, t_ev)
                v.tensor_scalar(ropeS[:, :, :], ropeS[:, :, :], HALF_PI, None,
                                ALU.add)
                ss(v)
                v.tensor_scalar(stg[:, 32:1056].rearrange("p (a b) -> p a b", b=32),
                                ropeS[:, :, :], PI_F, TWO_PI, ALU.is_gt, ALU.mult)
                ss(v)
                v.tensor_tensor(ropeS[:, :, :], ropeS[:, :, :],
                                stg[:, 32:1056].rearrange("p (a b) -> p a b", b=32),
                                ALU.subtract)
                ss(v)
                v.tensor_scalar(ropeS[:, :, :], ropeS[:, :, :], PI_F, -PI_F,
                                ALU.min, ALU.max)
                v.fence(vx.h)
            blk.vector(vecc)
            vx.inc()

            def scc(s, t_vx=vx.n):
                s.wait_ge(vx.h, t_vx)
                s.activation(ropeC[:, :, :], ropeS[:, :, :], AF.Sin,
                             bias=cbias[:, 2:3], scale=1.0).then_inc(ev.h, 1)
            blk.scalar(scc)
            ev.inc()

            def vecc2(v, t_ev=ev.n):
                v.wait_ge(ev.h, t_ev)
                v.tensor_copy(ropeS[:, :, :],
                              stg2[:, 0:1024].rearrange("p (a b) -> p a b", b=32))
                v.fence(vx.h)
            blk.vector(vecc2)
            vx.inc()
            barrier()

            def ldw(sy):
                sy.dma_start(out=kfeat.rearrange("p (c n) -> p c n", c=NC_D),
                             in_=dr["wqT"].rearrange("c p n -> p c n")).then_inc(ld.h, 16)
                sy.dma_start(out=qfeat.rearrange("p (c n) -> p c n", c=NC_D),
                             in_=dr["wkT"].rearrange("c p n -> p c n")).then_inc(ld.h, 16)
                sy.dma_start(out=vbufF[:, 0:4096].rearrange("p (c n) -> p c n", c=NC_D),
                             in_=dr["wvT"].rearrange("c p n -> p c n")).then_inc(ld.h, 16)
            blk.sync(ldw)
            t_w = ld.inc(48)

            qT = {"mm": {}, "ev": {}}
            for tt in range(32):
                xa = stg if tt % 2 == 0 else stg2
                pq = psA if tt % 2 == 0 else psB
                half, lt = tt // 16, tt % 16

                def ldf(sy, xa=xa, half=half, lt=lt, t_mm=mm.n):
                    sy.wait_ge(mm.h, t_mm)
                    sy.dma_start(out=xa[:, 0:1024].rearrange("p (c t) -> p c t", c=NC_D),
                                 in_=dr["ag_out"][half, :, :, lt * 128:(lt + 1) * 128]
                                 .rearrange("c p t -> p c t")).then_inc(ld.h, 16)
                blk.sync(ldf)
                t_ld = ld.inc(16)

                def tens(t, xa=xa, pq=pq, t_ld=t_ld, t_w=t_w, t_ev=ev.n):
                    t.wait_ge(ld.h, max(t_ld, t_w))
                    t.wait_ge(ev.h, t_ev)
                    for c in range(NC_D):
                        t.matmul(pq[:, 0, :], xa[:, c * 128:(c + 1) * 128],
                                 kfeat[:, c * 512:(c + 1) * 512],
                                 start=(c == 0), stop=(c == 7))
                        t.matmul(pq[:, 1, :], xa[:, c * 128:(c + 1) * 128],
                                 qfeat[:, c * 512:(c + 1) * 512],
                                 start=(c == 0), stop=(c == 7))
                        t.matmul(pq[:, 2, :], xa[:, c * 128:(c + 1) * 128],
                                 vbufF[:, c * 512:(c + 1) * 512],
                                 start=(c == 0), stop=(c == 7))
                    t.fence(mm.h)
                blk.tensor(tens)
                t_mm = mm.inc()
                qT["mm"][tt] = t_mm

                def sc(v, pq=pq, t_mm=t_mm, t_sp=sp.n):
                    v.wait_ge(mm.h, t_mm)
                    v.wait_ge(sp.h, t_sp)
                    v.tensor_copy(ptile[:, 0, :], pq[:, 0, :])
                    v.tensor_copy(ptile[:, 1, :], pq[:, 1, :])
                    v.tensor_copy(ptile[:, 2, :], pq[:, 2, :]).then_inc(ev.h, 1)
                blk.vector(sc)
                t_ev = ev.inc()
                qT["ev"][tt] = t_ev

                def vec(v, tt=tt, t_ev=t_ev):
                    v.wait_ge(ev.h, t_ev)
                    co = ropeC[:, tt, :]
                    si = ropeS[:, tt, :]
                    for which in (0, 1):
                        q4 = ptile[:, which, :].rearrange("p (h f two) -> p h f two",
                                                          h=8, two=2)
                        r4 = ptile[:, 3, :].rearrange("p (h f two) -> p h f two",
                                                      h=8, two=2)
                        for h in range(8):
                            v.scalar_tensor_tensor(r4[:, h, :, 0], q4[:, h, :, 1],
                                                   -1.0, si, ALU.mult, ALU.mult)
                            v.tensor_tensor(r4[:, h, :, 1], q4[:, h, :, 0], si,
                                            ALU.mult)
                            ss(v)
                            v.tensor_tensor(q4[:, h, :, :], q4[:, h, :, :],
                                            co.unsqueeze(2).to_broadcast([128, 32, 2]),
                                            ALU.mult)
                        ss(v)
                        v.tensor_tensor(ptile[:, which, :], ptile[:, which, :],
                                        ptile[:, 3, :], ALU.add)
                        if which == 0:
                            ss(v)
                    v.fence(vx.h)
                blk.vector(vec)
                t_vx = vx.inc()

                def gp2(g, tt=tt, t_vx=t_vx):
                    g.wait_ge(vx.h, t_vx)
                    g.dma_start(out=dr["qsp"][tt, :, :], in_=ptile[:, 0, :]).then_inc(sp.h, 16)
                    g.dma_start(out=dr["ksp"][tt, :, :], in_=ptile[:, 1, :]).then_inc(sp.h, 16)
                    g.dma_start(out=dr["vsp"][tt, :, :], in_=ptile[:, 2, :]).then_inc(sp.h, 16)
                blk.gpsimd(gp2)
                sp.inc(48)
            barrier()

            def vecb(v):
                v.memset(bmask[:, :], 1.0)
                v.memset(vbuf[:, :, 64:65], 1.0)
                v.memset(vbuf[:, :, 129:130], 1.0)
                v.fence(vx.h)
            blk.vector(vecb)
            vx.inc()

            def gpm(g, t_vx=vx.n):
                g.wait_ge(vx.h, t_vx)
                g.affine_select(bmask[:, :], bmask[:, :], [[1, 896]], ALU.is_ge,
                                0.0, base=-384, channel_multiplier=-1)
                g.fence(gx.h)
            blk.gpsimd(gpm)
            gx.inc()
            barrier()

            for pair in range(4):
                for tt in range(32):
                    def ldf(sy, tt=tt, pair=pair, t_mm=mm.n, t_vx=vx.n):
                        sy.wait_ge(mm.h, t_mm)
                        sy.wait_ge(vx.h, t_vx)
                        sy.dma_start(out=stg[:, (tt % 2) * 128:(tt % 2) * 128 + 128],
                                     in_=dr["ksp"][tt, :, pair * 128:(pair + 1) * 128]
                                     ).then_inc(ld.h, 16)
                        sy.dma_start(out=stg[:, 256 + (tt % 2) * 128:256 + (tt % 2) * 128 + 128],
                                     in_=dr["qsp"][tt, :, pair * 128:(pair + 1) * 128]
                                     ).then_inc(ld.h, 16)
                        sy.dma_start(out=stg2[:, (tt % 4) * 128:(tt % 4) * 128 + 128],
                                     in_=dr["vsp"][tt, :, pair * 128:(pair + 1) * 128]
                                     ).then_inc(ld.h, 16)
                    blk.sync(ldf)
                    t_ld = ld.inc(48)

                    def tens(t, tt=tt, t_ld=t_ld, t_ev=ev.n):
                        t.wait_ge(ld.h, t_ld)
                        t.wait_ge(ev.h, t_ev)
                        t.transpose(psB[:, tt % 2, 0:128],
                                    stg[:, (tt % 2) * 128:(tt % 2) * 128 + 128],
                                    ident[:, :])
                        t.transpose(psB[:, 2 + tt % 2, 0:128],
                                    stg[:, 256 + (tt % 2) * 128:256 + (tt % 2) * 128 + 128],
                                    ident[:, :]).then_inc(mm.h, 1)
                    blk.tensor(tens)
                    t_mm = mm.inc()

                    def sc(v, tt=tt, t_mm=t_mm):
                        v.wait_ge(mm.h, t_mm)
                        sl = slice(tt * 128, (tt + 1) * 128)
                        sl2 = slice(4096 + tt * 128, 4096 + (tt + 1) * 128)
                        v.tensor_copy(kf16[:, sl], psB[:, tt % 2, 0:128])
                        v.tensor_copy(qf16[:, sl], psB[:, 2 + tt % 2, 0:128])
                        ss(v)
                        v.tensor_tensor(kf16[:, sl2], psB[:, tt % 2, 0:128],
                                        kf16[:, sl], ALU.subtract)
                        v.tensor_tensor(qf16[:, sl2], psB[:, 2 + tt % 2, 0:128],
                                        qf16[:, sl], ALU.subtract).then_inc(ev.h, 1)
                    blk.vector(sc)
                    ev.inc()

                    def vecv(v, tt=tt, t_ld=t_ld):
                        v.wait_ge(ld.h, t_ld)
                        v.tensor_copy(vbuf[:, tt, 0:64],
                                      stg2[:, (tt % 4) * 128:(tt % 4) * 128 + 64])
                        v.tensor_copy(vbuf[:, tt, 65:129],
                                      stg2[:, (tt % 4) * 128 + 64:(tt % 4) * 128 + 128])
                        v.fence(vx.h)
                    blk.vector(vecv)
                    vx.inc()
                barrier()

                prev_last_exp = 0
                prev_sc_o = 0
                for hh in range(2):
                    hloc = pair * 2 + hh
                    for qb in range(8):
                        nk = 4 * qb + 4
                        exp_tgt = {}
                        pv_tgt = {}

                        def emit_scores(j, hh=hh, qb=qb):
                            def tens_j(t, j=j, hh=hh, qb=qb,
                                       w_ev=exp_tgt.get(j - 2),
                                       w_prev=prev_last_exp):
                                if w_ev is not None:
                                    t.wait_ge(ev.h, w_ev)
                                elif w_prev:
                                    t.wait_ge(ev.h, w_prev)
                                khi = kf16[hh * 64:(hh + 1) * 64,
                                           j * 128:(j + 1) * 128]
                                klo = kf16[hh * 64:(hh + 1) * 64,
                                           4096 + j * 128:4096 + (j + 1) * 128]
                                qhi = qf16[hh * 64:(hh + 1) * 64,
                                           qb * 512:(qb + 1) * 512]
                                qlo = qf16[hh * 64:(hh + 1) * 64,
                                           4096 + qb * 512:4096 + (qb + 1) * 512]
                                t.matmul(psB[:, j % 2, :], khi, qhi,
                                         start=True, stop=False)
                                t.matmul(psB[:, j % 2, :], khi, qlo,
                                         start=False, stop=False)
                                t.matmul(psB[:, j % 2, :], klo, qhi,
                                         start=False, stop=True).then_inc(mm.h, 1)
                            blk.tensor(tens_j)
                            t_mm = mm.inc()

                            def sc_j(s, j=j, t_mm=t_mm, w_pv=pv_tgt.get(j - 4)):
                                s.wait_ge(mm.h, t_mm)
                                if w_pv is not None:
                                    s.wait_ge(mm.h, w_pv)
                                s.activation(ptile[:, j % 4, :], psB[:, j % 2, :],
                                             AF.Exp, bias=cbias[:, 0:1],
                                             scale=0.125).then_inc(ev.h, 1)
                            blk.scalar(sc_j)
                            exp_tgt[j] = ev.inc()

                            if j >= 4 * qb:
                                def vec_j(v, j=j, qb=qb, t_exp=exp_tgt[j]):
                                    v.wait_ge(ev.h, t_exp)
                                    off = 384 + qb * 512 - j * 128
                                    v.tensor_tensor(ptile[:, j % 4, :],
                                                    ptile[:, j % 4, :],
                                                    bmask[:, off:off + 512], ALU.mult)
                                    v.fence(vx.h)
                                blk.vector(vec_j)
                                return ("vx", vx.inc())
                            return ("ev", exp_tgt[j])

                        def emit_pv(j, rdy, hh=hh, qb=qb, nk=nk):
                            def tens_pv(t, j=j, hh=hh, rdy=rdy, nk=nk,
                                        w_evac=prev_sc_o):
                                t.wait_ge({"vx": vx, "ev": ev}[rdy[0]].h, rdy[1])
                                if j == 0 and w_evac:
                                    t.wait_ge(ev.h, w_evac)
                                t.matmul(psA[0:65, 0, :],
                                         vbuf[:, j, hh * 65:(hh + 1) * 65],
                                         ptile[:, j % 4, :],
                                         start=(j == 0),
                                         stop=(j == nk - 1)).then_inc(mm.h, 1)
                            blk.tensor(tens_pv)
                            pv_tgt[j] = mm.inc()

                        rdys = {}
                        if FLASH_PIPE:
                            for j in range(nk):
                                rdys[j] = emit_scores(j)
                                if j >= 1:
                                    emit_pv(j - 1, rdys[j - 1])
                            emit_pv(nk - 1, rdys[nk - 1])
                        else:
                            for j in range(nk):
                                rdys[j] = emit_scores(j)
                                emit_pv(j, rdys[j])
                        prev_last_exp = exp_tgt[nk - 1]

                        def vec_e(v, t_mm=pv_tgt[nk - 1], t_sp=sp.n):
                            v.wait_ge(mm.h, t_mm)
                            v.wait_ge(sp.h, t_sp)
                            v.tensor_copy(orow[0:65, 0:512],
                                          psA[0:65, 0, :]).then_inc(ev.h, 1)
                        blk.vector(vec_e)
                        t_pe = ev.inc()
                        prev_sc_o = t_pe

                        sub_tgts = []
                        for sub_ in range(4):
                            def tr_s(t, sub_=sub_, t_pe=t_pe,
                                     w_vx=(sub_tgts[sub_ - 2] if sub_ >= 2 else 0)):
                                if sub_ == 0:
                                    t.wait_ge(ev.h, t_pe)
                                if w_vx:
                                    t.wait_ge(vx.h, w_vx)
                                t.transpose(psA[:, 1 + sub_ % 2, 0:65],
                                            orow[0:65, sub_ * 128:(sub_ + 1) * 128],
                                            ident[0:65, 0:65]).then_inc(mm.h, 1)
                            blk.tensor(tr_s)
                            t_tr = mm.inc()

                            def vec_s(v, sub_=sub_, t_tr=t_tr):
                                v.wait_ge(mm.h, t_tr)
                                v.reciprocal(scal[:, 3 + sub_:4 + sub_],
                                             psA[:, 1 + sub_ % 2, 64:65]).then_inc(sw.h, 1)
                                v.wait_ge(sw.h, sw.inc())
                                v.tensor_scalar(orow[:, 512 + sub_ * 64:512 + (sub_ + 1) * 64],
                                                psA[:, 1 + sub_ % 2, 0:64],
                                                scal[:, 3 + sub_:4 + sub_], None,
                                                ALU.mult)
                                v.fence(vx.h)
                            blk.vector(vec_s)
                            sub_tgts.append(vx.inc())

                        def gp_o(g, hloc=hloc, qb=qb, t_eo=sub_tgts[-1]):
                            g.wait_ge(vx.h, t_eo)
                            g.dma_start(
                                out=dr["amine"][qb * 4:(qb + 1) * 4, :,
                                                hloc * 64:(hloc + 1) * 64]
                                .rearrange("s p d -> p s d"),
                                in_=orow[:, 512:768].rearrange("p (s d) -> p s d", d=64),
                            ).then_inc(sp.h, 16)
                        blk.gpsimd(gp_o)
                        sp.inc(16)
                barrier()

            def ldwo(sy, t_sp=sp.n):
                sy.wait_ge(sp.h, t_sp)
                sy.dma_start(out=wta[:, :].rearrange("p (c n) -> p c n", c=1),
                             in_=dr["woT"][0:1, :, :].rearrange("c p n -> p c n")
                             ).then_inc(ld.h, 16)
                sy.dma_start(out=wtb[:, :].rearrange("p (c n) -> p c n", c=1),
                             in_=dr["woT"][1:2, :, :].rearrange("c p n -> p c n")
                             ).then_inc(ld.h, 16)
                sy.dma_start(out=orow[:, :].rearrange("p (c n) -> p c n", c=1),
                             in_=dr["woT"][2:3, :, :].rearrange("c p n -> p c n")
                             ).then_inc(ld.h, 16)
                sy.dma_start(out=bmask[:, 0:896],
                             in_=dr["woT"][3, :, 0:896]).then_inc(ld.h, 16)
                sy.dma_start(out=ptile[:, 0, 0:128],
                             in_=dr["woT"][3, :, 896:1024]).then_inc(ld.h, 16)
            blk.sync(ldwo)
            t_wo = ld.inc(80)
            wchunk = [wta[:, :], wtb[:, :], orow[:, :], None]

            for tt in range(32):
                buf = stg if tt % 2 == 0 else stg2

                def ldf(sy, tt=tt, buf=buf, t_mm=mm.n):
                    sy.wait_ge(mm.h, t_mm)
                    sy.dma_start(out=buf[:, 0:512],
                                 in_=dr["amine"][tt, :, :]).then_inc(ld.h, 16)
                blk.sync(ldf)
                t_ld = ld.inc(16)

                for c in range(4):
                    def tens(t, tt=tt, buf=buf, c=c, t_ld=t_ld, t_ev=ev.n):
                        if c == 0:
                            t.wait_ge(ld.h, t_ld)
                        t.wait_ge(ev.h, t_ev)
                        t.transpose(psB[:, c % 2, 0:128],
                                    buf[:, c * 128:(c + 1) * 128],
                                    ident[:, :]).then_inc(mm.h, 1)
                    blk.tensor(tens)
                    t_mm = mm.inc()

                    def sc(v, buf=buf, c=c, t_mm=t_mm):
                        v.wait_ge(mm.h, t_mm)
                        v.tensor_copy(buf[:, 512 + c * 128:512 + (c + 1) * 128],
                                      psB[:, c % 2, 0:128]).then_inc(ev.h, 1)
                    blk.vector(sc)
                    ev.inc()

                def tens2(t, tt=tt, buf=buf, t_wo=t_wo, t_evp=ev.n, t_vx=vx.n):
                    t.wait_ge(ev.h, t_evp)
                    t.wait_ge(ld.h, t_wo)
                    t.wait_ge(vx.h, t_vx)
                    for c in range(4):
                        w = wchunk[c]
                        for os_ in range(2):
                            if c < 3:
                                rhs = w[:, os_ * 512:(os_ + 1) * 512]
                                t.matmul(psA[:, os_, :],
                                         buf[:, 512 + c * 128:512 + (c + 1) * 128],
                                         rhs, start=(c == 0), stop=(c == 3))
                            else:
                                if os_ == 0:
                                    t.matmul(psA[:, 0, :],
                                             buf[:, 512 + c * 128:512 + (c + 1) * 128],
                                             bmask[:, 0:512], start=False, stop=True)
                                else:
                                    t.matmul(psA[:, 1, 0:384],
                                             buf[:, 512 + c * 128:512 + (c + 1) * 128],
                                             bmask[:, 512:896], start=False, stop=False)
                                    t.matmul(psA[:, 1, 384:512],
                                             buf[:, 512 + c * 128:512 + (c + 1) * 128],
                                             ptile[:, 0, 0:128], start=False, stop=True)
                    t.fence(mm.h)
                blk.tensor(tens2)
                t_mm2 = mm.inc()

                def vec2(v, tt=tt, buf=buf, t_mm2=t_mm2, t_sp=sp.n):
                    v.wait_ge(mm.h, t_mm2)
                    v.wait_ge(sp.h, t_sp)
                    v.tensor_copy(buf[:, 1024:1536], psA[:, 0, :])
                    v.tensor_copy(buf[:, 1536:2048], psA[:, 1, :])
                    v.fence(vx.h)
                blk.vector(vec2)
                t_vy = vx.inc()

                def gp_y(g, tt=tt, buf=buf, t_vy=t_vy):
                    g.wait_ge(vx.h, t_vy)
                    g.dma_start(out=dr["rs_in"][tt * 128:(tt + 1) * 128, :],
                                in_=buf[:, 1024:2048]).then_inc(sp.h, 16)
                blk.gpsimd(gp_y)
                sp.inc(16)

            def gpr(g, t_sp=sp.n):
                g.wait_ge(sp.h, t_sp)
                g.collective_compute(
                    "ReduceScatter", ALU.add,
                    replica_groups=[[0, 1], [2, 3], [4, 5], [6, 7]],
                    ins=[dr["rs_in"].rearrange("t d -> (t d)").opt()],
                    outs=[dr["rs_out"].rearrange("t d -> (t d)").opt()],
                ).then_inc(cc.h, 1)
                g.wait_ge(cc.h, cc.inc())
                g.sem_inc(gx.h, 1)
            blk.gpsimd(gpr)
            gx.inc()
            barrier()

            for tt in range(16):
                buf = stg if tt % 2 == 0 else stg2

                def ldf2(sy, tt=tt, buf=buf, t_mm=mm.n):
                    sy.wait_ge(mm.h, t_mm)
                    sy.dma_start(out=buf[:, 0:1024],
                                 in_=dr["rs_out"][tt * 128:(tt + 1) * 128, :]
                                 ).then_inc(ld.h, 16)
                blk.sync(ldf2)
                t_ld = ld.inc(16)

                for c in range(NC_D):
                    def tens(t, buf=buf, c=c, t_ld=t_ld, t_vx=vx.n):
                        if c == 0:
                            t.wait_ge(ld.h, t_ld)
                        t.wait_ge(vx.h, t_vx)
                        t.transpose(psB[:, c % 4, 0:128],
                                    buf[:, c * 128:(c + 1) * 128],
                                    ident[:, :]).then_inc(mm.h, 1)
                    blk.tensor(tens)
                    t_mm = mm.inc()

                    def vec(v, tt=tt, c=c, t_mm=t_mm):
                        v.wait_ge(mm.h, t_mm)
                        v.scalar_tensor_tensor(xres[:, c, tt * 128:(tt + 1) * 128],
                                               psB[:, c % 4, 0:128], 1.0,
                                               xres[:, c, tt * 128:(tt + 1) * 128],
                                               ALU.mult, ALU.add)
                        v.fence(vx.h)
                    blk.vector(vec)
                    vx.inc()
            barrier()

        def store_out():
            for tt in range(16):
                for c in range(NC_D):
                    def tens(t, tt=tt, c=c, t_ev=ev.n):
                        t.wait_ge(ev.h, t_ev)
                        t.transpose(psB[:, c % 4, 0:128],
                                    xres[:, c, tt * 128:(tt + 1) * 128],
                                    ident[:, :]).then_inc(mm.h, 1)
                    blk.tensor(tens)
                    t_mm = mm.inc()

                    def sc(v, tt=tt, c=c, t_mm=t_mm, t_sp=sp.n):
                        v.wait_ge(mm.h, t_mm)
                        v.wait_ge(sp.h, t_sp)
                        v.tensor_copy(stg[:, c * 128:(c + 1) * 128],
                                      psB[:, c % 4, 0:128]).then_inc(ev.h, 1)
                    blk.vector(sc)
                    ev.inc()

                def gp(g, tt=tt, t_ev=ev.n):
                    g.wait_ge(ev.h, t_ev)
                    g.dma_start(out=out[tt * 128:(tt + 1) * 128, :],
                                in_=stg[:, 0:1024]).then_inc(sp.h, 16)
                blk.gpsimd(gp)
                sp.inc(16)

        p0_consts()
        p0_input()
        if stage >= 1:
            p0_weights()
        barrier()
        if "wuT0" in tap:
            def ldt(sy, t_mm=mm.n):
                sy.dma_start(out=stg[:, 0:3072],
                             in_=dr["wuT"][0, :, :]).then_inc(ld.h, 16)
            blk.sync(ldt)
            mktap("wuT0", stg[:, 0:3072], [128, 3072], wait=(ld, ld.inc(16)))
        if stage >= 2:
            ffn_call("f1")
        if stage >= 3:
            attention()
        if stage >= 4:
            ffn_call("f2")
        store_out()

        def fin(g, t_sp=sp.n):
            g.wait_ge(sp.h, t_sp)
        blk.gpsimd(fin)

    return nc, tap_outs


def kernel(**inputs):
    x = np.ascontiguousarray(np.asarray(inputs["L0_attn_x"], np.float32))
    wq = np.asarray(inputs["L0_attn_wq"], np.float32)
    wk = np.asarray(inputs["L0_attn_wk"], np.float32)
    wv = np.asarray(inputs["L0_attn_wv"], np.float32)
    wo = np.ascontiguousarray(np.asarray(inputs["L0_attn_wo"], np.float32))
    wg = np.ascontiguousarray(np.asarray(inputs["L0_ffn_w_gate"], np.float32))
    wu = np.ascontiguousarray(np.asarray(inputs["L0_ffn_w_up"], np.float32))
    wd = np.ascontiguousarray(np.asarray(inputs["L0_ffn_w_down"], np.float32))

    nc, _ = build_program(stage=STAGE)
    in_maps = []
    for core in range(8):
        b, h = core // 2, core % 2
        in_maps.append({
            "x": np.ascontiguousarray(x[b, h * T:(h + 1) * T, :]),
            "wq": np.ascontiguousarray(wq[h * 512:(h + 1) * 512, :]),
            "wk": np.ascontiguousarray(wk[h * 512:(h + 1) * 512, :]),
            "wv": np.ascontiguousarray(wv[h * 512:(h + 1) * 512, :]),
            "wo": np.ascontiguousarray(wo[:, h * 512:(h + 1) * 512]),
            "wg": wg, "wu": wu, "wd": wd,
        })
    res = run_bass_kernel_spmd(nc, in_maps, core_ids=list(range(8)))
    outp = np.empty((4, S, D), np.float32)
    for core in range(8):
        b, h = core // 2, core % 2
        outp[b, h * T:(h + 1) * T, :] = res.results[core]["out"]
    return outp


# version 14
# speedup vs baseline: 1.1355x; 1.0013x over previous
import numpy as np
from contextlib import ExitStack

import concourse.bass as bass
import concourse.mybir as mybir
from concourse.bass_utils import run_bass_kernel_spmd

F32 = mybir.dt.float32
F16 = mybir.dt.float16
AF = mybir.ActivationFunctionType
ALU = mybir.AluOpType
AX = mybir.AxisListType.X
AXY = mybir.AxisListType.XY

MAGIC = 12582912.0
QMAX = 32767.0
C_EXP = 14.0
EPS_LN = 1e-5
LN10K = 9.210340371976184
TWO_PI = 6.283185307179586
CW1 = 6.28125
CW2 = float(np.float32(TWO_PI - 6.28125))
INV_2PI = 0.15915494309189535
HALF_PI = float(np.float32(np.pi / 2))
PI_F = float(np.float32(np.pi))
R7A = float(np.float32(1.0 / 7.0))
R7B = float(np.float32(1.0 / 7.0 - float(np.float32(1.0 / 7.0))))
RINV = float(np.float32(np.exp(-LN10K / 32.0)))

T = 2048
D = 1024
NC_D = 8
I3 = 3072
NI = 24
S = 4096
STAGE = 4
FFN_SUB = 7
FLASH_PIPE = True


class Sem:
    def __init__(self, ctx, nc, name):
        self.h = ctx.enter_context(nc.semaphore(name))
        self.n = 0

    def inc(self, k=1):
        self.n += k
        return self.n


_SEQ_ONLY = {"wait_ge", "wait_op", "sem_inc", "nop", "maybe_wait_ge"}


class EW:

    def __init__(self, eng):
        self._e = eng
        self._last = None

    def __getattr__(self, k):
        a = getattr(self._e, k)
        if callable(a) and k not in _SEQ_ONLY and not k.startswith("_"):
            def w(*args, **kw):
                r = a(*args, **kw)
                if k not in _SEQ_ONLY:
                    self._last = r
                return r
            return w
        return a

    def wait_ge(self, sem, val):
        return self._e.wait_ge(sem, val)

    def sem_inc(self, sem, val):
        return self._e.sem_inc(sem, val)

    def fence(self, sem, n=1):
        assert self._last is not None, "fence with no prior datapath op"
        self._last.then_inc(sem, n)


def build_program(stage=STAGE, taps=()):
    nc = bass.Bass("TRN2", target_bir_lowering=False,
                   detect_race_conditions=False)
    tap = set(taps)

    x_in = nc.dram_tensor("x", [T, D], F32, kind="ExternalInput").ap()
    wq_in = nc.dram_tensor("wq", [512, D], F32, kind="ExternalInput").ap()
    wk_in = nc.dram_tensor("wk", [512, D], F32, kind="ExternalInput").ap()
    wv_in = nc.dram_tensor("wv", [512, D], F32, kind="ExternalInput").ap()
    wo_in = nc.dram_tensor("wo", [D, 512], F32, kind="ExternalInput").ap()
    wg_in = nc.dram_tensor("wg", [I3, D], F32, kind="ExternalInput").ap()
    wu_in = nc.dram_tensor("wu", [I3, D], F32, kind="ExternalInput").ap()
    wd_in = nc.dram_tensor("wd", [D, I3], F32, kind="ExternalInput").ap()
    out = nc.dram_tensor("out", [T, D], F32, kind="ExternalOutput").ap()

    tap_outs = {}

    dr = {}
    dr["wqT"] = nc.dram_tensor("wqT", [NC_D, 128, 512], F32).ap()
    dr["wkT"] = nc.dram_tensor("wkT", [NC_D, 128, 512], F32).ap()
    dr["wvT"] = nc.dram_tensor("wvT", [NC_D, 128, 512], F32).ap()
    dr["woT"] = nc.dram_tensor("woT", [4, 128, D], F32).ap()
    dr["wgT"] = nc.dram_tensor("wgT", [NC_D, 128, I3], F32).ap()
    dr["wuT"] = nc.dram_tensor("wuT", [NC_D, 128, I3], F32).ap()
    dr["wdT"] = nc.dram_tensor("wdT", [NI, 128, D], F32).ap()
    dr["up"] = nc.dram_tensor("up_sp", [NI, 128, T], F32).ap()
    dr["gate"] = nc.dram_tensor("gate_sp", [NI, 128, T], F32).ap()
    dr["qu"] = nc.dram_tensor("qu_sp", [NI, 128, T], F32).ap()
    dr["qg"] = nc.dram_tensor("qg_sp", [NI, 128, T], F32).ap()
    dr["qt"] = nc.dram_tensor("qt_sp", [NI, 128, T], F32).ap()
    dr["ag_in"] = nc.dram_tensor("ag_in", [NC_D, 128, T], F32).ap()
    dr["ag_out"] = nc.dram_tensor("ag_out", [2, NC_D, 128, T], F32).ap()
    dr["qsp"] = nc.dram_tensor("q_sp", [32, 128, 512], F32).ap()
    dr["ksp"] = nc.dram_tensor("k_sp", [32, 128, 512], F32).ap()
    dr["vsp"] = nc.dram_tensor("v_sp", [32, 128, 512], F32).ap()
    dr["amine"] = nc.dram_tensor("amine", [32, 128, 512], F32).ap()
    dr["rs_in"] = nc.dram_tensor("rs_in", [S, D], F32).ap()
    dr["rs_out"] = nc.dram_tensor("rs_out", [T, D], F32).ap()
    dr["ar_in"] = nc.dram_tensor("ar_in", [256], F32).ap()
    dr["ar_out"] = nc.dram_tensor("ar_out", [256], F32).ap()
    dr["br"] = nc.dram_tensor("br", [16], F32).ap()
    dr["row"] = nc.dram_tensor("rowb", [4096], F32).ap()

    with ExitStack() as ctx:
        e = ctx.enter_context

        xres = e(nc.sbuf_tensor("xres", [128, NC_D, T], F32))
        xlnT = e(nc.sbuf_tensor("xln", [128, NC_D * T], F32))
        stg = e(nc.sbuf_tensor("stg", [128, 4096], F32))
        stg2 = e(nc.sbuf_tensor("stg2", [128, 4096], F32))
        pool = e(nc.sbuf_tensor("pool", [128, 4096], F32))
        wta = e(nc.sbuf_tensor("wta", [128, 1024], F32))
        wtb = e(nc.sbuf_tensor("wtb", [128, 1024], F32))
        ident = e(nc.sbuf_tensor("ident", [128, 128], F32))
        cbias = e(nc.sbuf_tensor("cbias", [128, 8], F32))
        scal = e(nc.sbuf_tensor("scal", [128, 16], F32))
        sc1 = e(nc.sbuf_tensor("sc1", [1, 384], F32))
        mcol = e(nc.sbuf_tensor("mcol", [128, 224], F32))
        bscale = e(nc.sbuf_tensor("bscale", [128, 96], F32))

        xln = xlnT[:, :].rearrange("p (c t) -> p c t", c=NC_D)
        kfeat = xlnT[:, 0:4096]
        qfeat = xlnT[:, 4096:8192]
        kf16 = xlnT[:, 0:4096].bitcast(F16)
        qf16 = xlnT[:, 4096:8192].bitcast(F16)
        vbufF = xlnT[:, 8192:12352]
        vbuf = vbufF.rearrange("p (a b) -> p a b", b=130)
        bmask = xlnT[:, 12352:13248]
        orow = xlnT[:, 13248:14272]
        mubc = pool[:, 0:2048]
        rsbc = pool[:, 2048:4096]
        ropeC = pool[:, 0:1024].rearrange("p (a b) -> p a b", b=32)
        ropeS = pool[:, 1024:2048].rearrange("p (a b) -> p a b", b=32)
        ptile = pool[:, 2048:4096].rearrange("p (a b) -> p a b", b=512)

        psA = e(nc.psum_tensor("psA", [128, 4, 512], F32))
        psB = e(nc.psum_tensor("psB", [128, 4, 512], F32))

        ld = Sem(ctx, nc, "ld")
        sp = Sem(ctx, nc, "sp")
        mm = Sem(ctx, nc, "mm")
        ev = Sem(ctx, nc, "ev")
        vx = Sem(ctx, nc, "vx")
        gx = Sem(ctx, nc, "gx")
        cc = Sem(ctx, nc, "cc")
        bs = Sem(ctx, nc, "bs")
        sw = Sem(ctx, nc, "sw")
        blk0 = e(nc.Block())

        def vrecip(v, out_ap, in_ap):
            v.reciprocal(out_ap, in_ap).then_inc(sw.h, 1)
            v.wait_ge(sw.h, sw.inc())

        def ss(v):
            v._last.then_inc(sw.h, 1)
            v.wait_ge(sw.h, sw.inc())

        class BlkW:
            def __getattr__(self, name):
                def reg(f):
                    getattr(blk0, name)(lambda eng: f(EW(eng)))
                return reg
        blk = BlkW()

        def barrier():
            tgt = bs.n + 1
            bs.n = tgt

            def gpf(en, targets=(ld.n, sp.n, cc.n, mm.n, ev.n, vx.n, gx.n),
                    t=tgt):
                for sem, val in zip((ld, sp, cc, mm, ev, vx, gx), targets):
                    if val:
                        en.wait_ge(sem.h, val)
                en.sem_inc(bs.h, 1)
            blk.gpsimd(gpf)
            for name in ("sync", "vector", "scalar", "tensor", "gpsimd"):
                def f(en, t=tgt):
                    en.wait_ge(bs.h, t)
                getattr(blk, name)(f)

        def mktap(name, src_ap, shape, wait=None):
            if name not in tap:
                return
            tp = nc.dram_tensor("tap_" + name, shape, F32, kind="ExternalOutput").ap()
            tap_outs[name] = tp

            def gp(g):
                if wait is not None:
                    g.wait_ge(wait[0].h, wait[1])
                g.dma_start(out=tp, in_=src_ap).then_inc(sp.h, 16)
            blk.gpsimd(gp)
            sp.inc(16)
            barrier()

        def p0_consts():
            def vec(v):
                v.memset(cbias[:, :], 0.0)
                ss(v)
                v.memset(cbias[:, 0:1], -C_EXP)
                v.memset(cbias[:, 1:2], EPS_LN)
                v.memset(scal[:, 15:16], 1.0)
                v.memset(stg[:, 0:128], 1.0)
                v.fence(vx.h)
            blk.vector(vec)
            vx.inc()

            def gp(g, t=vx.n):
                g.wait_ge(vx.h, t)
                g.affine_select(ident[:, :], stg[:, 0:128], [[1, 128]],
                                ALU.is_equal, 0.0, base=0, channel_multiplier=-1)
                g.fence(gx.h)
            blk.gpsimd(gp)
            gx.inc()

        def p0_input():
            for tt in range(16):
                buf = stg if tt % 2 == 0 else stg2

                def ldf(sy, tt=tt, buf=buf, t_mm=mm.n):
                    sy.wait_ge(mm.h, t_mm)
                    sy.dma_start(out=buf[:, 0:1024],
                                 in_=x_in[tt * 128:(tt + 1) * 128, :]).then_inc(ld.h, 16)
                blk.sync(ldf)
                t_ld = ld.inc(16)

                for c in range(NC_D):
                    def tens(t, tt=tt, buf=buf, c=c, t_ld=t_ld, t_ev=ev.n):
                        if c == 0:
                            t.wait_ge(ld.h, t_ld)
                        t.wait_ge(ev.h, t_ev)
                        t.transpose(psB[:, c % 4, 0:128],
                                    buf[:, c * 128:(c + 1) * 128],
                                    ident[:, :]).then_inc(mm.h, 1)
                    blk.tensor(tens)
                    t_mm = mm.inc()

                    def sc(v, tt=tt, c=c, t_mm=t_mm):
                        v.wait_ge(mm.h, t_mm)
                        v.tensor_copy(xres[:, c, tt * 128:(tt + 1) * 128],
                                      psB[:, c % 4, 0:128]).then_inc(ev.h, 1)
                    blk.vector(sc)
                    ev.inc()

        def prep_weight(w_ap, rows, dst, lpbq):
            cols = w_ap.shape[1]
            ncol = cols // 128
            wrow = pool[:, 0:cols]
            for r in range(rows // 128):
                def ldf(sy, r=r, t_mm=mm.n):
                    sy.wait_ge(mm.h, t_mm)
                    sy.dma_start(out=wrow[:, :],
                                 in_=w_ap[r * 128:(r + 1) * 128, :]).then_inc(ld.h, 16)
                blk.sync(ldf)
                t_ld = ld.inc(16)

                if lpbq:
                    nb = cols // 32

                    def vec(v, t_ld=t_ld, nb=nb):
                        v.wait_ge(ld.h, t_ld)
                        wv = wrow.rearrange("p (b k) -> p b k", k=32)
                        v.tensor_reduce(bscale[:, 0:nb], wv, AX, ALU.max,
                                        apply_absolute_value=True)
                        ss(v)
                        v.tensor_scalar(mcol[:, 0:nb], bscale[:, 0:nb], R7A,
                                        None, ALU.mult)
                        ss(v)
                        v.scalar_tensor_tensor(mcol[:, 0:nb], bscale[:, 0:nb],
                                               R7B, mcol[:, 0:nb],
                                               ALU.mult, ALU.add)
                        ss(v)
                        vrecip(v, bscale[:, 0:nb], mcol[:, 0:nb])
                        v.tensor_tensor(wv, wv,
                                        bscale[:, 0:nb].unsqueeze(2)
                                        .to_broadcast([128, nb, 32]), ALU.mult)
                        ss(v)
                        v.tensor_scalar(wv, wv, MAGIC, MAGIC, ALU.add, ALU.subtract)
                        ss(v)
                        v.tensor_tensor(wv, wv,
                                        mcol[:, 0:nb].unsqueeze(2)
                                        .to_broadcast([128, nb, 32]), ALU.mult)
                        v.fence(vx.h)
                    blk.vector(vec)
                    rdy = ("vx", vx.inc())
                else:
                    rdy = ("ld", t_ld)

                for c in range(ncol):
                    def tens(t, c=c, rdy=rdy, t_ev=ev.n):
                        if c == 0:
                            t.wait_ge({"vx": vx, "ld": ld}[rdy[0]].h, rdy[1])
                        t.wait_ge(ev.h, t_ev)
                        t.transpose(psB[:, c % 4, 0:128],
                                    wrow[:, c * 128:(c + 1) * 128],
                                    ident[:, :]).then_inc(mm.h, 1)
                    blk.tensor(tens)
                    t_mm = mm.inc()

                    def sc(v, c=c, t_mm=t_mm, t_sp=sp.n):
                        v.wait_ge(mm.h, t_mm)
                        v.wait_ge(sp.h, t_sp)
                        v.tensor_copy(stg2[:, (c % 8) * 128:(c % 8) * 128 + 128],
                                      psB[:, c % 4, 0:128]).then_inc(ev.h, 1)
                    blk.vector(sc)
                    t_ev = ev.inc()

                    def gp(g, c=c, r=r, t_ev=t_ev):
                        g.wait_ge(ev.h, t_ev)
                        g.dma_start(out=dst[c, :, r * 128:(r + 1) * 128],
                                    in_=stg2[:, (c % 8) * 128:(c % 8) * 128 + 128]
                                    ).then_inc(sp.h, 16)
                    blk.gpsimd(gp)
                    sp.inc(16)

        def p0_weights():
            prep_weight(wq_in, 512, dr["wqT"], False)
            prep_weight(wk_in, 512, dr["wkT"], False)
            prep_weight(wv_in, 512, dr["wvT"], False)
            prep_weight(wo_in, D, dr["woT"], False)
            prep_weight(wu_in, I3, dr["wuT"], True)
            prep_weight(wg_in, I3, dr["wgT"], True)
            prep_weight(wd_in, D, dr["wdT"], True)

        def layer_norm(tag):
            for g4 in range(4):
                sl = slice(g4 * 512, (g4 + 1) * 512)

                def sc(v, sl=sl, t_vx=vx.n):
                    v.wait_ge(vx.h, t_vx)
                    v.tensor_tensor(stg[:, :].rearrange("p (c t) -> p c t", c=NC_D),
                                    xres[:, :, sl], xres[:, :, sl],
                                    ALU.mult).then_inc(ev.h, 1)
                blk.vector(sc)
                t_sq = ev.inc()

                def tens(t, sl=sl, t_sq=t_sq, t_vx=vx.n):
                    t.wait_ge(ev.h, t_sq)
                    t.wait_ge(vx.h, t_vx)
                    for c in range(NC_D):
                        t.matmul(psB[0:1, 0, 0:512], scal[:, 15:16],
                                 xres[:, c, sl], start=(c == 0), stop=(c == 7))
                    for c in range(NC_D):
                        t.matmul(psB[0:1, 1, 0:512], scal[:, 15:16],
                                 stg[:, c * 512:(c + 1) * 512],
                                 start=(c == 0), stop=(c == 7))
                    t.fence(mm.h)
                blk.tensor(tens)
                t_mm = mm.inc()

                def vec(v, sl=sl, t_mm=t_mm):
                    v.wait_ge(mm.h, t_mm)
                    v.tensor_scalar(mubc[0:1, sl], psB[0:1, 0, 0:512],
                                    1.0 / 1024.0, None, ALU.mult)
                    v.tensor_scalar(rsbc[0:1, sl],
                                    psB[0:1, 1, 0:512], 1.0 / 1024.0, None, ALU.mult)
                    ss(v)
                    v.tensor_tensor(stg2[0:1, 0:512], mubc[0:1, sl], mubc[0:1, sl],
                                    ALU.mult)
                    ss(v)
                    v.tensor_tensor(rsbc[0:1, sl], rsbc[0:1, sl], stg2[0:1, 0:512],
                                    ALU.subtract)
                    v.fence(vx.h)
                blk.vector(vec)
                t_vx = vx.inc()

                def sc2(s, sl=sl, t_vx=t_vx):
                    s.wait_ge(vx.h, t_vx)
                    s.activation(rsbc[0:1, sl], rsbc[0:1, sl], AF.Ln,
                                 bias=cbias[0:1, 1:2], scale=1.0).then_inc(sw.h, 1)
                    s.wait_ge(sw.h, sw.inc())
                    s.activation(rsbc[0:1, sl], rsbc[0:1, sl], AF.Exp,
                                 bias=cbias[0:1, 2:3], scale=-0.5).then_inc(ev.h, 1)
                blk.scalar(sc2)
                t_ev = ev.inc()

                def vec2(v, sl=sl, t_ev=t_ev):
                    v.wait_ge(ev.h, t_ev)
                    v.fence(vx.h) if False else v.tensor_copy(
                        sc1[0:1, 380:381], rsbc[0:1, sl.start:sl.start + 1])
                    v.fence(vx.h)
                blk.vector(vec2)
                vx.inc()

            def gp(g, t_vx=vx.n):
                g.wait_ge(vx.h, t_vx)
                g.dma_start(out=dr["row"][0:2048], in_=mubc[0:1, :]).then_inc(sp.h, 16)
                g.dma_start(out=dr["row"][2048:4096], in_=rsbc[0:1, :]).then_inc(sp.h, 16)
                g.wait_ge(sp.h, sp.inc(32))
                g.dma_start(out=mubc[:, :],
                            in_=dr["row"][0:2048].unsqueeze(0).to_broadcast([128, 2048])).then_inc(sp.h, 16)
                g.dma_start(out=rsbc[:, :],
                            in_=dr["row"][2048:4096].unsqueeze(0).to_broadcast([128, 2048])).then_inc(sp.h, 16)
                g.wait_ge(sp.h, sp.inc(32))
                g.sem_inc(gx.h, 1)
            blk.gpsimd(gp)
            gx.inc()

            def vec3(v, t_gx=gx.n):
                v.wait_ge(gx.h, t_gx)
                for c in range(NC_D):
                    v.tensor_tensor(xln[:, c, :], xres[:, c, :], mubc[:, :],
                                    ALU.subtract)
                    ss(v)
                    v.tensor_tensor(xln[:, c, :], xln[:, c, :], rsbc[:, :],
                                    ALU.mult)
                v.fence(vx.h)
            blk.vector(vec3)
            vx.inc()
            mktap("xln_" + tag, xlnT[:, :], [128, NC_D * T], wait=(vx, vx.n))

        def allreduce_max(n_vals, compute_scalars, nbc):
            def gp(g, t_vx=vx.n, n_vals=n_vals):
                g.wait_ge(vx.h, t_vx)
                for j in range(n_vals):
                    g.dma_start(out=dr["ar_in"][j * 128:(j + 1) * 128],
                                in_=mcol[:, 200 + j:201 + j]).then_inc(sp.h, 16)
                g.wait_ge(sp.h, sp.inc(16 * n_vals))
                g.collective_compute(
                    "AllReduce", ALU.max,
                    replica_groups=[[0, 1, 2, 3, 4, 5, 6, 7]],
                    ins=[dr["ar_in"][0:n_vals * 128].opt()],
                    outs=[dr["ar_out"][0:n_vals * 128].opt()],
                ).then_inc(cc.h, 1)
                g.wait_ge(cc.h, cc.inc())
                g.dma_start(out=sc1[0:1, 0:n_vals * 128],
                            in_=dr["ar_out"][0:n_vals * 128]).then_inc(sp.h, 16)
                g.wait_ge(sp.h, sp.inc(16))
                g.sem_inc(gx.h, 1)
            blk.gpsimd(gp)
            gx.inc()

            def vec(v, t_gx=gx.n, n_vals=n_vals):
                v.wait_ge(gx.h, t_gx)
                for j in range(n_vals):
                    r = v.tensor_reduce(sc1[0:1, 320 + j:321 + j],
                                        sc1[0:1, j * 128:(j + 1) * 128], AX, ALU.max)
                r.then_inc(sw.h, 1)
                v.wait_ge(sw.h, sw.inc())
                compute_scalars(v)
                v.fence(vx.h)
            blk.vector(vec)
            vx.inc()

            def gp2(g, t_vx=vx.n, nbc=nbc):
                g.wait_ge(vx.h, t_vx)
                g.dma_start(out=dr["br"][0:nbc], in_=sc1[0:1, 340:340 + nbc]).then_inc(sp.h, 16)
                g.wait_ge(sp.h, sp.inc(16))
                g.dma_start(out=scal[:, 0:nbc],
                            in_=dr["br"][0:nbc].unsqueeze(0).to_broadcast([128, nbc])).then_inc(sp.h, 16)
                g.wait_ge(sp.h, sp.inc(16))
                g.sem_inc(gx.h, 1)
            blk.gpsimd(gp2)
            gx.inc()
            barrier()

        def ffn_call(tag):
            sub = FFN_SUB
            layer_norm(tag)
            if sub < 2:
                return

            def vecm(v, t_vx=vx.n):
                v.wait_ge(vx.h, t_vx)
                v.tensor_reduce(mcol[:, 200:201],
                                xlnT[:, :], AX, ALU.max,
                                apply_absolute_value=True)
                v.fence(vx.h)
            blk.vector(vecm)
            vx.inc()

            def scalars1(v):
                vrecip(v, sc1[0:1, 340:341], sc1[0:1, 340:341] if False else sc1[0:1, 320:321])
                v.tensor_scalar(sc1[0:1, 340:341], sc1[0:1, 340:341], QMAX, None,
                                ALU.mult)
                ss(v)
                v.tensor_scalar(sc1[0:1, 300:301], sc1[0:1, 320:321], 1.0 / QMAX,
                                None, ALU.mult)
            allreduce_max(1, scalars1, 1)
            mktap("sc1_" + tag, sc1[0:1, :], [1, 384], wait=(gx, gx.n))
            mktap("scal_" + tag, scal[:, :], [128, 16], wait=(gx, gx.n))
            mktap("mcol_" + tag, mcol[:, :], [128, 224], wait=(gx, gx.n))
            if sub < 3:
                return

            def vecq(v):
                v.tensor_scalar(xln[:, :, :], xln[:, :, :], scal[:, 0:1], MAGIC,
                                ALU.mult, ALU.add)
                ss(v)
                v.tensor_scalar(xln[:, :, :], xln[:, :, :], MAGIC, None,
                                ALU.subtract)
                v.fence(vx.h)
            blk.vector(vecq)
            vx.inc()
            mktap("qx_" + tag, xlnT[:, :], [128, NC_D * T], wait=(vx, vx.n))
            barrier()
            if sub < 4:
                return

            bT = {"mm": {}, "ev": {}, "vx": {}, "sp": {}}
            step = [0]
            for which, wsrc, spdst, mbase in (("u", dr["wuT"], dr["up"], 8),
                                              ("g", dr["wgT"], dr["gate"], 40)):
                for i in range(NI):
                    k_ = step[0]
                    par = k_ % 2
                    wbuf = wta if par == 0 else wtb
                    ps = psA if par == 0 else psB
                    sbuf = stg if par == 0 else stg2

                    def ldf(sy, i=i, wbuf=wbuf, wsrc=wsrc,
                            t_mm=bT["mm"].get(k_ - 2, 0)):
                        if t_mm:
                            sy.wait_ge(mm.h, t_mm)
                        sy.dma_start(out=wbuf[:, :].rearrange("p (c n) -> p c n", c=NC_D),
                                     in_=wsrc[:, :, i * 128:(i + 1) * 128]
                                     .rearrange("c p n -> p c n")).then_inc(ld.h, 16)
                    blk.sync(ldf)
                    t_ld = ld.inc(16)

                    def tens(t, wbuf=wbuf, ps=ps, t_ld=t_ld,
                             t_ev=bT["ev"].get(k_ - 2, 0)):
                        t.wait_ge(ld.h, t_ld)
                        if t_ev:
                            t.wait_ge(ev.h, t_ev)
                        for c in range(NC_D):
                            for tb in range(4):
                                t.matmul(ps[:, tb, :], wbuf[:, c * 128:(c + 1) * 128],
                                         xln[:, c, tb * 512:(tb + 1) * 512],
                                         start=(c == 0), stop=(c == 7))
                        t.fence(mm.h)
                    blk.tensor(tens)
                    t_mm = mm.inc()
                    bT["mm"][k_] = t_mm

                    def sc(v, ps=ps, sbuf=sbuf, t_mm=t_mm,
                           t_sp=bT["sp"].get(k_ - 2, 0),
                           t_vx=bT["vx"].get(k_ - 2, 0)):
                        v.wait_ge(mm.h, t_mm)
                        if t_sp:
                            v.wait_ge(sp.h, t_sp)
                        if t_vx:
                            v.wait_ge(vx.h, t_vx)
                        v.tensor_copy(sbuf[:, 0:2048],
                                      ps[:, :, :].rearrange("p a b -> p (a b)")
                                      ).then_inc(ev.h, 1)
                    blk.vector(sc)
                    t_ev = ev.inc()
                    bT["ev"][k_] = t_ev

                    def vecr(v, i=i, sbuf=sbuf, mbase=mbase, t_ev=t_ev):
                        v.wait_ge(ev.h, t_ev)
                        v.tensor_reduce(mcol[:, mbase + i:mbase + i + 1],
                                        sbuf[:, 0:2048], AX, ALU.max,
                                        apply_absolute_value=True)
                        v.fence(vx.h)
                    blk.vector(vecr)
                    bT["vx"][k_] = vx.inc()

                    def gp(g, i=i, sbuf=sbuf, spdst=spdst, t_ev=t_ev):
                        g.wait_ge(ev.h, t_ev)
                        g.dma_start(out=spdst[i, :, :],
                                    in_=sbuf[:, 0:2048]).then_inc(sp.h, 16)
                    blk.gpsimd(gp)
                    bT["sp"][k_] = sp.inc(16)
                    step[0] += 1

                def vecf(v, mbase=mbase, which=which, t_vx=vx.n):
                    v.wait_ge(vx.h, t_vx)
                    col = 200 if which == "u" else 201
                    v.tensor_reduce(mcol[:, col:col + 1],
                                    mcol[:, mbase:mbase + NI], AX, ALU.max)
                    v.fence(vx.h)
                blk.vector(vecf)
                vx.inc()

            def scalars2(v):
                v.reciprocal(sc1[0:1, 340:341], sc1[0:1, 320:321])
                vrecip(v, sc1[0:1, 341:342], sc1[0:1, 321:322])
                v.tensor_scalar(sc1[0:1, 340:342], sc1[0:1, 340:342], QMAX, None,
                                ALU.mult)
                ss(v)
                v.tensor_scalar(sc1[0:1, 301:303], sc1[0:1, 320:322], 1.0 / QMAX,
                                None, ALU.mult)
                ss(v)
                v.tensor_tensor(sc1[0:1, 301:303], sc1[0:1, 301:303],
                                sc1[0:1, 300:301].to_broadcast([1, 2]), ALU.mult)
                ss(v)
                v.tensor_copy(sc1[0:1, 342:343], sc1[0:1, 302:303])
                ss(v)
                v.tensor_scalar(sc1[0:1, 344:345], sc1[0:1, 302:303], -1.0, None,
                                ALU.mult)
            allreduce_max(2, scalars2, 5)
            if ("upE_" + tag) in tap:
                def ldt(sy, t_mm=mm.n):
                    sy.dma_start(out=stg[:, 0:2048],
                                 in_=dr["up"][0, :, :]).then_inc(ld.h, 16)
                blk.sync(ldt)
                mktap("upE_" + tag, stg[:, 0:2048], [128, 2048],
                      wait=(ld, ld.inc(16)))
            if sub < 5:
                return

            sgb = pool[:, 0:2048]
            cT = {"vx": {}, "sp": {}}
            for i in range(NI):
                b0 = stg if i % 2 == 0 else stg2

                def ldf(sy, i=i, b0=b0, t_vx=cT["vx"].get(i - 2, 0),
                        t_sp=cT["sp"].get(i - 2, 0)):
                    if t_vx:
                        sy.wait_ge(vx.h, t_vx)
                    if t_sp:
                        sy.wait_ge(sp.h, t_sp)
                    sy.dma_start(out=b0[:, 0:2048], in_=dr["up"][i, :, :]).then_inc(ld.h, 16)
                    sy.dma_start(out=b0[:, 2048:4096], in_=dr["gate"][i, :, :]).then_inc(ld.h, 16)
                blk.sync(ldf)
                t_ld = ld.inc(32)

                def vec(v, b0=b0, t_ld=t_ld):
                    v.wait_ge(ld.h, t_ld)
                    v.tensor_scalar(b0[:, 0:2048], b0[:, 0:2048], scal[:, 0:1],
                                    MAGIC, ALU.mult, ALU.add)
                    ss(v)
                    v.tensor_scalar(b0[:, 0:2048], b0[:, 0:2048], MAGIC, None,
                                    ALU.subtract)
                    v.tensor_scalar(b0[:, 2048:4096], b0[:, 2048:4096], scal[:, 1:2],
                                    MAGIC, ALU.mult, ALU.add)
                    ss(v)
                    v.tensor_scalar(b0[:, 2048:4096], b0[:, 2048:4096], MAGIC, None,
                                    ALU.subtract)
                    v.fence(vx.h)
                blk.vector(vec)
                t_q = vx.inc()

                def gp(g, i=i, b0=b0, t_q=t_q):
                    g.wait_ge(vx.h, t_q)
                    g.dma_start(out=dr["qu"][i, :, :], in_=b0[:, 0:2048]).then_inc(sp.h, 16)
                    g.dma_start(out=dr["qg"][i, :, :], in_=b0[:, 2048:4096]).then_inc(sp.h, 16)
                blk.gpsimd(gp)
                cT["sp"][i] = sp.inc(32)

                def sc(s, b0=b0, t_q=t_q, t_vx2=vx.n):
                    s.wait_ge(vx.h, t_q)
                    s.activation(sgb, b0[:, 2048:4096], AF.Sigmoid,
                                 bias=cbias[:, 2:3], scale=scal[:, 2:3]).then_inc(ev.h, 1)
                blk.scalar(sc)
                t_sg = ev.inc()

                def vec2(v, i=i, b0=b0, t_sg=t_sg):
                    v.wait_ge(ev.h, t_sg)
                    v.tensor_scalar(sgb, sgb, 65536.0, MAGIC, ALU.mult, ALU.add)
                    ss(v)
                    v.tensor_scalar(sgb, sgb, MAGIC, None, ALU.subtract)
                    ss(v)
                    v.tensor_tensor(sgb, b0[:, 2048:4096], sgb, ALU.mult)
                    ss(v)
                    v.tensor_reduce(mcol[:, 72 + i:73 + i], sgb, AX, ALU.max,
                                    apply_absolute_value=True)
                    v.fence(vx.h)
                blk.vector(vec2)
                cT["vx"][i] = vx.inc()

            def vecf2(v, t_vx=vx.n):
                v.wait_ge(vx.h, t_vx)
                v.tensor_reduce(mcol[:, 200:201], mcol[:, 72:72 + NI], AX, ALU.max)
                v.fence(vx.h)
            blk.vector(vecf2)
            vx.inc()

            def scalars3(v):
                vrecip(v, sc1[0:1, 340:341], sc1[0:1, 320:321])
                v.tensor_scalar(sc1[0:1, 340:341], sc1[0:1, 340:341], QMAX, None,
                                ALU.mult)
                ss(v)
                v.tensor_scalar(sc1[0:1, 303:304], sc1[0:1, 320:321],
                                1.0 / 65536.0 / QMAX, None, ALU.mult)
                ss(v)
                v.tensor_tensor(sc1[0:1, 303:304], sc1[0:1, 303:304],
                                sc1[0:1, 302:303], ALU.mult)
                ss(v)
                v.tensor_copy(sc1[0:1, 341:342], sc1[0:1, 302:303])
                ss(v)
                v.tensor_scalar(sc1[0:1, 344:345], sc1[0:1, 302:303], -1.0, None,
                                ALU.mult)
            allreduce_max(1, scalars3, 5)
            if sub < 6:
                return

            dT = {}
            for i in range(NI):
                b0 = stg if i % 2 == 0 else stg2

                def ldf(sy, i=i, b0=b0, t_vx=dT.get(i - 2, 0)):
                    if t_vx:
                        sy.wait_ge(vx.h, t_vx)
                    sy.dma_start(out=b0[:, 0:2048], in_=dr["qu"][i, :, :]).then_inc(ld.h, 16)
                    sy.dma_start(out=b0[:, 2048:4096], in_=dr["qg"][i, :, :]).then_inc(ld.h, 16)
                blk.sync(ldf)
                t_ld = ld.inc(32)

                def sc(s, b0=b0, t_ld=t_ld, t_sp=sp.n, t_vx=vx.n):
                    s.wait_ge(ld.h, t_ld)
                    s.wait_ge(sp.h, t_sp)
                    s.wait_ge(vx.h, t_vx)
                    s.activation(sgb, b0[:, 2048:4096], AF.Sigmoid,
                                 bias=cbias[:, 2:3], scale=scal[:, 1:2]).then_inc(ev.h, 1)
                blk.scalar(sc)
                t_sg = ev.inc()

                def vec(v, i=i, b0=b0, t_sg=t_sg):
                    v.wait_ge(ev.h, t_sg)
                    v.tensor_scalar(sgb, sgb, 65536.0, MAGIC, ALU.mult, ALU.add)
                    v.tensor_scalar(sgb, sgb, MAGIC, None, ALU.subtract)
                    v.tensor_tensor(sgb, b0[:, 2048:4096], sgb, ALU.mult)
                    v.tensor_scalar(sgb, sgb, scal[:, 0:1], MAGIC, ALU.mult, ALU.add)
                    v.tensor_scalar(sgb, sgb, MAGIC, None, ALU.subtract)
                    v.tensor_tensor(b0[:, 2048:4096], sgb, b0[:, 0:2048], ALU.mult)
                    v.tensor_reduce(mcol[:, 104 + i:105 + i], b0[:, 2048:4096],
                                    AX, ALU.max, apply_absolute_value=True)
                    v.fence(vx.h)
                blk.vector(vec)
                t_q = vx.inc()
                dT[i] = t_q

                def gp(g, i=i, t_q=t_q):
                    g.wait_ge(vx.h, t_q)
                    g.dma_start(out=dr["qt"][i, :, :], in_=sgb).then_inc(sp.h, 16)
                blk.gpsimd(gp)
                sp.inc(16)

            def vecf3(v, t_vx=vx.n):
                v.wait_ge(vx.h, t_vx)
                v.tensor_reduce(mcol[:, 200:201], mcol[:, 104:104 + NI], AX, ALU.max)
                v.fence(vx.h)
            blk.vector(vecf3)
            vx.inc()

            def scalars4(v):
                vrecip(v, sc1[0:1, 340:341], sc1[0:1, 320:321])
                v.tensor_scalar(sc1[0:1, 340:341], sc1[0:1, 340:341], QMAX, None,
                                ALU.mult)
                ss(v)
                v.tensor_scalar(sc1[0:1, 304:305], sc1[0:1, 320:321], 1.0 / QMAX,
                                None, ALU.mult)
                ss(v)
                v.tensor_tensor(sc1[0:1, 304:305], sc1[0:1, 304:305],
                                sc1[0:1, 303:304], ALU.mult)
                ss(v)
                v.tensor_tensor(sc1[0:1, 304:305], sc1[0:1, 304:305],
                                sc1[0:1, 301:302], ALU.mult)
                ss(v)
                v.tensor_scalar(sc1[0:1, 341:342], sc1[0:1, 304:305], 0.5, None,
                                ALU.mult)
            allreduce_max(1, scalars4, 2)
            if sub < 7:
                return

            eT = {}
            for tb in range(4):
                for k in range(NI):
                    b0 = stg if k % 2 == 0 else stg2
                    ks_ = tb * NI + k

                    def ldf(sy, tb=tb, k=k, b0=b0, t_mm=eT.get(ks_ - 2, 0)):
                        if t_mm:
                            sy.wait_ge(mm.h, t_mm)
                        sy.dma_start(out=b0[:, 0:512],
                                     in_=dr["qt"][k, :, tb * 512:(tb + 1) * 512]).then_inc(ld.h, 16)
                        sy.dma_start(out=b0[:, 512:1024],
                                     in_=dr["qu"][k, :, tb * 512:(tb + 1) * 512]).then_inc(ld.h, 16)
                        sy.dma_start(out=b0[:, 1024:2048],
                                     in_=dr["wdT"][k, :, :]).then_inc(ld.h, 16)
                    blk.sync(ldf)
                    t_ld = ld.inc(48)

                    def vec(v, b0=b0, t_ld=t_ld):
                        v.wait_ge(ld.h, t_ld)
                        v.tensor_tensor(b0[:, 0:512], b0[:, 0:512], b0[:, 512:1024],
                                        ALU.mult)
                        v.tensor_scalar(b0[:, 0:512], b0[:, 0:512], scal[:, 0:1],
                                        MAGIC, ALU.mult, ALU.add)
                        v.tensor_scalar(b0[:, 0:512], b0[:, 0:512], MAGIC, None,
                                        ALU.subtract)
                        v.fence(vx.h)
                    blk.vector(vec)
                    t_oq = vx.inc()

                    def tens(t, tb=tb, k=k, b0=b0, t_oq=t_oq, t_vx=vx.n):
                        t.wait_ge(vx.h, t_oq)
                        if k == 0:
                            t.wait_ge(vx.h, t_vx)
                        for dti in range(8):
                            ps = psA if dti < 4 else psB
                            t.matmul(ps[:, dti % 4, :],
                                     b0[:, 1024 + dti * 128:1024 + (dti + 1) * 128],
                                     b0[:, 0:512], start=(k == 0), stop=(k == NI - 1))
                        t.fence(mm.h)
                    blk.tensor(tens)
                    eT[ks_] = mm.inc()

                def vecu(v, tb=tb, t_mm=mm.n):
                    v.wait_ge(mm.h, t_mm)
                    for dti in range(8):
                        ps = psA if dti < 4 else psB
                        v.scalar_tensor_tensor(
                            xres[:, dti, tb * 512:(tb + 1) * 512],
                            ps[:, dti % 4, :], scal[:, 1:2],
                            xres[:, dti, tb * 512:(tb + 1) * 512],
                            ALU.mult, ALU.add)
                    v.fence(vx.h)
                blk.vector(vecu)
                vx.inc()
            barrier()

            for nm in ("up", "gate", "qu", "qg", "qt", "up2"):
                tnm = nm + "0_" + tag
                if tnm in tap:
                    src = "up" if nm == "up2" else nm
                    def ldt(sy, src=src, t_mm=mm.n):
                        sy.dma_start(out=stg[:, 0:2048],
                                     in_=dr[src][0, :, :]).then_inc(ld.h, 16)
                    blk.sync(ldt)
                    mktap(tnm, stg[:, 0:2048], [128, 2048], wait=(ld, ld.inc(16)))

        def attention():
            layer_norm("a")
            def gp(g, t_vx=vx.n):
                g.wait_ge(vx.h, t_vx)
                g.dma_start(out=dr["ag_in"].rearrange("c p t -> p c t"),
                            in_=xln[:, :, :]).then_inc(sp.h, 16)
                g.wait_ge(sp.h, sp.inc(16))
                g.collective_compute(
                    "AllGather", ALU.bypass,
                    replica_groups=[[0, 1], [2, 3], [4, 5], [6, 7]],
                    ins=[dr["ag_in"].rearrange("c p t -> (c p t)").opt()],
                    outs=[dr["ag_out"].rearrange("h c p t -> (h c p t)").opt()],
                ).then_inc(cc.h, 1)
                g.wait_ge(cc.h, cc.inc())
                g.sem_inc(gx.h, 1)
            blk.gpsimd(gp)
            gx.inc()
            barrier()

            def gpi(g):
                g.iota(ropeC[:, :, 0:1], [[128, 32], [0, 1]], base=0,
                       channel_multiplier=1, allow_small_or_imprecise_dtypes=True)
                g.fence(gx.h)
            blk.gpsimd(gpi)
            gx.inc()

            def veci(v, t_gx=gx.n):
                v.wait_ge(gx.h, t_gx)
                v.memset(sc1[0:1, 256:257], 1.0)
                ss(v)
                for i in range(1, 32):
                    v.tensor_scalar(sc1[0:1, 256 + i:257 + i], sc1[0:1, 255 + i:256 + i],
                                    RINV, None, ALU.mult)
                    ss(v)
                v.tensor_copy(sc1[0:1, 379:380], sc1[0:1, 256:257])
                v.fence(vx.h)
            blk.vector(veci)
            vx.inc()

            def gpb(g, t_vx=vx.n):
                g.wait_ge(vx.h, t_vx)
                g.dma_start(out=dr["br"][0:16] if False else dr["ar_in"][0:32],
                            in_=sc1[0:1, 256:288]).then_inc(sp.h, 16)
                g.wait_ge(sp.h, sp.inc(16))
                g.dma_start(out=stg[:, 0:32],
                            in_=dr["ar_in"][0:32].unsqueeze(0).to_broadcast([128, 32])).then_inc(sp.h, 16)
                g.wait_ge(sp.h, sp.inc(16))
                g.sem_inc(gx.h, 1)
            blk.gpsimd(gpb)
            gx.inc()

            def vecr(v, t_gx=gx.n):
                v.wait_ge(gx.h, t_gx)
                for j in range(32):
                    v.tensor_tensor(ropeS[:, j, :],
                                    ropeC[:, j, 0:1].to_broadcast([128, 32]),
                                    stg[:, 0:32], ALU.mult)
                ss(v)
                v.tensor_scalar(ropeC[:, :, :], ropeS[:, :, :], INV_2PI, MAGIC,
                                ALU.mult, ALU.add)
                ss(v)
                v.tensor_scalar(ropeC[:, :, :], ropeC[:, :, :], MAGIC, None,
                                ALU.subtract)
                ss(v)
                v.scalar_tensor_tensor(ropeS[:, :, :], ropeC[:, :, :], -CW1,
                                       ropeS[:, :, :], ALU.mult, ALU.add)
                ss(v)
                v.scalar_tensor_tensor(ropeS[:, :, :], ropeC[:, :, :], -CW2,
                                       ropeS[:, :, :], ALU.mult, ALU.add)
                ss(v)
                v.tensor_scalar(ropeS[:, :, :], ropeS[:, :, :], PI_F, -PI_F,
                                ALU.min, ALU.max)
                v.fence(vx.h)
            blk.vector(vecr)
            vx.inc()

            def scs(s, t_vx=vx.n):
                s.wait_ge(vx.h, t_vx)
                s.activation(stg2[:, 0:1024].rearrange("p (a b) -> p a b", b=32),
                             ropeS[:, :, :], AF.Sin, bias=cbias[:, 2:3],
                             scale=1.0).then_inc(ev.h, 1)
            blk.scalar(scs)
            ev.inc()

            def vecc(v, t_ev=ev.n):
                v.wait_ge(ev.h, t_ev)
                v.tensor_scalar(ropeS[:, :, :], ropeS[:, :, :], HALF_PI, None,
                                ALU.add)
                ss(v)
                v.tensor_scalar(stg[:, 32:1056].rearrange("p (a b) -> p a b", b=32),
                                ropeS[:, :, :], PI_F, TWO_PI, ALU.is_gt, ALU.mult)
                ss(v)
                v.tensor_tensor(ropeS[:, :, :], ropeS[:, :, :],
                                stg[:, 32:1056].rearrange("p (a b) -> p a b", b=32),
                                ALU.subtract)
                ss(v)
                v.tensor_scalar(ropeS[:, :, :], ropeS[:, :, :], PI_F, -PI_F,
                                ALU.min, ALU.max)
                v.fence(vx.h)
            blk.vector(vecc)
            vx.inc()

            def scc(s, t_vx=vx.n):
                s.wait_ge(vx.h, t_vx)
                s.activation(ropeC[:, :, :], ropeS[:, :, :], AF.Sin,
                             bias=cbias[:, 2:3], scale=1.0).then_inc(ev.h, 1)
            blk.scalar(scc)
            ev.inc()

            def vecc2(v, t_ev=ev.n):
                v.wait_ge(ev.h, t_ev)
                v.tensor_copy(ropeS[:, :, :],
                              stg2[:, 0:1024].rearrange("p (a b) -> p a b", b=32))
                v.fence(vx.h)
            blk.vector(vecc2)
            vx.inc()
            barrier()

            def ldw(sy):
                sy.dma_start(out=kfeat.rearrange("p (c n) -> p c n", c=NC_D),
                             in_=dr["wqT"].rearrange("c p n -> p c n")).then_inc(ld.h, 16)
                sy.dma_start(out=qfeat.rearrange("p (c n) -> p c n", c=NC_D),
                             in_=dr["wkT"].rearrange("c p n -> p c n")).then_inc(ld.h, 16)
                sy.dma_start(out=vbufF[:, 0:4096].rearrange("p (c n) -> p c n", c=NC_D),
                             in_=dr["wvT"].rearrange("c p n -> p c n")).then_inc(ld.h, 16)
            blk.sync(ldw)
            t_w = ld.inc(48)

            qT = {"mm": {}, "ev": {}}
            for tt in range(32):
                xa = stg if tt % 2 == 0 else stg2
                pq = psA if tt % 2 == 0 else psB
                half, lt = tt // 16, tt % 16

                def ldf(sy, xa=xa, half=half, lt=lt, t_mm=mm.n):
                    sy.wait_ge(mm.h, t_mm)
                    sy.dma_start(out=xa[:, 0:1024].rearrange("p (c t) -> p c t", c=NC_D),
                                 in_=dr["ag_out"][half, :, :, lt * 128:(lt + 1) * 128]
                                 .rearrange("c p t -> p c t")).then_inc(ld.h, 16)
                blk.sync(ldf)
                t_ld = ld.inc(16)

                def tens(t, xa=xa, pq=pq, t_ld=t_ld, t_w=t_w, t_ev=ev.n):
                    t.wait_ge(ld.h, max(t_ld, t_w))
                    t.wait_ge(ev.h, t_ev)
                    for c in range(NC_D):
                        t.matmul(pq[:, 0, :], xa[:, c * 128:(c + 1) * 128],
                                 kfeat[:, c * 512:(c + 1) * 512],
                                 start=(c == 0), stop=(c == 7))
                        t.matmul(pq[:, 1, :], xa[:, c * 128:(c + 1) * 128],
                                 qfeat[:, c * 512:(c + 1) * 512],
                                 start=(c == 0), stop=(c == 7))
                        t.matmul(pq[:, 2, :], xa[:, c * 128:(c + 1) * 128],
                                 vbufF[:, c * 512:(c + 1) * 512],
                                 start=(c == 0), stop=(c == 7))
                    t.fence(mm.h)
                blk.tensor(tens)
                t_mm = mm.inc()
                qT["mm"][tt] = t_mm

                def sc(v, pq=pq, t_mm=t_mm, t_sp=sp.n):
                    v.wait_ge(mm.h, t_mm)
                    v.wait_ge(sp.h, t_sp)
                    v.tensor_copy(ptile[:, 0, :], pq[:, 0, :])
                    v.tensor_copy(ptile[:, 1, :], pq[:, 1, :])
                    v.tensor_copy(ptile[:, 2, :], pq[:, 2, :]).then_inc(ev.h, 1)
                blk.vector(sc)
                t_ev = ev.inc()
                qT["ev"][tt] = t_ev

                def vec(v, tt=tt, t_ev=t_ev):
                    v.wait_ge(ev.h, t_ev)
                    co = ropeC[:, tt, :]
                    si = ropeS[:, tt, :]
                    for which in (0, 1):
                        q4 = ptile[:, which, :].rearrange("p (h f two) -> p h f two",
                                                          h=8, two=2)
                        r4 = ptile[:, 3, :].rearrange("p (h f two) -> p h f two",
                                                      h=8, two=2)
                        for h in range(8):
                            v.scalar_tensor_tensor(r4[:, h, :, 0], q4[:, h, :, 1],
                                                   -1.0, si, ALU.mult, ALU.mult)
                            v.tensor_tensor(r4[:, h, :, 1], q4[:, h, :, 0], si,
                                            ALU.mult)
                            ss(v)
                            v.tensor_tensor(q4[:, h, :, :], q4[:, h, :, :],
                                            co.unsqueeze(2).to_broadcast([128, 32, 2]),
                                            ALU.mult)
                        ss(v)
                        v.tensor_tensor(ptile[:, which, :], ptile[:, which, :],
                                        ptile[:, 3, :], ALU.add)
                        if which == 0:
                            ss(v)
                    v.fence(vx.h)
                blk.vector(vec)
                t_vx = vx.inc()

                def gp2(g, tt=tt, t_vx=t_vx):
                    g.wait_ge(vx.h, t_vx)
                    g.dma_start(out=dr["qsp"][tt, :, :], in_=ptile[:, 0, :]).then_inc(sp.h, 16)
                    g.dma_start(out=dr["ksp"][tt, :, :], in_=ptile[:, 1, :]).then_inc(sp.h, 16)
                    g.dma_start(out=dr["vsp"][tt, :, :], in_=ptile[:, 2, :]).then_inc(sp.h, 16)
                blk.gpsimd(gp2)
                sp.inc(48)
            barrier()

            def vecb(v):
                v.memset(bmask[:, :], 1.0)
                v.memset(vbuf[:, :, 64:65], 1.0)
                v.memset(vbuf[:, :, 129:130], 1.0)
                v.fence(vx.h)
            blk.vector(vecb)
            vx.inc()

            def gpm(g, t_vx=vx.n):
                g.wait_ge(vx.h, t_vx)
                g.affine_select(bmask[:, :], bmask[:, :], [[1, 896]], ALU.is_ge,
                                0.0, base=-384, channel_multiplier=-1)
                g.fence(gx.h)
            blk.gpsimd(gpm)
            gx.inc()
            barrier()

            for pair in range(4):
                for tt in range(32):
                    def ldf(sy, tt=tt, pair=pair, t_mm=mm.n, t_vx=vx.n):
                        sy.wait_ge(mm.h, t_mm)
                        sy.wait_ge(vx.h, t_vx)
                        sy.dma_start(out=stg[:, (tt % 2) * 128:(tt % 2) * 128 + 128],
                                     in_=dr["ksp"][tt, :, pair * 128:(pair + 1) * 128]
                                     ).then_inc(ld.h, 16)
                        sy.dma_start(out=stg[:, 256 + (tt % 2) * 128:256 + (tt % 2) * 128 + 128],
                                     in_=dr["qsp"][tt, :, pair * 128:(pair + 1) * 128]
                                     ).then_inc(ld.h, 16)
                        sy.dma_start(out=stg2[:, (tt % 4) * 128:(tt % 4) * 128 + 128],
                                     in_=dr["vsp"][tt, :, pair * 128:(pair + 1) * 128]
                                     ).then_inc(ld.h, 16)
                    blk.sync(ldf)
                    t_ld = ld.inc(48)

                    def tens(t, tt=tt, t_ld=t_ld, t_ev=ev.n):
                        t.wait_ge(ld.h, t_ld)
                        t.wait_ge(ev.h, t_ev)
                        t.transpose(psB[:, tt % 2, 0:128],
                                    stg[:, (tt % 2) * 128:(tt % 2) * 128 + 128],
                                    ident[:, :])
                        t.transpose(psB[:, 2 + tt % 2, 0:128],
                                    stg[:, 256 + (tt % 2) * 128:256 + (tt % 2) * 128 + 128],
                                    ident[:, :]).then_inc(mm.h, 1)
                    blk.tensor(tens)
                    t_mm = mm.inc()

                    def sc(v, tt=tt, t_mm=t_mm):
                        v.wait_ge(mm.h, t_mm)
                        sl = slice(tt * 128, (tt + 1) * 128)
                        sl2 = slice(4096 + tt * 128, 4096 + (tt + 1) * 128)
                        v.tensor_copy(kf16[:, sl], psB[:, tt % 2, 0:128])
                        v.tensor_copy(qf16[:, sl], psB[:, 2 + tt % 2, 0:128])
                        ss(v)
                        v.tensor_tensor(kf16[:, sl2], psB[:, tt % 2, 0:128],
                                        kf16[:, sl], ALU.subtract)
                        v.tensor_tensor(qf16[:, sl2], psB[:, 2 + tt % 2, 0:128],
                                        qf16[:, sl], ALU.subtract).then_inc(ev.h, 1)
                    blk.vector(sc)
                    ev.inc()

                    def vecv(v, tt=tt, t_ld=t_ld):
                        v.wait_ge(ld.h, t_ld)
                        v.tensor_copy(vbuf[:, tt, 0:64],
                                      stg2[:, (tt % 4) * 128:(tt % 4) * 128 + 64])
                        v.tensor_copy(vbuf[:, tt, 65:129],
                                      stg2[:, (tt % 4) * 128 + 64:(tt % 4) * 128 + 128])
                        v.fence(vx.h)
                    blk.vector(vecv)
                    vx.inc()
                barrier()

                prev_last_exp = 0
                prev_sc_o = 0
                for hh in range(2):
                    hloc = pair * 2 + hh
                    for qb in range(8):
                        nk = 4 * qb + 4
                        exp_tgt = {}
                        pv_tgt = {}

                        def emit_scores(j, hh=hh, qb=qb):
                            def tens_j(t, j=j, hh=hh, qb=qb,
                                       w_ev=exp_tgt.get(j - 4),
                                       w_prev=prev_last_exp):
                                if w_ev is not None:
                                    t.wait_ge(ev.h, w_ev)
                                elif w_prev:
                                    t.wait_ge(ev.h, w_prev)
                                khi = kf16[hh * 64:(hh + 1) * 64,
                                           j * 128:(j + 1) * 128]
                                klo = kf16[hh * 64:(hh + 1) * 64,
                                           4096 + j * 128:4096 + (j + 1) * 128]
                                qhi = qf16[hh * 64:(hh + 1) * 64,
                                           qb * 512:(qb + 1) * 512]
                                qlo = qf16[hh * 64:(hh + 1) * 64,
                                           4096 + qb * 512:4096 + (qb + 1) * 512]
                                t.matmul(psB[:, j % 4, :], khi, qhi,
                                         start=True, stop=False)
                                t.matmul(psB[:, j % 4, :], khi, qlo,
                                         start=False, stop=False)
                                t.matmul(psB[:, j % 4, :], klo, qhi,
                                         start=False, stop=True).then_inc(mm.h, 1)
                            blk.tensor(tens_j)
                            t_mm = mm.inc()

                            def sc_j(s, j=j, t_mm=t_mm, w_pv=pv_tgt.get(j - 4)):
                                s.wait_ge(mm.h, t_mm)
                                if w_pv is not None:
                                    s.wait_ge(mm.h, w_pv)
                                s.activation(ptile[:, j % 4, :], psB[:, j % 4, :],
                                             AF.Exp, bias=cbias[:, 0:1],
                                             scale=0.125).then_inc(ev.h, 1)
                            blk.scalar(sc_j)
                            exp_tgt[j] = ev.inc()

                            if j >= 4 * qb:
                                def vec_j(v, j=j, qb=qb, t_exp=exp_tgt[j]):
                                    v.wait_ge(ev.h, t_exp)
                                    off = 384 + qb * 512 - j * 128
                                    v.tensor_tensor(ptile[:, j % 4, :],
                                                    ptile[:, j % 4, :],
                                                    bmask[:, off:off + 512], ALU.mult)
                                    v.fence(vx.h)
                                blk.vector(vec_j)
                                return ("vx", vx.inc())
                            return ("ev", exp_tgt[j])

                        def emit_pv(j, rdy, hh=hh, qb=qb, nk=nk):
                            def tens_pv(t, j=j, hh=hh, rdy=rdy, nk=nk,
                                        w_evac=prev_sc_o):
                                t.wait_ge({"vx": vx, "ev": ev}[rdy[0]].h, rdy[1])
                                if j == 0 and w_evac:
                                    t.wait_ge(ev.h, w_evac)
                                t.matmul(psA[0:65, 0, :],
                                         vbuf[:, j, hh * 65:(hh + 1) * 65],
                                         ptile[:, j % 4, :],
                                         start=(j == 0),
                                         stop=(j == nk - 1)).then_inc(mm.h, 1)
                            blk.tensor(tens_pv)
                            pv_tgt[j] = mm.inc()

                        rdys = {}
                        if FLASH_PIPE:
                            for j in range(nk):
                                rdys[j] = emit_scores(j)
                                if j >= 1:
                                    emit_pv(j - 1, rdys[j - 1])
                            emit_pv(nk - 1, rdys[nk - 1])
                        else:
                            for j in range(nk):
                                rdys[j] = emit_scores(j)
                                emit_pv(j, rdys[j])
                        prev_last_exp = exp_tgt[nk - 1]

                        def vec_e(v, t_mm=pv_tgt[nk - 1], t_sp=sp.n):
                            v.wait_ge(mm.h, t_mm)
                            v.wait_ge(sp.h, t_sp)
                            v.tensor_copy(orow[0:65, 0:512],
                                          psA[0:65, 0, :]).then_inc(ev.h, 1)
                        blk.vector(vec_e)
                        t_pe = ev.inc()
                        prev_sc_o = t_pe

                        sub_tgts = []
                        for sub_ in range(4):
                            def tr_s(t, sub_=sub_, t_pe=t_pe,
                                     w_vx=(sub_tgts[sub_ - 2] if sub_ >= 2 else 0)):
                                if sub_ == 0:
                                    t.wait_ge(ev.h, t_pe)
                                if w_vx:
                                    t.wait_ge(vx.h, w_vx)
                                t.transpose(psA[:, 1 + sub_ % 2, 0:65],
                                            orow[0:65, sub_ * 128:(sub_ + 1) * 128],
                                            ident[0:65, 0:65]).then_inc(mm.h, 1)
                            blk.tensor(tr_s)
                            t_tr = mm.inc()

                            def vec_s(v, sub_=sub_, t_tr=t_tr):
                                v.wait_ge(mm.h, t_tr)
                                v.reciprocal(scal[:, 3 + sub_:4 + sub_],
                                             psA[:, 1 + sub_ % 2, 64:65]).then_inc(sw.h, 1)
                                v.wait_ge(sw.h, sw.inc())
                                v.tensor_scalar(orow[:, 512 + sub_ * 64:512 + (sub_ + 1) * 64],
                                                psA[:, 1 + sub_ % 2, 0:64],
                                                scal[:, 3 + sub_:4 + sub_], None,
                                                ALU.mult)
                                v.fence(vx.h)
                            blk.vector(vec_s)
                            sub_tgts.append(vx.inc())

                        def gp_o(g, hloc=hloc, qb=qb, t_eo=sub_tgts[-1]):
                            g.wait_ge(vx.h, t_eo)
                            g.dma_start(
                                out=dr["amine"][qb * 4:(qb + 1) * 4, :,
                                                hloc * 64:(hloc + 1) * 64]
                                .rearrange("s p d -> p s d"),
                                in_=orow[:, 512:768].rearrange("p (s d) -> p s d", d=64),
                            ).then_inc(sp.h, 16)
                        blk.gpsimd(gp_o)
                        sp.inc(16)
                barrier()

            def ldwo(sy, t_sp=sp.n):
                sy.wait_ge(sp.h, t_sp)
                sy.dma_start(out=wta[:, :].rearrange("p (c n) -> p c n", c=1),
                             in_=dr["woT"][0:1, :, :].rearrange("c p n -> p c n")
                             ).then_inc(ld.h, 16)
                sy.dma_start(out=wtb[:, :].rearrange("p (c n) -> p c n", c=1),
                             in_=dr["woT"][1:2, :, :].rearrange("c p n -> p c n")
                             ).then_inc(ld.h, 16)
                sy.dma_start(out=orow[:, :].rearrange("p (c n) -> p c n", c=1),
                             in_=dr["woT"][2:3, :, :].rearrange("c p n -> p c n")
                             ).then_inc(ld.h, 16)
                sy.dma_start(out=bmask[:, 0:896],
                             in_=dr["woT"][3, :, 0:896]).then_inc(ld.h, 16)
                sy.dma_start(out=ptile[:, 0, 0:128],
                             in_=dr["woT"][3, :, 896:1024]).then_inc(ld.h, 16)
            blk.sync(ldwo)
            t_wo = ld.inc(80)
            wchunk = [wta[:, :], wtb[:, :], orow[:, :], None]

            for tt in range(32):
                buf = stg if tt % 2 == 0 else stg2

                def ldf(sy, tt=tt, buf=buf, t_mm=mm.n):
                    sy.wait_ge(mm.h, t_mm)
                    sy.dma_start(out=buf[:, 0:512],
                                 in_=dr["amine"][tt, :, :]).then_inc(ld.h, 16)
                blk.sync(ldf)
                t_ld = ld.inc(16)

                for c in range(4):
                    def tens(t, tt=tt, buf=buf, c=c, t_ld=t_ld, t_ev=ev.n):
                        if c == 0:
                            t.wait_ge(ld.h, t_ld)
                        t.wait_ge(ev.h, t_ev)
                        t.transpose(psB[:, c % 2, 0:128],
                                    buf[:, c * 128:(c + 1) * 128],
                                    ident[:, :]).then_inc(mm.h, 1)
                    blk.tensor(tens)
                    t_mm = mm.inc()

                    def sc(v, buf=buf, c=c, t_mm=t_mm):
                        v.wait_ge(mm.h, t_mm)
                        v.tensor_copy(buf[:, 512 + c * 128:512 + (c + 1) * 128],
                                      psB[:, c % 2, 0:128]).then_inc(ev.h, 1)
                    blk.vector(sc)
                    ev.inc()

                def tens2(t, tt=tt, buf=buf, t_wo=t_wo, t_evp=ev.n, t_vx=vx.n):
                    t.wait_ge(ev.h, t_evp)
                    t.wait_ge(ld.h, t_wo)
                    t.wait_ge(vx.h, t_vx)
                    for c in range(4):
                        w = wchunk[c]
                        for os_ in range(2):
                            if c < 3:
                                rhs = w[:, os_ * 512:(os_ + 1) * 512]
                                t.matmul(psA[:, os_, :],
                                         buf[:, 512 + c * 128:512 + (c + 1) * 128],
                                         rhs, start=(c == 0), stop=(c == 3))
                            else:
                                if os_ == 0:
                                    t.matmul(psA[:, 0, :],
                                             buf[:, 512 + c * 128:512 + (c + 1) * 128],
                                             bmask[:, 0:512], start=False, stop=True)
                                else:
                                    t.matmul(psA[:, 1, 0:384],
                                             buf[:, 512 + c * 128:512 + (c + 1) * 128],
                                             bmask[:, 512:896], start=False, stop=False)
                                    t.matmul(psA[:, 1, 384:512],
                                             buf[:, 512 + c * 128:512 + (c + 1) * 128],
                                             ptile[:, 0, 0:128], start=False, stop=True)
                    t.fence(mm.h)
                blk.tensor(tens2)
                t_mm2 = mm.inc()

                def vec2(v, tt=tt, buf=buf, t_mm2=t_mm2, t_sp=sp.n):
                    v.wait_ge(mm.h, t_mm2)
                    v.wait_ge(sp.h, t_sp)
                    v.tensor_copy(buf[:, 1024:1536], psA[:, 0, :])
                    v.tensor_copy(buf[:, 1536:2048], psA[:, 1, :])
                    v.fence(vx.h)
                blk.vector(vec2)
                t_vy = vx.inc()

                def gp_y(g, tt=tt, buf=buf, t_vy=t_vy):
                    g.wait_ge(vx.h, t_vy)
                    g.dma_start(out=dr["rs_in"][tt * 128:(tt + 1) * 128, :],
                                in_=buf[:, 1024:2048]).then_inc(sp.h, 16)
                blk.gpsimd(gp_y)
                sp.inc(16)

            def gpr(g, t_sp=sp.n):
                g.wait_ge(sp.h, t_sp)
                g.collective_compute(
                    "ReduceScatter", ALU.add,
                    replica_groups=[[0, 1], [2, 3], [4, 5], [6, 7]],
                    ins=[dr["rs_in"].rearrange("t d -> (t d)").opt()],
                    outs=[dr["rs_out"].rearrange("t d -> (t d)").opt()],
                ).then_inc(cc.h, 1)
                g.wait_ge(cc.h, cc.inc())
                g.sem_inc(gx.h, 1)
            blk.gpsimd(gpr)
            gx.inc()
            barrier()

            for tt in range(16):
                buf = stg if tt % 2 == 0 else stg2

                def ldf2(sy, tt=tt, buf=buf, t_mm=mm.n):
                    sy.wait_ge(mm.h, t_mm)
                    sy.dma_start(out=buf[:, 0:1024],
                                 in_=dr["rs_out"][tt * 128:(tt + 1) * 128, :]
                                 ).then_inc(ld.h, 16)
                blk.sync(ldf2)
                t_ld = ld.inc(16)

                for c in range(NC_D):
                    def tens(t, buf=buf, c=c, t_ld=t_ld, t_vx=vx.n):
                        if c == 0:
                            t.wait_ge(ld.h, t_ld)
                        t.wait_ge(vx.h, t_vx)
                        t.transpose(psB[:, c % 4, 0:128],
                                    buf[:, c * 128:(c + 1) * 128],
                                    ident[:, :]).then_inc(mm.h, 1)
                    blk.tensor(tens)
                    t_mm = mm.inc()

                    def vec(v, tt=tt, c=c, t_mm=t_mm):
                        v.wait_ge(mm.h, t_mm)
                        v.scalar_tensor_tensor(xres[:, c, tt * 128:(tt + 1) * 128],
                                               psB[:, c % 4, 0:128], 1.0,
                                               xres[:, c, tt * 128:(tt + 1) * 128],
                                               ALU.mult, ALU.add)
                        v.fence(vx.h)
                    blk.vector(vec)
                    vx.inc()
            barrier()

        def store_out():
            for tt in range(16):
                for c in range(NC_D):
                    def tens(t, tt=tt, c=c, t_ev=ev.n):
                        t.wait_ge(ev.h, t_ev)
                        t.transpose(psB[:, c % 4, 0:128],
                                    xres[:, c, tt * 128:(tt + 1) * 128],
                                    ident[:, :]).then_inc(mm.h, 1)
                    blk.tensor(tens)
                    t_mm = mm.inc()

                    def sc(v, tt=tt, c=c, t_mm=t_mm, t_sp=sp.n):
                        v.wait_ge(mm.h, t_mm)
                        v.wait_ge(sp.h, t_sp)
                        v.tensor_copy(stg[:, c * 128:(c + 1) * 128],
                                      psB[:, c % 4, 0:128]).then_inc(ev.h, 1)
                    blk.vector(sc)
                    ev.inc()

                def gp(g, tt=tt, t_ev=ev.n):
                    g.wait_ge(ev.h, t_ev)
                    g.dma_start(out=out[tt * 128:(tt + 1) * 128, :],
                                in_=stg[:, 0:1024]).then_inc(sp.h, 16)
                blk.gpsimd(gp)
                sp.inc(16)

        p0_consts()
        p0_input()
        if stage >= 1:
            p0_weights()
        barrier()
        if "wuT0" in tap:
            def ldt(sy, t_mm=mm.n):
                sy.dma_start(out=stg[:, 0:3072],
                             in_=dr["wuT"][0, :, :]).then_inc(ld.h, 16)
            blk.sync(ldt)
            mktap("wuT0", stg[:, 0:3072], [128, 3072], wait=(ld, ld.inc(16)))
        if stage >= 2:
            ffn_call("f1")
        if stage >= 3:
            attention()
        if stage >= 4:
            ffn_call("f2")
        store_out()

        def fin(g, t_sp=sp.n):
            g.wait_ge(sp.h, t_sp)
        blk.gpsimd(fin)

    return nc, tap_outs


def kernel(**inputs):
    x = np.ascontiguousarray(np.asarray(inputs["L0_attn_x"], np.float32))
    wq = np.asarray(inputs["L0_attn_wq"], np.float32)
    wk = np.asarray(inputs["L0_attn_wk"], np.float32)
    wv = np.asarray(inputs["L0_attn_wv"], np.float32)
    wo = np.ascontiguousarray(np.asarray(inputs["L0_attn_wo"], np.float32))
    wg = np.ascontiguousarray(np.asarray(inputs["L0_ffn_w_gate"], np.float32))
    wu = np.ascontiguousarray(np.asarray(inputs["L0_ffn_w_up"], np.float32))
    wd = np.ascontiguousarray(np.asarray(inputs["L0_ffn_w_down"], np.float32))

    nc, _ = build_program(stage=STAGE)
    in_maps = []
    for core in range(8):
        b, h = core // 2, core % 2
        in_maps.append({
            "x": np.ascontiguousarray(x[b, h * T:(h + 1) * T, :]),
            "wq": np.ascontiguousarray(wq[h * 512:(h + 1) * 512, :]),
            "wk": np.ascontiguousarray(wk[h * 512:(h + 1) * 512, :]),
            "wv": np.ascontiguousarray(wv[h * 512:(h + 1) * 512, :]),
            "wo": np.ascontiguousarray(wo[:, h * 512:(h + 1) * 512]),
            "wg": wg, "wu": wu, "wd": wd,
        })
    res = run_bass_kernel_spmd(nc, in_maps, core_ids=list(range(8)))
    outp = np.empty((4, S, D), np.float32)
    for core in range(8):
        b, h = core // 2, core % 2
        outp[b, h * T:(h + 1) * T, :] = res.results[core]["out"]
    return outp
